# version 01
# baseline (speedup 1.0000x reference)
import numpy as np
import ml_dtypes
import concourse.bass as bass
import concourse.mybir as mybir
from contextlib import ExitStack


F32 = mybir.dt.float32
BF16 = mybir.dt.bfloat16
I32 = mybir.dt.int32
AF = mybir.ActivationFunctionType
ALU = mybir.AluOpType
AX = mybir.AxisListType

NCORE = 8
BIG = 65536.0
OOBF = 1.0e9


class S:
    def __init__(self, ctx, nc, name):
        self.h = ctx.enter_context(nc.semaphore(name))
        self.n = 0

    def inc(self, inst, amt=1):
        inst.then_inc(self.h, amt)
        self.n += amt
        return inst

    def incd(self, inst):
        return self.inc(inst, 16)

    def reserve(self, amt):
        base = self.n
        self.n += amt
        return base

    def inc_at(self, inst, amt=1):
        inst.then_inc(self.h, amt)
        return inst

    def wait(self, eng):
        if self.n > 0:
            eng.wait_ge(self.h, self.n)

    def wait_at(self, eng, val):
        if val > 0:
            eng.wait_ge(self.h, val)




class Chain:

    def __init__(self, sem, eng):
        self.sem, self.eng = sem, eng

    def __call__(self, inst):
        self.sem.inc(inst)
        self.eng.wait_ge(self.sem.h, self.sem.n)
        return inst


def default_cfg():
    return dict(T=4096, D=1024, H=16, DH=64, FF=1408, E=16, CAP=5120,
                moe="dense", stop_after="P5")


def host_prep(inputs, cfg):
    T, D, H, DH, FF, E = cfg["T"], cfg["D"], cfg["H"], cfg["DH"], cfg["FF"], cfg["E"]
    x = np.ascontiguousarray(inputs["L0_attn_x"])
    NP = H // 2

    def pack_qkv(M):
        Mt = np.transpose(M, (2, 1, 0))
        return np.ascontiguousarray(
            np.concatenate([Mt[0::2], Mt[1::2]], axis=2))

    Qp = pack_qkv(inputs["L0_attn_Q"])
    Kp = pack_qkv(inputs["L0_attn_K"])
    Vp = pack_qkv(inputs["L0_attn_V"])
    Wt = np.transpose(inputs["L0_attn_W"], (2, 1, 0))
    Wp = np.ascontiguousarray(np.concatenate([Wt[0::2], Wt[1::2]], axis=1))

    shards = []
    NT = T // 128
    for c in range(NCORE):
        xc = np.ascontiguousarray(x[c])
        xq = np.ascontiguousarray(xc.reshape(NT, 128, D).transpose(0, 2, 1))
        sh = {
            "x_tok": xc, "xq": xq,
            "Qp": Qp, "Kp": Kp, "Vp": Vp, "Wp": Wp,
            "rw": np.ascontiguousarray(inputs["L0_ffn_router_w"]),
            "rb": np.ascontiguousarray(inputs["L0_ffn_router_b"].reshape(1, E)),
            "swg": np.ascontiguousarray(inputs["L0_ffn_sw_gate"]),
            "swu": np.ascontiguousarray(inputs["L0_ffn_sw_up"]),
            "swd": np.ascontiguousarray(inputs["L0_ffn_sw_down"]),
            "coremask": np.ascontiguousarray(
                (np.arange(NCORE) < c).astype(np.float32).reshape(NCORE, 1)),
            "gidbase": np.full((1, 1), float(c * T), np.float32),
        }
        if cfg["moe"] == "dense":
            sh["wg"] = np.ascontiguousarray(inputs["L0_ffn_w_gate"])
            sh["wu"] = np.ascontiguousarray(inputs["L0_ffn_w_up"])
            sh["wd"] = np.ascontiguousarray(inputs["L0_ffn_w_down"])
        else:
            sh["wg"] = np.ascontiguousarray(inputs["L0_ffn_w_gate"][2 * c:2 * c + 2])
            sh["wu"] = np.ascontiguousarray(inputs["L0_ffn_w_up"][2 * c:2 * c + 2])
            sh["wd"] = np.ascontiguousarray(inputs["L0_ffn_w_down"][2 * c:2 * c + 2])
        shards.append(sh)
    return shards


def build(cfg):
    T, D, H, DH = cfg["T"], cfg["D"], cfg["H"], cfg["DH"]
    FF, E, CAP = cfg["FF"], cfg["E"], cfg["CAP"]
    moe = cfg["moe"]
    stop_after = cfg.get("stop_after", "P5")
    NT, NB, NCH, NFS, NP = T // 128, T // 512, D // 128, FF // 128, H // 2
    QSCALE = 1.0 / D
    ZSCALE = float(np.sqrt(D / DH) / H)
    VSCALE = float(np.sqrt(DH / D))

    nc = bass.Bass("TRN2", target_bir_lowering=False,
                   detect_race_conditions=cfg.get("race", True))

    x_tok = nc.dram_tensor("x_tok", [T, D], F32, kind="ExternalInput").ap()
    xq_in = nc.dram_tensor("xq", [NT, D, 128], F32, kind="ExternalInput").ap()
    Qp_in = nc.dram_tensor("Qp", [NP, D, 2 * DH], F32, kind="ExternalInput").ap()
    Kp_in = nc.dram_tensor("Kp", [NP, D, 2 * DH], F32, kind="ExternalInput").ap()
    Vp_in = nc.dram_tensor("Vp", [NP, D, 2 * DH], F32, kind="ExternalInput").ap()
    Wp_in = nc.dram_tensor("Wp", [NP, 2 * DH, D], F32, kind="ExternalInput").ap()
    rw_in = nc.dram_tensor("rw", [D, E], F32, kind="ExternalInput").ap()
    rb_in = nc.dram_tensor("rb", [1, E], F32, kind="ExternalInput").ap()
    swg_in = nc.dram_tensor("swg", [D, FF], F32, kind="ExternalInput").ap()
    swu_in = nc.dram_tensor("swu", [D, FF], F32, kind="ExternalInput").ap()
    swd_in = nc.dram_tensor("swd", [FF, D], F32, kind="ExternalInput").ap()
    nexp_w = E if moe == "dense" else 2
    wg_in = nc.dram_tensor("wg", [nexp_w, D, FF], F32, kind="ExternalInput").ap()
    wu_in = nc.dram_tensor("wu", [nexp_w, D, FF], F32, kind="ExternalInput").ap()
    wd_in = nc.dram_tensor("wd", [nexp_w, FF, D], F32, kind="ExternalInput").ap()
    cmask_in = nc.dram_tensor("coremask", [NCORE, 1], F32, kind="ExternalInput").ap()
    gidb_in = nc.dram_tensor("gidbase", [1, 1], F32, kind="ExternalInput").ap()

    out = nc.dram_tensor("out", [T, D], F32, kind="ExternalOutput").ap()

    acc_d = nc.dram_tensor("acc_d", [T, D], F32, kind="Internal")
    qT_d = nc.dram_tensor("qT_d", [NP, 128, T], BF16, kind="Internal")
    v_d = nc.dram_tensor("v_d", [NP, 128, (T // 128) * 130], BF16, kind="Internal")
    xnT_d = nc.dram_tensor("xnT_d", [NCH, 128, T], BF16, kind="Internal")
    cnt_in_d = nc.dram_tensor("cnt_in_d", [1, E], F32, kind="Internal")
    cnt_out_d = nc.dram_tensor("cnt_out_d", [NCORE, E], F32, kind="Internal")
    if moe == "sharded":
        xn_in_d = nc.dram_tensor("xn_in_d", [T, D], BF16, kind="Internal")
        xn_all_d = nc.dram_tensor("xn_all_d", [NCORE * T, D], BF16, kind="Internal")
        rt_in_d = nc.dram_tensor("rt_in_d", [128, NT * 4], F32, kind="Internal")
        rt_all_d = nc.dram_tensor("rt_all_d", [NCORE * 128, NT * 4], F32,
                                  kind="Internal")
        tbl_d = nc.dram_tensor("tbl_d", [2 * CAP, 1], F32, kind="Internal")
        eo_in_d = nc.dram_tensor("eo_in_d", [2 * CAP, D], BF16, kind="Internal")
        eo_all_d = nc.dram_tensor("eo_all_d", [NCORE * 2 * CAP, D], BF16,
                                  kind="Internal")

    ident_c = nc.inline_tensor(np.eye(128, dtype=ml_dtypes.bfloat16), "ident128")
    u128_c = nc.inline_tensor(
        np.triu(np.ones((128, 128), np.float32), 1).astype(ml_dtypes.bfloat16),
        "u128")
    ones128b_c = nc.inline_tensor(np.ones((128, 1), ml_dtypes.bfloat16), "ones128b")
    onescol_c = nc.inline_tensor(np.ones((128, 1), np.float32), "onescol")
    onesrow_c = nc.inline_tensor(np.ones((1, 128), np.float32), "onesrow")
    iota16_c = nc.inline_tensor(
        np.tile(np.arange(E, dtype=np.float32), (128, 1)), "iota16")
    iotaB_c = nc.inline_tensor(
        np.tile(np.arange(E, dtype=np.float32) + BIG, (128, 1)), "iotaB")
    masks_np = np.stack([
        (np.arange(128)[:, None] + 128 * d <= np.arange(512)[None, :])
        .astype(ml_dtypes.bfloat16) for d in range(4)])
    mask_c = nc.inline_tensor(
        np.ascontiguousarray(masks_np.transpose(1, 0, 2).reshape(128, 4 * 512)),
        "cmask")

    ctx = ExitStack()
    ctx2 = ExitStack()
    ctx1 = ExitStack()

    def mk_sb(stack):
        def f(name, shape, dt):
            return stack.enter_context(nc.sbuf_tensor(name, shape, dt))
        return f

    def mk_ps(stack):
        def f(name, shape, dt=F32):
            return stack.enter_context(nc.psum_tensor(name, shape, dt))
        return f

    sb, psum = mk_sb(ctx), mk_ps(ctx)
    sb2x, ps2x = mk_sb(ctx2), mk_ps(ctx2)
    sb1, ps1 = mk_sb(ctx1), mk_ps(ctx1)

    route = sb("route", [128, NT * 6], F32)
    gates = sb("gates", [128, NT * 2], F32)
    G_sb = sb("G", [128, NT * E], F32) if moe == "dense" else None
    retidx = sb("retidx", [128, NT * 2], I32) if moe == "sharded" else None
    ident_sb = sb("ident_sb", [128, 128], BF16)
    u128_sb = sb("u128_sb", [128, 128], BF16)
    ones128b_sb = sb("ones128b_sb", [128, 1], BF16)
    onescol_sb = sb("onescol_sb", [128, 1], F32)
    onesrow_sb = sb("onesrow_sb", [1, 128], F32)
    iota16_sb = sb("iota16_sb", [128, E], F32)
    iotaB_sb = sb("iotaB_sb", [128, E], F32)
    mask_sb = sb("mask_sb", [128, 4 * 512], BF16)
    rw_sb = sb("rw_sb", [128, NCH * E], F32)
    rb_bc = sb("rb_bc", [128, E], F32)
    cs_bc = sb("cs_bc", [128, E], F32)
    cpre_bc = sb("cpre_bc", [128, E], F32)
    prefix_bc = sb("prefix_bc", [128, E], F32)
    cnt_sb = sb("cnt_sb", [NCORE, E], F32)
    cmask_sb = sb("cmask_sb", [NCORE, 1], F32)
    gidb_sb = sb("gidb_sb", [1, 1], F32)
    negM = sb("negM", [128, 1], F32)
    stat = sb("stat", [128, 16], F32)
    ct_row = sb("ct_row", [1, E], F32)
    l16 = sb("l16", [128, E], F32)
    oh1 = sb("oh1", [128, E], F32)
    oh2 = sb("oh2", [128, E], F32)
    t16 = sb("t16", [128, E], F32)
    a16 = sb("a16", [128, E], F32)
    xnT = [sb2x(f"xnT{c}", [128, T], BF16) for c in range(NCH)]

    xa = [sb1(f"xa{j}", [128, D], F32) for j in range(2)]
    xqb = [sb1(f"xqb{j}", [128, D], F32) for j in range(2)]
    xnb = [sb1(f"xnb{j}", [128, D], BF16) for j in range(2)]
    xnf = sb1("xnf", [128, D], F32)
    sqscr = sb1("sqscr", [128, D], F32)
    acc0 = [sb1(f"acc0{j}", [128, D], F32) for j in range(2)]
    e16 = sb1("e16", [128, E], F32)
    m8 = sb1("m8", [128, 8], F32)
    u16 = sb1("u16", [128, E], F32)
    ohc = sb1("ohc", [128, E], BF16)
    posf = sb1("posf", [128, E], F32)
    ps_lg = ps1("ps_lg", [128, E])
    ps_cs = ps1("ps_cs", [128, E])
    ps_ct = ps1("ps_ct", [1, E])
    ps_bc = ps1("ps_bc", [128, E])
    ps_tr = [ps1(f"ps_tr{j}", [128, 128], BF16) for j in range(2)]

    s_cst = S(ctx, nc, "s_cst")
    s_x = S(ctx, nc, "s_x")
    s_sq = S(ctx, nc, "s_sq")
    s_v = S(ctx, nc, "s_v")
    s_s = S(ctx, nc, "s_s")
    s_pe = S(ctx, nc, "s_pe")
    s_g = S(ctx, nc, "s_g")
    s_acc = S(ctx, nc, "s_acc")
    s_accg = S(ctx, nc, "s_accg")
    s_cc = S(ctx, nc, "s_cc")
    s_wl = S(ctx, nc, "s_wl")
    s_q2 = S(ctx, nc, "s_q2")
    s_st = S(ctx, nc, "s_st")
    s_kv = S(ctx, nc, "s_kv")
    s_p2a = S(ctx, nc, "s_p2a")
    s_p2s = S(ctx, nc, "s_p2s")
    s_p2v = S(ctx, nc, "s_p2v")
    s_p2y = S(ctx, nc, "s_p2y")
    s_mw = S(ctx, nc, "s_mw")
    s_p3 = S(ctx, nc, "s_p3")
    s_p3s = S(ctx, nc, "s_p3s")
    s_p3v = S(ctx, nc, "s_p3v")
    s_p4 = S(ctx, nc, "s_p4")
    s_vq = S(ctx, nc, "s_vq")
    s_p5 = S(ctx, nc, "s_p5")

    with nc.Block("P0") as blk:
        @blk.sync
        def _(sy):
            for dst, src in [(ident_sb, ident_c), (u128_sb, u128_c),
                             (ones128b_sb, ones128b_c), (onescol_sb, onescol_c),
                             (onesrow_sb, onesrow_c), (iota16_sb, iota16_c),
                             (iotaB_sb, iotaB_c), (mask_sb, mask_c)]:
                s_cst.incd(sy.dma_start(out=dst[:, :], in_=src.ap()))
            s_cst.incd(sy.dma_start(
                out=rw_sb[:, :].rearrange("p (c e) -> p c e", e=E),
                in_=rw_in.rearrange("(c p) e -> p c e", p=128)))
            s_cst.incd(sy.dma_start(out=rb_bc[0:1, :], in_=rb_in[:, :]))
            s_cst.incd(sy.dma_start(out=cmask_sb[:, :], in_=cmask_in[:, :]))
            s_cst.incd(sy.dma_start(out=gidb_sb[:, :], in_=gidb_in[:, :]))

        @blk.tensor
        def _(t):
            s_cst.wait(t)
            for c in range(NCH):
                mm = t.matmul(ps_ct[:, :], onescol_sb[:, :],
                              rw_sb[:, c * E:(c + 1) * E],
                              start=(c == 0), stop=(c == NCH - 1))
            s_pe.inc(mm)
            s_pe.inc(t.matmul(ps_bc[:, :], onesrow_sb[:, :], rb_bc[0:1, :],
                              start=True, stop=True))

        @blk.vector
        def _(v):
            s_pe.wait_at(v, 1)
            s_v.inc(v.tensor_copy(ct_row[:, :], ps_ct[:, :]))
            s_pe.wait_at(v, 2)
            s_v.inc(v.tensor_copy(rb_bc[:, :], ps_bc[:, :]))

        @blk.tensor
        def _(t):
            s_v.wait(t)
            s_pe.inc(t.matmul(ps_bc[:, :], onesrow_sb[:, :], ct_row[:, :],
                              start=True, stop=True))

        @blk.vector
        def _(v):
            s_pe.wait(v)
            s_v.inc(v.tensor_copy(cs_bc[:, :], ps_bc[:, :]))
            v.memset(prefix_bc[:, :], 0.0)
            s_v.inc(v.memset(stat[:, :], 0.0))

    for i in range(NT):
        j = i % 2
        with nc.Block(f"P1_{i}") as blk:
            vp, sp, pp = s_v.n, s_s.n, s_pe.n

            @blk.sync
            def _(sy, i=i, j=j, vp=vp, sp=sp, pp=pp):
                if i >= 2:
                    s_v.wait_at(sy, vp)
                    s_s.wait_at(sy, sp)
                    s_pe.wait_at(sy, pp)
                    s_acc.wait_at(sy, 16 * (i - 1))
                s_x.incd(sy.dma_start(out=xa[j][:, :],
                                      in_=x_tok[i * 128:(i + 1) * 128, :]))
                s_x.incd(sy.dma_start(
                    out=xqb[j][:, :].rearrange("p (c n) -> p c n", n=128),
                    in_=xq_in[i, :, :].rearrange("(c p) n -> p c n", p=128)))

            @blk.scalar
            def _(s, i=i, j=j, vp=vp):
                s_x.wait_at(s, (2 * i + 2) * 16)
                s_v.wait_at(s, vp)
                s_sq.inc(s.activation(sqscr[:, :], xa[j][:, :], AF.Square))

            @blk.vector
            def _(v, i=i, j=j):
                C = Chain(s_v, v)
                s_x.wait_at(v, (2 * i + 2) * 16)
                C(v.reduce_sum(stat[:, 0:1], xa[j][:, :], axis=AX.X))
                C(v.tensor_scalar_mul(stat[:, 2:3], stat[:, 0:1], 1.0 / D))
                s_sq.wait(v)
                C(v.reduce_sum(stat[:, 1:2], sqscr[:, :], axis=AX.X))
                C(v.tensor_mul(stat[:, 3:4], stat[:, 2:3], stat[:, 2:3]))
                C(v.scalar_tensor_tensor(stat[:, 3:4], stat[:, 1:2], 1.0 / D,
                                         stat[:, 3:4], ALU.mult, ALU.subtract))
                C(v.tensor_scalar_add(stat[:, 3:4], stat[:, 3:4], 1e-5))

            @blk.scalar
            def _(s, i=i, j=j):
                Cs = Chain(s_s, s)
                s_v.wait(s)
                Cs(s.activation(stat[:, 4:5], stat[:, 3:4], AF.Ln))
                Cs(s.activation(stat[:, 4:5], stat[:, 4:5], AF.Exp, scale=-0.5))

            @blk.vector
            def _(v, i=i, j=j):
                C = Chain(s_v, v)
                s_s.wait(v)
                C(v.tensor_scalar(xnf[:, :], xa[j][:, :], stat[:, 2:3],
                                  stat[:, 4:5], ALU.subtract, ALU.mult))
                C(v.tensor_copy(xnb[j][:, :], xnf[:, :]))
                C(v.tensor_add(acc0[j][:, :], xa[j][:, :], xnf[:, :]))

            @blk.sync
            def _(sy, i=i, j=j):
                s_v.wait(sy)
                s_acc.incd(sy.dma_start(out=acc_d[i * 128:(i + 1) * 128, :],
                                        in_=acc0[j][:, :]))

            if moe == "sharded":
                @blk.gpsimd
                def _(gg, i=i, j=j):
                    s_v.wait(gg)
                    s_g.incd(gg.dma_start(out=xn_in_d[i * 128:(i + 1) * 128, :],
                                          in_=xnb[j][:, :]))

            ev_base = s_s.reserve(NCH)
            pe_tr_base = s_pe.n

            @blk.tensor
            def _(t, i=i, j=j, ev_base=ev_base):
                s_v.wait(t)
                for c in range(NCH):
                    if c >= 2:
                        s_s.wait_at(t, ev_base + c - 1)
                    s_pe.inc(t.transpose(ps_tr[c % 2][:, :],
                                         xnb[j][:, c * 128:(c + 1) * 128],
                                         ident_sb[:, :]))

            @blk.scalar
            def _(s, i=i, j=j, ev_base=ev_base, pe_tr_base=pe_tr_base):
                for c in range(NCH):
                    s_pe.wait_at(s, pe_tr_base + c + 1)
                    s_s.inc_at(s.copy(xnT[c][:, i * 128:(i + 1) * 128],
                                      ps_tr[c % 2][:, :]))

            @blk.gpsimd
            def _(gg, i=i, j=j, ev_base=ev_base):
                s_s.wait_at(gg, ev_base + NCH)
                for c in range(NCH):
                    s_g.incd(gg.dma_start(
                        out=xnT_d[c, :, i * 128:(i + 1) * 128],
                        in_=xnT[c][:, i * 128:(i + 1) * 128]))

            @blk.tensor
            def _(t, i=i, j=j):
                s_x.wait_at(t, (2 * i + 2) * 16)
                s_v.wait_at(t, s_v.n)
                for c in range(NCH):
                    mm = t.matmul(ps_lg[:, :], xqb[j][:, c * 128:(c + 1) * 128],
                                  rw_sb[:, c * E:(c + 1) * E],
                                  start=(c == 0), stop=(c == NCH - 1))
                s_pe.inc(mm)

            @blk.vector
            def _(v, i=i, j=j):
                C = Chain(s_v, v)
                s_pe.wait(v)
                C(v.tensor_scalar(t16[:, :], cs_bc[:, :], stat[:, 2:3], None,
                                  ALU.mult))
                C(v.tensor_sub(l16[:, :], ps_lg[:, :], t16[:, :]))
                C(v.tensor_scalar(l16[:, :], l16[:, :], stat[:, 4:5], None,
                                  ALU.mult))
                C(v.tensor_add(l16[:, :], l16[:, :], rb_bc[:, :]))
                C(v.max(m8[:, :], l16[:, :]))
                C(v.tensor_scalar_mul(stat[:, 6:7], m8[:, 0:1], -1.0))
                C(v.tensor_sub(stat[:, 7:8], m8[:, 1:2], m8[:, 0:1]))

            @blk.scalar
            def _(s, i=i, j=j):
                Cs = Chain(s_s, s)
                s_v.wait(s)
                Cs(s.activation(e16[:, :], l16[:, :], AF.Exp, bias=stat[:, 6:7]))
                Cs(s.activation(stat[:, 8:9], stat[:, 7:8], AF.Exp))

            @blk.vector
            def _(v, i=i, j=j):
                C = Chain(s_v, v)
                s_s.wait(v)
                C(v.reduce_sum(stat[:, 9:10], e16[:, :], axis=AX.X))
                C(v.scalar_tensor_tensor(stat[:, 10:11], stat[:, 9:10], 1e-9,
                                         stat[:, 8:9], ALU.mult, ALU.add))
                C(v.tensor_scalar_add(stat[:, 10:11], stat[:, 10:11], 1.0))
                C(v.reciprocal(stat[:, 10:11], stat[:, 10:11]))
                C(v.tensor_mul(stat[:, 11:12], stat[:, 8:9], stat[:, 10:11]))
                C(v.tensor_scalar(u16[:, :], l16[:, :], m8[:, 0:1], None,
                                  ALU.is_ge))
                C(v.scalar_tensor_tensor(a16[:, :], u16[:, :], -BIG,
                                         iotaB_sb[:, :], ALU.mult, ALU.add))
                C(v.tensor_reduce(stat[:, 12:13], a16[:, :], axis=AX.X,
                                  op=ALU.min))
                C(v.tensor_scalar(oh1[:, :], iota16_sb[:, :], stat[:, 12:13],
                                  None, ALU.is_equal))
                C(v.tensor_scalar(u16[:, :], l16[:, :], m8[:, 1:2], None,
                                  ALU.is_ge))
                C(v.scalar_tensor_tensor(a16[:, :], u16[:, :], -BIG,
                                         iotaB_sb[:, :], ALU.mult, ALU.add))
                C(v.scalar_tensor_tensor(a16[:, :], oh1[:, :], 2 * BIG,
                                         a16[:, :], ALU.mult, ALU.add))
                C(v.tensor_reduce(stat[:, 13:14], a16[:, :], axis=AX.X,
                                  op=ALU.min))
                C(v.tensor_scalar(oh2[:, :], iota16_sb[:, :], stat[:, 13:14],
                                  None, ALU.is_equal))
                C(v.tensor_add(ohc[:, :], oh1[:, :], oh2[:, :]))

            @blk.tensor
            def _(t, i=i, j=j):
                s_v.wait(t)
                t.matmul(ps_cs[:, :], u128_sb[:, :], ohc[:, :],
                         start=True, stop=True)
                s_pe.inc(t.matmul(ps_ct[:, :], ones128b_sb[:, :], ohc[:, :],
                                  start=True, stop=True))

            @blk.vector
            def _(v, i=i, j=j):
                C = Chain(s_v, v)
                s_pe.wait(v)
                C(v.tensor_add(posf[:, :], ps_cs[:, :], prefix_bc[:, :]))
                C(v.tensor_mul(t16[:, :], posf[:, :], oh1[:, :]))
                C(v.reduce_sum(route[:, 6 * i + 1:6 * i + 2], t16[:, :],
                               axis=AX.X))
                C(v.tensor_mul(t16[:, :], posf[:, :], oh2[:, :]))
                C(v.reduce_sum(route[:, 6 * i + 4:6 * i + 5], t16[:, :],
                               axis=AX.X))
                v.tensor_copy(route[:, 6 * i + 0:6 * i + 1], stat[:, 12:13])
                v.tensor_copy(route[:, 6 * i + 2:6 * i + 3], stat[:, 10:11])
                v.tensor_copy(route[:, 6 * i + 3:6 * i + 4], stat[:, 13:14])
                v.tensor_copy(route[:, 6 * i + 5:6 * i + 6], stat[:, 11:12])
                C(v.tensor_copy(ct_row[:, :], ps_ct[:, :]))

            @blk.tensor
            def _(t, i=i, j=j):
                s_v.wait(t)
                s_pe.inc(t.matmul(ps_bc[:, :], onesrow_sb[:, :], ct_row[:, :],
                                  start=True, stop=True))

            @blk.vector
            def _(v, i=i, j=j):
                s_pe.wait(v)
                s_v.inc(v.tensor_add(prefix_bc[:, :], prefix_bc[:, :],
                                     ps_bc[:, :]))

    L = dict(locals())
    L["p1_acc_end"] = s_acc.n

    with nc.Block("C0") as blk:
        @blk.vector
        def _(v):
            s_v.inc(v.tensor_copy(ct_row[:, :], prefix_bc[0:1, :]))

        @blk.gpsimd
        def _(gg):
            s_v.wait(gg)
            s_g.incd(gg.dma_start(out=cnt_in_d.ap(), in_=ct_row[:, :]))
            s_g.wait(gg)
            s_cc.inc(gg.collective_compute(
                "AllGather", ALU.bypass, replica_groups=[list(range(NCORE))],
                ins=[cnt_in_d.ap().opt()], outs=[cnt_out_d.ap().opt()]))

        @blk.sync
        def _(sy):
            s_cc.wait(sy)
            s_cst.incd(sy.dma_start(out=cnt_sb[:, :], in_=cnt_out_d.ap()))

        @blk.tensor
        def _(t):
            s_cst.wait(t)
            s_v.wait(t)
            s_pe.inc(t.matmul(ps_ct[:, :], cmask_sb[:, :], cnt_sb[:, :],
                              start=True, stop=True))

        @blk.vector
        def _(v):
            s_pe.wait(v)
            s_v.inc(v.tensor_copy(ct_row[:, :], ps_ct[:, :]))

        @blk.tensor
        def _(t):
            s_v.wait(t)
            s_pe.inc(t.matmul(ps_bc[:, :], onesrow_sb[:, :], ct_row[:, :],
                              start=True, stop=True))

        @blk.vector
        def _(v):
            s_pe.wait(v)
            s_v.inc(v.tensor_copy(cpre_bc[:, :], ps_bc[:, :]))

    with nc.Block("P19") as blk:
        @blk.vector
        def _(v):
            C = Chain(s_v, v)
            for i in range(NT):
                for k in range(2):
                    cidx = route[:, 6 * i + 3 * k:6 * i + 3 * k + 1]
                    cpos = route[:, 6 * i + 3 * k + 1:6 * i + 3 * k + 2]
                    cval = route[:, 6 * i + 3 * k + 2:6 * i + 3 * k + 3]
                    ohk = oh1 if k == 0 else oh2
                    C(v.tensor_scalar(ohk[:, :], iota16_sb[:, :], cidx, None,
                                      ALU.is_equal))
                    C(v.tensor_mul(t16[:, :], ohk[:, :], cpre_bc[:, :]))
                    C(v.reduce_sum(stat[:, 0:1], t16[:, :], axis=AX.X))
                    C(v.tensor_add(stat[:, 1:2], cpos, stat[:, 0:1]))
                    C(v.tensor_scalar(stat[:, 2:3], stat[:, 1:2], float(CAP),
                                      None, ALU.is_lt))
                    C(v.tensor_mul(gates[:, 2 * i + k:2 * i + k + 1], cval,
                                   stat[:, 2:3]))
                    if moe == "sharded":
                        C(v.tensor_scalar(stat[:, 3:4], cidx, float(CAP), None,
                                          ALU.mult))
                        C(v.tensor_add(stat[:, 3:4], stat[:, 3:4], stat[:, 1:2]))
                        C(v.tensor_scalar(stat[:, 4:5], stat[:, 2:3], -OOBF,
                                          None, ALU.mult))
                        C(v.tensor_scalar_add(stat[:, 4:5], stat[:, 4:5], OOBF))
                        C(v.tensor_add(stat[:, 3:4], stat[:, 3:4], stat[:, 4:5]))
                        C(v.tensor_copy(retidx[:, 2 * i + k:2 * i + k + 1],
                                        stat[:, 3:4]))
                if moe == "dense":
                    C(v.tensor_scalar(t16[:, :], oh1[:, :],
                                      gates[:, 2 * i:2 * i + 1], None, ALU.mult))
                    C(v.tensor_scalar(a16[:, :], oh2[:, :],
                                      gates[:, 2 * i + 1:2 * i + 2], None,
                                      ALU.mult))
                    C(v.tensor_add(G_sb[:, i * E:(i + 1) * E], t16[:, :],
                                   a16[:, :]))
            s_v.inc(v.tensor_copy(stat[:, 15:16], stat[:, 0:1]))

    L = dict(locals())
    L["p1_acc_end"] = NT * 16
    L["dbg_p2"] = cfg.get("dbg_p2", False)
    if stop_after == "P1":
        emit_p5(nc, L)
        return nc, ctx, L
    ctx1.close()

    L["sb2"], L["ps2"] = sb2x, ps2x
    ctx2t = ExitStack()
    L["sb2t"], L["ps2t"] = mk_sb(ctx2t), mk_ps(ctx2t)
    emit_p15(nc, L)
    ctx2t.close()
    if stop_after == "P15":
        emit_p5(nc, L)
        return nc, ctx, L
    emit_p2(nc, L)
    ctx2.close()
    if stop_after == "P2":
        emit_p5(nc, L)
        return nc, ctx, L

    ctx3 = ExitStack()
    L["sb3"], L["ps3"] = mk_sb(ctx3), mk_ps(ctx3)
    emit_p3(nc, L)
    if stop_after == "P3":
        emit_p5(nc, L)
        return nc, ctx, L
    if moe == "dense":
        emit_p4_dense(nc, L)
    else:
        emit_p4_sharded(nc, L)
    ctx3.close()
    emit_p5(nc, L)
    return nc, ctx, L





def emit_p15(nc, L):
    g = lambda k: L[k]
    T, D, DH, NP, NCH, NT, NB = (L["T"], L["D"], L["DH"], L["NP"], L["NCH"],
                                 L["NT"], L["NB"])
    QSCALE, VSCALE = L["QSCALE"], L["VSCALE"]
    xnT, ident_sb = L["xnT"], L["ident_sb"]
    ones128b_sb, onesrow_sb = L["ones128b_sb"], L["onesrow_sb"]
    negM, stat = L["negM"], L["stat"]
    qT_d = L["qT_d"]
    Qp_in, Kp_in, Vp_in, Wp_in = L["Qp_in"], L["Kp_in"], L["Vp_in"], L["Wp_in"]
    s_v, s_s, s_pe, s_wl, s_q2, s_kv, s_g = (L["s_v"], L["s_s"], L["s_pe"],
                                        L["s_wl"], L["s_q2"], L["s_kv"], L["s_g"])
    sb2, ps2 = L["sb2"], L["ps2"]
    sb2t, ps2t = L["sb2t"], L["ps2t"]

    kT_sb = [sb2(f"kT{p}", [128, T], BF16) for p in range(NP)]
    ps_q = ps2("ps_q", [128, 512])
    ps_k = ps2("ps_k", [128, 512])
    qw_sb = sb2t("qw_sb", [128, NCH * 128], BF16)
    kw_sb = sb2t("kw_sb", [128, NCH * 128], BF16)
    vw_sb = sb2t("vw_sb", [128, NCH * 128], BF16)
    qevac = sb2t("qevac", [128, 512], BF16)
    sqs = sb2t("sqs", [128, 512], BF16)
    v_stage = sb2t("v_stage", [128, 520], BF16)
    mrow = sb2t("mrow", [1, 4], F32)
    L["kT_sb"] = kT_sb
    v_d = L["v_d"]

    ps_v = ps2t("ps_v", [128, 128])
    ps_m = ps2t("ps_m", [1, 512])
    ps_nm = ps2t("ps_nm", [128, 1])
    L["ps_att"] = [ps_q, ps_k]

    with nc.Block("P15") as blk:
        @blk.vector
        def _(v):
            s_v.wait(v)
            s_s.wait(v)
            v.memset(v_stage[:, :], 1.0)
            s_v.inc(v.memset(mrow[:, :], 0.0))

        for p in range(NP):
            @blk.gpsimd
            def _(gg, p=p):
                if p >= 1:
                    s_pe.wait(gg)
                s_wl.incd(gg.dma_start(
                    out=qw_sb[:, :].rearrange("p (c n) -> p c n", n=2 * DH),
                    in_=Qp_in[p, :, :].rearrange("(c p) n -> p c n", p=128)))
                s_wl.incd(gg.dma_start(
                    out=kw_sb[:, :].rearrange("p (c n) -> p c n", n=2 * DH),
                    in_=Kp_in[p, :, :].rearrange("(c p) n -> p c n", p=128)))
                s_wl.incd(gg.dma_start(
                    out=vw_sb[:, :].rearrange("p (c n) -> p c n", n=2 * DH),
                    in_=Vp_in[p, :, :].rearrange("(c p) n -> p c n", p=128)))

            for b in range(NB):
                @blk.tensor
                def _(t, p=p, b=b):
                    s_wl.wait(t)
                    s_s.wait(t)
                    s_v.wait(t)
                    for c in range(NCH):
                        t.matmul(ps_q[:, :], qw_sb[:, c * 128:(c + 1) * 128],
                                 xnT[c][:, b * 512:(b + 1) * 512],
                                 start=(c == 0), stop=(c == NCH - 1))
                    for c in range(NCH):
                        mm = t.matmul(ps_k[:, :], kw_sb[:, c * 128:(c + 1) * 128],
                                      xnT[c][:, b * 512:(b + 1) * 512],
                                      start=(c == 0), stop=(c == NCH - 1))
                    s_pe.inc(mm)

                @blk.scalar
                def _(s, p=p, b=b):
                    s_pe.wait(s)
                    s.activation(qevac[:, :], ps_q[:, :], AF.Copy, scale=QSCALE)
                    s.copy(kT_sb[p][:, b * 512:(b + 1) * 512], ps_k[:, :])
                    s_s.inc(s.activation(sqs[:, :], qevac[:, :], AF.Square))

                @blk.sync
                def _(sy, p=p, b=b):
                    s_s.wait(sy)
                    s_q2.incd(sy.dma_start(
                        out=qT_d[p, :, b * 512:(b + 1) * 512], in_=qevac[:, :]))

                @blk.tensor
                def _(t, p=p, b=b):
                    s_s.wait(t)
                    s_pe.inc(t.matmul(ps_m[:, :], ones128b_sb[:, :], sqs[:, :],
                                      start=True, stop=True))

                @blk.vector
                def _(v, p=p, b=b):
                    s_pe.wait(v)
                    s_v.inc(v.reduce_max(stat[0:1, 0:1], ps_m[:, :], axis=AX.X))
                    s_v.wait(v)
                    s_v.inc(v.tensor_max(mrow[:, 0:1], mrow[:, 0:1],
                                         stat[0:1, 0:1]))

                @blk.scalar
                def _(s, p=p, b=b):
                    s_v.wait(s)
                    s_s.inc(s.activation(sqs[:, :],
                                         kT_sb[p][:, b * 512:(b + 1) * 512],
                                         AF.Square))

                @blk.tensor
                def _(t, p=p, b=b):
                    s_s.wait(t)
                    s_pe.inc(t.matmul(ps_m[:, :], ones128b_sb[:, :], sqs[:, :],
                                      start=True, stop=True))

                @blk.vector
                def _(v, p=p, b=b):
                    s_pe.wait(v)
                    s_v.inc(v.reduce_max(stat[0:1, 0:1], ps_m[:, :], axis=AX.X))
                    s_v.wait(v)
                    s_v.inc(v.tensor_max(mrow[:, 1:2], mrow[:, 1:2],
                                         stat[0:1, 0:1]))

                sv_base = s_s.reserve(4)
                pe_v_base = s_pe.n

                @blk.tensor
                def _(t, p=p, b=b, sv_base=sv_base):
                    s_v.wait(t)
                    for tt in range(4):
                        i = b * 4 + tt
                        if tt >= 1:
                            s_s.wait_at(t, sv_base + tt)
                        for c in range(NCH):
                            mm = t.matmul(ps_v[:, :],
                                          xnT[c][:, i * 128:(i + 1) * 128],
                                          vw_sb[:, c * 128:(c + 1) * 128],
                                          start=(c == 0), stop=(c == NCH - 1))
                        s_pe.inc(mm)

                @blk.scalar
                def _(s, p=p, b=b, sv_base=sv_base, pe_v_base=pe_v_base):
                    s_g.wait_at(s, s_g.n)
                    for tt in range(4):
                        s_pe.wait_at(s, pe_v_base + tt + 1)
                        s.activation(v_stage[:, tt * 130:tt * 130 + 64],
                                     ps_v[:, 0:64], AF.Copy, scale=VSCALE)
                        s_s.inc_at(s.activation(
                            v_stage[:, tt * 130 + 65:tt * 130 + 129],
                            ps_v[:, 64:128], AF.Copy, scale=VSCALE))

                @blk.gpsimd
                def _(gg, p=p, b=b, sv_base=sv_base):
                    s_s.wait_at(gg, sv_base + 4)
                    s_g.incd(gg.dma_start(
                        out=v_d[p, :, b * 520:(b + 1) * 520],
                        in_=v_stage[:, :]))

        @blk.vector
        def _(v):
            s_v.wait(v)
            s_v.inc(v.tensor_mul(mrow[:, 2:3], mrow[:, 0:1], mrow[:, 1:2]))

        @blk.scalar
        def _(s):
            s_v.wait(s)
            s_s.inc(s.activation(mrow[:, 3:4], mrow[:, 2:3], AF.Sqrt))

        @blk.vector
        def _(v):
            s_s.wait(v)
            s_v.inc(v.tensor_scalar_mul(mrow[:, 3:4], mrow[:, 3:4], -1.02))

        @blk.tensor
        def _(t):
            s_v.wait(t)
            s_pe.inc(t.matmul(ps_nm[:, :], onesrow_sb[:, :], mrow[:, 3:4],
                              start=True, stop=True))

        @blk.vector
        def _(v):
            s_pe.wait(v)
            s_kv.inc(v.tensor_copy(negM[:, :], ps_nm[:, :]))


def emit_p2(nc, L):
    g = lambda k: L[k]
    T, D, NP, NB, NT = L["T"], L["D"], L["NP"], L["NB"], L["NT"]
    ZSCALE = L["ZSCALE"]
    kT_sb = L["kT_sb"]
    v_d, Wp_in = L["v_d"], L["Wp_in"]
    s_wl, s_g = L["s_wl"], L["s_g"]
    negM, mask_sb, onesrow_sb = L["negM"], L["mask_sb"], L["onesrow_sb"]
    qT_d, acc_d = L["qT_d"], L["acc_d"]
    s_q2, s_kv, s_acc, s_accg = L["s_q2"], L["s_kv"], L["s_acc"], L["s_accg"]
    s_p2a, s_p2s, s_p2v, s_p2y = L["s_p2a"], L["s_p2s"], L["s_p2v"], L["s_p2y"]
    sb2, ps2 = L["sb2"], L["ps2"]
    ps_att = L["ps_att"]
    ps_y = [ps2("ps_y0", [65, 512]), ps2("ps_y1", [65, 512])]
    ps_bc2 = ps2("ps_bc2", [64, 512])
    ps_z = ps2("ps_z", [128, D])

    Wp_sb = sb2("Wp_sb", [128, NP * D], BF16)
    qsb = sb2("qsb", [128, NP * 512], BF16)
    vq = [sb2("vq0", [128, NT * 130], BF16), sb2("vq1", [128, NT * 130], BF16)]
    s_vq = L["s_vq"]
    pT = [sb2("pT0", [128, 512], BF16), sb2("pT1", [128, 512], BF16)]
    ysb = [sb2("ysb0", [64, 512], BF16), sb2("ysb1", [64, 512], BF16)]
    yTn = sb2("yTn", [128, NP * 512], BF16)
    rs_row = sb2("rs_row", [1, 512], F32)
    zsb = [sb2("zsb0", [128, D], F32), sb2("zsb1", [128, D], F32)]

    p2_acc_base = s_accg.n
    with nc.Block("P2W") as blk:
        @blk.gpsimd
        def _(gg):
            s_wl.incd(gg.dma_start(
                out=Wp_sb[:, :].rearrange("p (np d) -> p np d", np=NP),
                in_=Wp_in.rearrange("np p d -> p np d")))

    for qb in range(NB):
        nk = 4 * qb + 4
        with nc.Block(f"P2_{qb}") as blk:
            a0, sB, v0, y0 = s_p2a.n, s_p2s.n, s_p2v.n, s_p2y.n
            s_p2a.reserve(NP * (nk + 2) + 4)
            s_p2s.reserve(NP * (nk + 2) + 4)
            s_p2v.reserve(NP * 8)
            s_p2y.reserve(NP * nk)
            A0 = lambda p, a0=a0, nk=nk: a0 + p * (nk + 2)
            S0 = lambda p, sB=sB, nk=nk: sB + p * (nk + 2)
            V0 = lambda p, v0=v0: v0 + p * 8
            Y0 = lambda p, y0=y0, nk=nk: y0 + p * nk
            az = a0 + NP * (nk + 2)
            sz = sB + NP * (nk + 2)

            vqb = s_vq.n

            @blk.sync
            def _(sy, qb=qb, a0=a0, nk=nk, Y0=Y0, y0=y0):
                s_p2a.wait_at(sy, a0)
                s_q2.wait(sy)
                for p in range(NP):
                    s_q2.incd(sy.dma_start(
                        out=qsb[:, p * 512:(p + 1) * 512],
                        in_=qT_d[p, :, qb * 512:(qb + 1) * 512]))
                for p in range(NP):
                    if p >= 2:
                        s_p2y.wait_at(sy, Y0(p - 1))
                    s_vq.incd(sy.dma_start(
                        out=vq[p % 2][:, 0:nk * 130],
                        in_=v_d[p, :, 0:nk * 130]))

            @blk.tensor
            def _(t, qb=qb, nk=nk, A0=A0, S0=S0, V0=V0, Y0=Y0, sB=sB, az=az,
                  sz=sz, v0=v0, vqb=vqb):
                s_q2.wait(t)
                s_kv.wait(t)
                L["s_s"].wait_at(t, L["s_s"].n)
                L["s_v"].wait_at(t, L["s_v"].n)
                s_p2s.wait_at(t, sB)
                for p in range(NP):
                    for kt in range(nk):
                        if kt > 0:
                            s_p2s.wait_at(t, S0(p) + kt)
                        elif p > 0:
                            s_p2s.wait_at(t, S0(p))
                            s_p2v.wait_at(t, V0(p))
                        t.matmul(ps_att[0][:, :],
                                 kT_sb[p][0:64, kt * 128:(kt + 1) * 128],
                                 qsb[0:64, p * 512:(p + 1) * 512],
                                 start=True, stop=True, tile_position=(0, 0))
                        mm = t.matmul(ps_att[1][:, :],
                                      kT_sb[p][64:128, kt * 128:(kt + 1) * 128],
                                      qsb[64:128, p * 512:(p + 1) * 512],
                                      start=True, stop=True,
                                      tile_position=(64, 0))
                        s_p2a.inc_at(mm)
                        s_p2s.wait_at(t, S0(p) + kt + 1)
                        if kt >= nk - 4:
                            s_p2v.wait_at(t, V0(p) + (kt - (nk - 4)) + 1)
                        if kt == 0:
                            s_vq.wait_at(t, vqb + 16 * (p + 1))
                        t.matmul(ps_y[0][:, :],
                                 vq[p % 2][:, kt * 130:kt * 130 + 65],
                                 pT[0][:, :], start=(kt == 0),
                                 stop=(kt == nk - 1))
                        mm = t.matmul(ps_y[1][:, :],
                                      vq[p % 2][:, kt * 130 + 65:kt * 130 + 130],
                                      pT[1][:, :], start=(kt == 0),
                                      stop=(kt == nk - 1))
                        s_p2y.inc_at(mm)
                    s_p2v.wait_at(t, V0(p) + 5)
                    mm = t.matmul(ps_bc2[:, :], onesrow_sb[:, 0:64], rs_row[:, :],
                                  start=True, stop=True)
                    s_p2a.inc_at(mm)
                    s_p2v.wait_at(t, V0(p) + 7)
                    mm = t.matmul(ps_bc2[:, :], onesrow_sb[:, 0:64], rs_row[:, :],
                                  start=True, stop=True)
                    s_p2a.inc_at(mm)
                s_p2v.wait_at(t, v0 + NP * 8)
                dcols = [(a, min(a + 512, D)) for a in range(0, D, 512)]
                for tt in range(4):
                    if tt >= 1:
                        s_p2s.wait_at(t, sz + tt)
                    elif qb > 0:
                        s_p2s.wait_at(t, sB)
                    for p in range(NP):
                        for (ca, cb) in dcols:
                            mm = t.matmul(
                                ps_z[:, ca:cb],
                                yTn[:, p * 512 + tt * 128:p * 512 + (tt + 1) * 128],
                                Wp_sb[:, p * D + ca:p * D + cb],
                                start=(p == 0), stop=(p == NP - 1))
                    s_p2a.inc_at(mm)

            @blk.scalar
            def _(s, qb=qb, nk=nk, A0=A0, S0=S0, V0=V0, Y0=Y0, az=az, sz=sz,
                  p2_acc_base=p2_acc_base):
                if qb == 0:
                    s_kv.wait(s)
                for p in range(NP):
                    for kt in range(nk):
                        s_p2a.wait_at(s, A0(p) + kt + 1)
                        s_p2y.wait_at(s, Y0(p) + kt)
                        s.activation(pT[0][:, :], ps_att[0][:, :], AF.Exp,
                                     bias=negM[:, :])
                        mm = s.activation(pT[1][:, :], ps_att[1][:, :], AF.Exp,
                                          bias=negM[:, :])
                        s_p2s.inc_at(mm)
                    s_p2y.wait_at(s, Y0(p) + nk)
                    if p > 0:
                        s_p2v.wait_at(s, V0(p))
                    s_p2s.inc_at(s.copy(ysb[0][:, :], ps_y[0][0:64, :]))
                    s_p2s.inc_at(s.copy(ysb[1][:, :], ps_y[1][0:64, :]))
                for tt in range(4):
                    m = qb * 4 + tt
                    s_p2a.wait_at(s, az + tt + 1)
                    if m >= 2:
                        s_accg.wait_at(s, p2_acc_base + 16 * (m - 1))
                    mm = s.activation(zsb[tt % 2][:, :], ps_z[:, :], AF.Copy,
                                      scale=ZSCALE)
                    s_p2s.inc_at(mm)

            @blk.vector
            def _(v, qb=qb, nk=nk, A0=A0, S0=S0, V0=V0, Y0=Y0):
                for p in range(NP):
                    for d in range(4):
                        kt = nk - 4 + d
                        s_p2s.wait_at(v, S0(p) + kt + 1)
                        v.tensor_mul(pT[0][:, :], pT[0][:, :],
                                     mask_sb[:, d * 512:(d + 1) * 512])
                        mm = v.tensor_mul(pT[1][:, :], pT[1][:, :],
                                          mask_sb[:, d * 512:(d + 1) * 512])
                        s_p2v.inc_at(mm)
                    for h in range(2):
                        s_p2y.wait_at(v, Y0(p) + nk)
                        mm = v.reciprocal(rs_row[:, :], ps_y[h][64:65, :])
                        s_p2v.inc_at(mm)
                        s_p2v.wait_at(v, V0(p) + 5 + 2 * h)
                        s_p2a.wait_at(v, A0(p) + nk + 1 + h)
                        s_p2s.wait_at(v, S0(p) + nk + 1 + h)
                        mm = v.tensor_mul(
                            yTn[64 * h:64 * h + 64, p * 512:(p + 1) * 512],
                            ysb[h][:, :], ps_bc2[:, :])
                        s_p2v.inc_at(mm)

            @blk.gpsimd
            def _(gg, qb=qb, sz=sz, p2_acc_base=p2_acc_base):
                for tt in range(4):
                    s_p2s.wait_at(gg, sz + tt + 1)
                    if qb == 0 and tt == 0:
                        s_acc.wait_at(gg, L["p1_acc_end"])
                    i = qb * 4 + tt
                    s_accg.incd(gg.dma_start(
                        out=acc_d[i * 128:(i + 1) * 128, :],
                        in_=zsb[tt % 2][:, :], accum_op=ALU.add))


def _emit_mlp(nc, blk, L, wgs, wus, wds, token_src, gate_col_fn, accdst,
              rows_base, n_blocks, first_waits, bf16_out=None,
              plain_store=False):
    D, FF, NCH, NFS = L["D"], L["FF"], L["NCH"], L["NFS"]
    s_p3, s_p3s, s_p3v = L["s_p3"], L["s_p3s"], L["s_p3v"]
    s_acc, s_accg = L["s_acc"], L["s_accg"]
    ps_g, ps_u, ps_z3 = L["ps_g3"], L["ps_u3"], L["ps_z3"]
    prod, gsb, zsb3 = L["prod"], L["gsb"], L["zsb3"]

    per_t, per_s, per_v = NFS + 4, NFS + 4, NFS
    tb = s_p3.reserve(n_blocks * per_t)
    sbb = s_p3s.reserve(n_blocks * per_s)
    vb = s_p3v.reserve(n_blocks * per_v)
    acc_base = s_accg.n
    n_tiles_total = n_blocks * 4

    def TB(b, k=0):
        return tb + b * per_t + k

    def SB(b, k=0):
        return sbb + b * per_s + k

    def VB(b, k=0):
        return vb + b * per_v + k

    @blk.tensor
    def _(t):
        for sem, val in first_waits:
            sem.wait_at(t, sem.n if val is None else val)
        for b in range(n_blocks):
            for fs in range(NFS):
                if not (b == 0 and fs == 0):
                    pb, pf = (b, fs - 1) if fs > 0 else (b - 1, NFS - 1)
                    s_p3s.wait_at(t, SB(pb, pf + 1))
                    s_p3v.wait_at(t, VB(pb, pf + 1))
                for c in range(NCH):
                    t.matmul(ps_g[:, :],
                             wgs[:, c * FF + fs * 128:c * FF + (fs + 1) * 128],
                             token_src[c][:, b * 512:(b + 1) * 512],
                             start=(c == 0), stop=(c == NCH - 1))
                for c in range(NCH):
                    mm = t.matmul(ps_u[:, :],
                                  wus[:, c * FF + fs * 128:c * FF + (fs + 1) * 128],
                                  token_src[c][:, b * 512:(b + 1) * 512],
                                  start=(c == 0), stop=(c == NCH - 1))
                s_p3.inc_at(mm)
            for tt in range(4):
                s_p3v.wait_at(t, VB(b, NFS))
                if not (b == 0 and tt == 0):
                    pb, pt = (b, tt - 1) if tt > 0 else (b - 1, 3)
                    s_p3s.wait_at(t, SB(pb, NFS + pt + 1))
                dcols = [(a, min(a + 512, D)) for a in range(0, D, 512)]
                for fs in range(NFS):
                    for (ca, cb) in dcols:
                        mm = t.matmul(
                            ps_z3[:, ca:cb],
                            prod[:, fs * 512 + tt * 128:fs * 512 + (tt + 1) * 128],
                            wds[:, fs * D + ca:fs * D + cb],
                            start=(fs == 0), stop=(fs == NFS - 1))
                s_p3.inc_at(mm)

    @blk.scalar
    def _(s):
        for b in range(n_blocks):
            for fs in range(NFS):
                s_p3.wait_at(s, TB(b, fs + 1))
                if not (b == 0 and fs == 0):
                    pb, pf = (b, fs - 1) if fs > 0 else (b - 1, NFS - 1)
                    s_p3v.wait_at(s, VB(pb, pf + 1))
                s_p3s.inc_at(s.activation(gsb[:, :], ps_g[:, :], AF.Silu))
            for tt in range(4):
                i = b * 4 + tt
                s_p3.wait_at(s, TB(b, NFS + tt + 1))
                if i >= 2:
                    s_accg.wait_at(s, acc_base + 16 * (i - 1))
                gc = gate_col_fn(i) if gate_col_fn is not None else None
                if gc is not None:
                    mm = s.activation(zsb3[i % 2][:, :], ps_z3[:, :], AF.Copy,
                                      scale=gc)
                else:
                    mm = s.copy(zsb3[i % 2][:, :], ps_z3[:, :])
                s_p3s.inc_at(mm)

    @blk.vector
    def _(v):
        for b in range(n_blocks):
            for fs in range(NFS):
                s_p3s.wait_at(v, SB(b, fs + 1))
                mm = v.tensor_mul(prod[:, fs * 512:(fs + 1) * 512], gsb[:, :],
                                  ps_u[:, :])
                s_p3v.inc_at(mm)

    @blk.gpsimd
    def _(gg):
        for b in range(n_blocks):
            for tt in range(4):
                i = b * 4 + tt
                s_p3s.wait_at(gg, SB(b, NFS + tt + 1))
                s_accg.incd(gg.dma_start(
                    out=accdst[(rows_base + i) * 128:(rows_base + i + 1) * 128, :],
                    in_=zsb3[i % 2][:, :],
                    accum_op=(ALU.bypass if plain_store else ALU.add)))

    return TB, SB, VB


def emit_p3(nc, L):
    NCH, NFS, FF, D, NB, T = L["NCH"], L["NFS"], L["FF"], L["D"], L["NB"], L["T"]
    sb3, ps3 = L["sb3"], L["ps3"]
    s_mw, s_q2 = L["s_mw"], L["s_q2"]
    swg_in, swu_in, swd_in = L["swg_in"], L["swu_in"], L["swd_in"]
    xnT_d = L["xnT_d"]

    xnT3 = [sb3(f"xnT3_{c}", [128, T], BF16) for c in range(NCH)]
    L["xnT3"] = xnT3
    wgs = sb3("wgs", [128, NCH * FF], BF16)
    wus = sb3("wus", [128, NCH * FF], BF16)
    wds = sb3("wds", [128, NFS * D], BF16)
    gsb = sb3("gsb", [128, 512], BF16)
    prod = sb3("prod", [128, NFS * 512], BF16)
    zsb3 = [sb3("zsb3_0", [128, D], F32), sb3("zsb3_1", [128, D], F32)]
    ps_g = ps3("ps_g3", [128, 512])
    ps_u = ps3("ps_u3", [128, 512])
    ps_z3 = ps3("ps_z3", [128, D])
    L.update(wgs=wgs, wus=wus, wds=wds, gsb=gsb, prod=prod, zsb3=zsb3,
             ps_g3=ps_g, ps_u3=ps_u, ps_z3=ps_z3)

    with nc.Block("P3") as blk:
        @blk.sync
        def _(sy):
            L["s_p2a"].wait_at(sy, L["s_p2a"].n)
            L["s_p2s"].wait_at(sy, L["s_p2s"].n)
            L["s_p2v"].wait_at(sy, L["s_p2v"].n)
            for c in range(NCH):
                s_q2.incd(sy.dma_start(out=xnT3[c][:, :], in_=xnT_d[c, :, :]))

        @blk.gpsimd
        def _(gg):
            s_mw.incd(gg.dma_start(
                out=wgs[:, :].rearrange("p (c f) -> p c f", f=FF),
                in_=swg_in.rearrange("(c p) f -> p c f", p=128)))
            s_mw.incd(gg.dma_start(
                out=wus[:, :].rearrange("p (c f) -> p c f", f=FF),
                in_=swu_in.rearrange("(c p) f -> p c f", p=128)))
            s_mw.incd(gg.dma_start(
                out=wds[:, :].rearrange("p (c d) -> p c d", d=D),
                in_=swd_in.rearrange("(c p) d -> p c d", p=128)))

        tb_info = _emit_mlp(
            nc, blk, L, wgs, wus, wds, xnT3, None, L["acc_d"], 0, NB,
            first_waits=[(L["s_mw"], None), (L["s_q2"], None),
                         (L["s_p2a"], None), (L["s_p2s"], None),
                         (L["s_p2v"], None)])
        L["p3_last_t"] = L["s_p3"].n


def emit_p4_dense(nc, L):
    E, NB, NCH, NFS, FF, D = L["E"], L["NB"], L["NCH"], L["NFS"], L["FF"], L["D"]
    s_mw, s_p3 = L["s_mw"], L["s_p3"]
    wg_in, wu_in, wd_in = L["wg_in"], L["wu_in"], L["wd_in"]
    wgs, wus, wds = L["wgs"], L["wus"], L["wds"]
    G_sb = L["G_sb"]

    for e in range(E):
        with nc.Block(f"P4d_{e}") as blk:
            @blk.gpsimd
            def _(gg, e=e):
                s_p3.wait(gg)
                L["s_p3s"].wait(gg)
                L["s_p3v"].wait(gg)
                s_mw.incd(gg.dma_start(
                    out=wgs[:, :].rearrange("p (c f) -> p c f", f=FF),
                    in_=wg_in[e, :, :].rearrange("(c p) f -> p c f", p=128)))
                s_mw.incd(gg.dma_start(
                    out=wus[:, :].rearrange("p (c f) -> p c f", f=FF),
                    in_=wu_in[e, :, :].rearrange("(c p) f -> p c f", p=128)))
                s_mw.incd(gg.dma_start(
                    out=wds[:, :].rearrange("p (c d) -> p c d", d=D),
                    in_=wd_in[e, :, :].rearrange("(c p) d -> p c d", p=128)))

            _emit_mlp(nc, blk, L, wgs, wus, wds, L["xnT3"],
                      (lambda i, e=e: G_sb[:, i * E + e:i * E + e + 1]),
                      L["acc_d"], 0, NB, first_waits=[(s_mw, None)])


def emit_p4_sharded(nc, L):
    raise NotImplementedError


def emit_p5(nc, L):
    T, D, NT = L["T"], L["D"], L["NT"]
    acc_d, out = L["acc_d"], L["out"]
    s_acc, s_st, s_accg = L["s_acc"], L["s_st"], L["s_accg"]
    ob = [L["sb"]("ob0", [128, D], F32), L["sb"]("ob1", [128, D], F32)]

    with nc.Block("P5") as blk:
        @blk.sync
        def _(sy):
            s_acc.wait(sy)
            s_accg.wait(sy)
            for i in range(NT):
                jj = i % 2
                s_st.incd(sy.dma_start(out=ob[jj][:, :],
                                       in_=acc_d[i * 128:(i + 1) * 128, :]))
                s_st.wait_at(sy, s_st.n)
                s_st.incd(sy.dma_start(out=out[i * 128:(i + 1) * 128, :],
                                       in_=ob[jj][:, :]))
            s_st.wait(sy)


_CFG = dict(T=4096, D=1024, H=16, DH=64, FF=1408, E=16, CAP=5120,
            moe="dense", stop_after="P5", race=False)


def kernel(**inputs):
    from concourse.bass_utils import run_bass_kernel_spmd
    cfg = dict(_CFG)
    shards = host_prep(inputs, cfg)
    nc, ctx, L = build(cfg)
    res = run_bass_kernel_spmd(nc, shards, core_ids=list(range(NCORE)))
    return np.stack([r["out"] for r in res.results]).astype(np.float32)


# version 02
# speedup vs baseline: 1.3203x; 1.3203x over previous
import numpy as np
import ml_dtypes
import concourse.bass as bass
import concourse.mybir as mybir
from contextlib import ExitStack


F32 = mybir.dt.float32
BF16 = mybir.dt.bfloat16
I32 = mybir.dt.int32
AF = mybir.ActivationFunctionType
ALU = mybir.AluOpType
AX = mybir.AxisListType

NCORE = 8
BIG = 65536.0
OOBF = 1.0e9


class S:
    def __init__(self, ctx, nc, name):
        self.h = ctx.enter_context(nc.semaphore(name))
        self.n = 0

    def inc(self, inst, amt=1):
        inst.then_inc(self.h, amt)
        self.n += amt
        return inst

    def incd(self, inst):
        return self.inc(inst, 16)

    def reserve(self, amt):
        base = self.n
        self.n += amt
        return base

    def inc_at(self, inst, amt=1):
        inst.then_inc(self.h, amt)
        return inst

    def wait(self, eng):
        if self.n > 0:
            eng.wait_ge(self.h, self.n)

    def wait_at(self, eng, val):
        if val > 0:
            eng.wait_ge(self.h, val)




class Chain:

    def __init__(self, sem, eng):
        self.sem, self.eng = sem, eng

    def __call__(self, inst):
        self.sem.inc(inst)
        self.eng.wait_ge(self.sem.h, self.sem.n)
        return inst


def default_cfg():
    return dict(T=4096, D=1024, H=16, DH=64, FF=1408, E=16, CAP=5120,
                moe="dense", stop_after="P5")


def host_prep(inputs, cfg):
    T, D, H, DH, FF, E = cfg["T"], cfg["D"], cfg["H"], cfg["DH"], cfg["FF"], cfg["E"]
    x = np.ascontiguousarray(inputs["L0_attn_x"])
    NP = H // 2

    def pack_qkv(M):
        Mt = np.transpose(M, (2, 1, 0))
        return np.ascontiguousarray(
            np.concatenate([Mt[0::2], Mt[1::2]], axis=2))

    Qp = pack_qkv(inputs["L0_attn_Q"])
    Kp = pack_qkv(inputs["L0_attn_K"])
    Vp = pack_qkv(inputs["L0_attn_V"])
    Wt = np.transpose(inputs["L0_attn_W"], (2, 1, 0))
    Wp = np.ascontiguousarray(np.concatenate([Wt[0::2], Wt[1::2]], axis=1))

    shards = []
    NT = T // 128
    for c in range(NCORE):
        xc = np.ascontiguousarray(x[c])
        xq = np.ascontiguousarray(xc.reshape(NT, 128, D).transpose(0, 2, 1))
        sh = {
            "x_tok": xc, "xq": xq,
            "Qp": Qp, "Kp": Kp, "Vp": Vp, "Wp": Wp,
            "rw": np.ascontiguousarray(inputs["L0_ffn_router_w"]),
            "rb": np.ascontiguousarray(inputs["L0_ffn_router_b"].reshape(1, E)),
            "swg": np.ascontiguousarray(inputs["L0_ffn_sw_gate"]),
            "swu": np.ascontiguousarray(inputs["L0_ffn_sw_up"]),
            "swd": np.ascontiguousarray(inputs["L0_ffn_sw_down"]),
            "coremask": np.ascontiguousarray(
                (np.arange(NCORE) < c).astype(np.float32).reshape(NCORE, 1)),
            "gidbase": np.full((1, 1), float(c * T), np.float32),
        }
        if cfg["moe"] == "dense":
            sh["wg"] = np.ascontiguousarray(inputs["L0_ffn_w_gate"])
            sh["wu"] = np.ascontiguousarray(inputs["L0_ffn_w_up"])
            sh["wd"] = np.ascontiguousarray(inputs["L0_ffn_w_down"])
        else:
            sh["wg"] = np.ascontiguousarray(inputs["L0_ffn_w_gate"][2 * c:2 * c + 2])
            sh["wu"] = np.ascontiguousarray(inputs["L0_ffn_w_up"][2 * c:2 * c + 2])
            sh["wd"] = np.ascontiguousarray(inputs["L0_ffn_w_down"][2 * c:2 * c + 2])
            sh["myexp"] = np.array([[2 * c, 2 * c + 1]], np.float32)
        shards.append(sh)
    return shards


def build(cfg):
    T, D, H, DH = cfg["T"], cfg["D"], cfg["H"], cfg["DH"]
    FF, E, CAP = cfg["FF"], cfg["E"], cfg["CAP"]
    moe = cfg["moe"]
    stop_after = cfg.get("stop_after", "P5")
    NT, NB, NCH, NFS, NP = T // 128, T // 512, D // 128, FF // 128, H // 2
    QSCALE = 1.0 / D
    ZSCALE = float(np.sqrt(D / DH) / H)
    VSCALE = float(np.sqrt(DH / D))

    nc = bass.Bass("TRN2", target_bir_lowering=False,
                   detect_race_conditions=cfg.get("race", True))

    x_tok = nc.dram_tensor("x_tok", [T, D], F32, kind="ExternalInput").ap()
    xq_in = nc.dram_tensor("xq", [NT, D, 128], F32, kind="ExternalInput").ap()
    Qp_in = nc.dram_tensor("Qp", [NP, D, 2 * DH], F32, kind="ExternalInput").ap()
    Kp_in = nc.dram_tensor("Kp", [NP, D, 2 * DH], F32, kind="ExternalInput").ap()
    Vp_in = nc.dram_tensor("Vp", [NP, D, 2 * DH], F32, kind="ExternalInput").ap()
    Wp_in = nc.dram_tensor("Wp", [NP, 2 * DH, D], F32, kind="ExternalInput").ap()
    rw_in = nc.dram_tensor("rw", [D, E], F32, kind="ExternalInput").ap()
    rb_in = nc.dram_tensor("rb", [1, E], F32, kind="ExternalInput").ap()
    swg_in = nc.dram_tensor("swg", [D, FF], F32, kind="ExternalInput").ap()
    swu_in = nc.dram_tensor("swu", [D, FF], F32, kind="ExternalInput").ap()
    swd_in = nc.dram_tensor("swd", [FF, D], F32, kind="ExternalInput").ap()
    nexp_w = E if moe == "dense" else 2
    wg_in = nc.dram_tensor("wg", [nexp_w, D, FF], F32, kind="ExternalInput").ap()
    wu_in = nc.dram_tensor("wu", [nexp_w, D, FF], F32, kind="ExternalInput").ap()
    wd_in = nc.dram_tensor("wd", [nexp_w, FF, D], F32, kind="ExternalInput").ap()
    cmask_in = nc.dram_tensor("coremask", [NCORE, 1], F32, kind="ExternalInput").ap()
    if moe == "sharded":
        myexp_in = nc.dram_tensor("myexp", [1, 2], F32, kind="ExternalInput").ap()
    gidb_in = nc.dram_tensor("gidbase", [1, 1], F32, kind="ExternalInput").ap()

    out = nc.dram_tensor("out", [T, D], F32, kind="ExternalOutput").ap()

    acc_d = nc.dram_tensor("acc_d", [T, D], F32, kind="Internal")
    qT_d = nc.dram_tensor("qT_d", [NP, 128, T], BF16, kind="Internal")
    v_d = nc.dram_tensor("v_d", [NP, 128, (T // 128) * 130], BF16, kind="Internal")
    xnT_d = nc.dram_tensor("xnT_d", [NCH, 128, T], BF16, kind="Internal")
    cnt_in_d = nc.dram_tensor("cnt_in_d", [1, E], F32, kind="Internal")
    cnt_out_d = nc.dram_tensor("cnt_out_d", [NCORE, E], F32, kind="Internal")
    if moe == "sharded":
        xn_in_d = nc.dram_tensor("xn_in_d", [T, D], BF16, kind="Internal")
        xn_all_d = nc.dram_tensor("xn_all_d", [NCORE * T, D], BF16, kind="Internal")
        rt_in_d = nc.dram_tensor("rt_in_d", [128, NT * 4], F32, kind="Internal")
        rt_all_d = nc.dram_tensor("rt_all_d", [NCORE * 128, NT * 4], F32,
                                  kind="Internal")
        tbl_d = nc.dram_tensor("tbl_d", [2 * CAP, 2], F32, kind="Internal")
        eo_in_d = nc.dram_tensor("eo_in_d", [2 * CAP, D], BF16, kind="Internal")
        eo_all_d = nc.dram_tensor("eo_all_d", [NCORE * 2 * CAP, D], BF16,
                                  kind="Internal")

    ident_c = nc.inline_tensor(np.eye(128, dtype=ml_dtypes.bfloat16), "ident128")
    u128_c = nc.inline_tensor(
        np.triu(np.ones((128, 128), np.float32), 1).astype(ml_dtypes.bfloat16),
        "u128")
    ones128b_c = nc.inline_tensor(np.ones((128, 1), ml_dtypes.bfloat16), "ones128b")
    onescol_c = nc.inline_tensor(np.ones((128, 1), np.float32), "onescol")
    onesrow_c = nc.inline_tensor(np.ones((1, 128), np.float32), "onesrow")
    iota16_c = nc.inline_tensor(
        np.tile(np.arange(E, dtype=np.float32), (128, 1)), "iota16")
    iotaB_c = nc.inline_tensor(
        np.tile(np.arange(E, dtype=np.float32) + BIG, (128, 1)), "iotaB")
    masks_np = np.stack([
        (np.arange(128)[:, None] + 128 * d <= np.arange(512)[None, :])
        .astype(ml_dtypes.bfloat16) for d in range(4)])
    mask_c = nc.inline_tensor(
        np.ascontiguousarray(masks_np.transpose(1, 0, 2).reshape(128, 4 * 512)),
        "cmask")

    ctx = ExitStack()
    ctx2 = ExitStack()
    ctx1 = ExitStack()

    def mk_sb(stack):
        def f(name, shape, dt):
            return stack.enter_context(nc.sbuf_tensor(name, shape, dt))
        return f

    def mk_ps(stack):
        def f(name, shape, dt=F32):
            return stack.enter_context(nc.psum_tensor(name, shape, dt))
        return f

    sb, psum = mk_sb(ctx), mk_ps(ctx)
    sb2x, ps2x = mk_sb(ctx2), mk_ps(ctx2)
    sb1, ps1 = mk_sb(ctx1), mk_ps(ctx1)

    route = sb("route", [128, NT * 6], F32)
    gates = sb("gates", [128, NT * 2], F32)
    G_sb = sb("G", [128, NT * E], F32) if moe == "dense" else None
    retidx = sb("retidx", [128, NT * 2], I32) if moe == "sharded" else None
    rt_sb = sb("rt_sb", [128, NT * 4], F32) if moe == "sharded" else None
    myexp_sb = sb("myexp_sb", [1, 2], F32) if moe == "sharded" else None
    ident_sb = sb("ident_sb", [128, 128], BF16)
    u128_sb = sb("u128_sb", [128, 128], BF16)
    ones128b_sb = sb("ones128b_sb", [128, 1], BF16)
    onescol_sb = sb("onescol_sb", [128, 1], F32)
    onesrow_sb = sb("onesrow_sb", [1, 128], F32)
    iota16_sb = sb("iota16_sb", [128, E], F32)
    iotaB_sb = sb("iotaB_sb", [128, E], F32)
    mask_sb = sb("mask_sb", [128, 4 * 512], BF16)
    rw_sb = sb("rw_sb", [128, NCH * E], F32)
    rb_bc = sb("rb_bc", [128, E], F32)
    cs_bc = sb("cs_bc", [128, E], F32)
    cpre_bc = sb("cpre_bc", [128, E], F32)
    prefix_bc = sb("prefix_bc", [128, E], F32)
    cnt_sb = sb("cnt_sb", [NCORE, E], F32)
    cmask_sb = sb("cmask_sb", [NCORE, 1], F32)
    gidb_sb = sb("gidb_sb", [1, 1], F32)
    negM = sb("negM", [128, 1], F32)
    stat = sb("stat", [128, 16], F32)
    ct_row = sb("ct_row", [1, E], F32)
    l16 = sb("l16", [128, E], F32)
    oh1 = sb("oh1", [128, E], F32)
    oh2 = sb("oh2", [128, E], F32)
    t16 = sb("t16", [128, E], F32)
    a16 = sb("a16", [128, E], F32)
    xnT = [sb2x(f"xnT{c}", [128, T], BF16) for c in range(NCH)]

    xa = [sb1(f"xa{j}", [128, D], F32) for j in range(2)]
    xqb = [sb1(f"xqb{j}", [128, D], F32) for j in range(2)]
    xnb = [sb1(f"xnb{j}", [128, D], BF16) for j in range(2)]
    xnf = sb1("xnf", [128, D], F32)
    sqscr = sb1("sqscr", [128, D], F32)
    acc0 = [sb1(f"acc0{j}", [128, D], F32) for j in range(2)]
    e16 = sb1("e16", [128, E], F32)
    m8 = sb1("m8", [128, 8], F32)
    u16 = sb1("u16", [128, E], F32)
    ohc = sb1("ohc", [128, E], BF16)
    posf = sb1("posf", [128, E], F32)
    ps_lg = ps1("ps_lg", [128, E])
    ps_cs = ps1("ps_cs", [128, E])
    ps_ct = ps1("ps_ct", [1, E])
    ps_bc = ps1("ps_bc", [128, E])
    ps_tr = [ps1(f"ps_tr{j}", [128, 128], BF16) for j in range(2)]

    s_cst = S(ctx, nc, "s_cst")
    s_x = S(ctx, nc, "s_x")
    s_sq = S(ctx, nc, "s_sq")
    s_v = S(ctx, nc, "s_v")
    s_s = S(ctx, nc, "s_s")
    s_pe = S(ctx, nc, "s_pe")
    s_g = S(ctx, nc, "s_g")
    s_acc = S(ctx, nc, "s_acc")
    s_accg = S(ctx, nc, "s_accg")
    s_cc = S(ctx, nc, "s_cc")
    s_wl = S(ctx, nc, "s_wl")
    s_q2 = S(ctx, nc, "s_q2")
    s_st = S(ctx, nc, "s_st")
    s_kv = S(ctx, nc, "s_kv")
    s_p2a = S(ctx, nc, "s_p2a")
    s_p2s = S(ctx, nc, "s_p2s")
    s_p2v = S(ctx, nc, "s_p2v")
    s_p2y = S(ctx, nc, "s_p2y")
    s_mw = S(ctx, nc, "s_mw")
    s_p3 = S(ctx, nc, "s_p3")
    s_p3s = S(ctx, nc, "s_p3s")
    s_p3v = S(ctx, nc, "s_p3v")
    s_p4 = S(ctx, nc, "s_p4")
    s_vq = S(ctx, nc, "s_vq")
    s_tb = S(ctx, nc, "s_tb")
    s_eg = S(ctx, nc, "s_eg")
    s_cc2 = S(ctx, nc, "s_cc2")
    s_p5g = S(ctx, nc, "s_p5g")
    s_v5 = S(ctx, nc, "s_v5")
    s_ld5 = S(ctx, nc, "s_ld5")
    s_p5 = S(ctx, nc, "s_p5")

    with nc.Block("P0") as blk:
        @blk.sync
        def _(sy):
            for dst, src in [(ident_sb, ident_c), (u128_sb, u128_c),
                             (ones128b_sb, ones128b_c), (onescol_sb, onescol_c),
                             (onesrow_sb, onesrow_c), (iota16_sb, iota16_c),
                             (iotaB_sb, iotaB_c), (mask_sb, mask_c)]:
                s_cst.incd(sy.dma_start(out=dst[:, :], in_=src.ap()))
            s_cst.incd(sy.dma_start(
                out=rw_sb[:, :].rearrange("p (c e) -> p c e", e=E),
                in_=rw_in.rearrange("(c p) e -> p c e", p=128)))
            s_cst.incd(sy.dma_start(out=rb_bc[0:1, :], in_=rb_in[:, :]))
            s_cst.incd(sy.dma_start(out=cmask_sb[:, :], in_=cmask_in[:, :]))
            s_cst.incd(sy.dma_start(out=gidb_sb[:, :], in_=gidb_in[:, :]))
            if moe == "sharded":
                s_cst.incd(sy.dma_start(out=myexp_sb[:, :], in_=myexp_in[:, :]))

        @blk.tensor
        def _(t):
            s_cst.wait(t)
            for c in range(NCH):
                mm = t.matmul(ps_ct[:, :], onescol_sb[:, :],
                              rw_sb[:, c * E:(c + 1) * E],
                              start=(c == 0), stop=(c == NCH - 1))
            s_pe.inc(mm)
            s_pe.inc(t.matmul(ps_bc[:, :], onesrow_sb[:, :], rb_bc[0:1, :],
                              start=True, stop=True))

        @blk.vector
        def _(v):
            s_pe.wait_at(v, 1)
            s_v.inc(v.tensor_copy(ct_row[:, :], ps_ct[:, :]))
            s_pe.wait_at(v, 2)
            s_v.inc(v.tensor_copy(rb_bc[:, :], ps_bc[:, :]))

        @blk.tensor
        def _(t):
            s_v.wait(t)
            s_pe.inc(t.matmul(ps_bc[:, :], onesrow_sb[:, :], ct_row[:, :],
                              start=True, stop=True))

        @blk.vector
        def _(v):
            s_pe.wait(v)
            s_v.inc(v.tensor_copy(cs_bc[:, :], ps_bc[:, :]))
            v.memset(prefix_bc[:, :], 0.0)
            s_v.inc(v.memset(stat[:, :], 0.0))

    for i in range(NT):
        j = i % 2
        with nc.Block(f"P1_{i}") as blk:
            vp, sp, pp = s_v.n, s_s.n, s_pe.n

            @blk.sync
            def _(sy, i=i, j=j, vp=vp, sp=sp, pp=pp):
                if i >= 2:
                    s_v.wait_at(sy, vp)
                    s_s.wait_at(sy, sp)
                    s_pe.wait_at(sy, pp)
                    s_acc.wait_at(sy, 16 * (i - 1))
                s_x.incd(sy.dma_start(out=xa[j][:, :],
                                      in_=x_tok[i * 128:(i + 1) * 128, :]))
                s_x.incd(sy.dma_start(
                    out=xqb[j][:, :].rearrange("p (c n) -> p c n", n=128),
                    in_=xq_in[i, :, :].rearrange("(c p) n -> p c n", p=128)))

            @blk.scalar
            def _(s, i=i, j=j, vp=vp):
                s_x.wait_at(s, (2 * i + 2) * 16)
                s_v.wait_at(s, vp)
                s_sq.inc(s.activation(sqscr[:, :], xa[j][:, :], AF.Square))

            @blk.vector
            def _(v, i=i, j=j):
                C = Chain(s_v, v)
                s_x.wait_at(v, (2 * i + 2) * 16)
                C(v.reduce_sum(stat[:, 0:1], xa[j][:, :], axis=AX.X))
                C(v.tensor_scalar_mul(stat[:, 2:3], stat[:, 0:1], 1.0 / D))
                s_sq.wait(v)
                C(v.reduce_sum(stat[:, 1:2], sqscr[:, :], axis=AX.X))
                C(v.tensor_mul(stat[:, 3:4], stat[:, 2:3], stat[:, 2:3]))
                C(v.scalar_tensor_tensor(stat[:, 3:4], stat[:, 1:2], 1.0 / D,
                                         stat[:, 3:4], ALU.mult, ALU.subtract))
                C(v.tensor_scalar_add(stat[:, 3:4], stat[:, 3:4], 1e-5))

            @blk.scalar
            def _(s, i=i, j=j):
                Cs = Chain(s_s, s)
                s_v.wait(s)
                Cs(s.activation(stat[:, 4:5], stat[:, 3:4], AF.Ln))
                Cs(s.activation(stat[:, 4:5], stat[:, 4:5], AF.Exp, scale=-0.5))

            @blk.vector
            def _(v, i=i, j=j):
                C = Chain(s_v, v)
                s_s.wait(v)
                C(v.tensor_scalar(xnf[:, :], xa[j][:, :], stat[:, 2:3],
                                  stat[:, 4:5], ALU.subtract, ALU.mult))
                C(v.tensor_copy(xnb[j][:, :], xnf[:, :]))
                C(v.tensor_add(acc0[j][:, :], xa[j][:, :], xnf[:, :]))

            @blk.sync
            def _(sy, i=i, j=j):
                s_v.wait(sy)
                s_acc.incd(sy.dma_start(out=acc_d[i * 128:(i + 1) * 128, :],
                                        in_=acc0[j][:, :]))

            if moe == "sharded":
                @blk.gpsimd
                def _(gg, i=i, j=j):
                    s_v.wait(gg)
                    s_g.incd(gg.dma_start(out=xn_in_d[i * 128:(i + 1) * 128, :],
                                          in_=xnb[j][:, :]))

            ev_base = s_s.reserve(NCH)
            pe_tr_base = s_pe.n

            @blk.tensor
            def _(t, i=i, j=j, ev_base=ev_base):
                s_v.wait(t)
                for c in range(NCH):
                    if c >= 2:
                        s_s.wait_at(t, ev_base + c - 1)
                    s_pe.inc(t.transpose(ps_tr[c % 2][:, :],
                                         xnb[j][:, c * 128:(c + 1) * 128],
                                         ident_sb[:, :]))

            @blk.scalar
            def _(s, i=i, j=j, ev_base=ev_base, pe_tr_base=pe_tr_base):
                for c in range(NCH):
                    s_pe.wait_at(s, pe_tr_base + c + 1)
                    s_s.inc_at(s.copy(xnT[c][:, i * 128:(i + 1) * 128],
                                      ps_tr[c % 2][:, :]))

            @blk.gpsimd
            def _(gg, i=i, j=j, ev_base=ev_base):
                s_s.wait_at(gg, ev_base + NCH)
                for c in range(NCH):
                    s_g.incd(gg.dma_start(
                        out=xnT_d[c, :, i * 128:(i + 1) * 128],
                        in_=xnT[c][:, i * 128:(i + 1) * 128]))

            @blk.tensor
            def _(t, i=i, j=j):
                s_x.wait_at(t, (2 * i + 2) * 16)
                s_v.wait_at(t, s_v.n)
                for c in range(NCH):
                    mm = t.matmul(ps_lg[:, :], xqb[j][:, c * 128:(c + 1) * 128],
                                  rw_sb[:, c * E:(c + 1) * E],
                                  start=(c == 0), stop=(c == NCH - 1))
                s_pe.inc(mm)

            @blk.vector
            def _(v, i=i, j=j):
                C = Chain(s_v, v)
                s_pe.wait(v)
                C(v.tensor_scalar(t16[:, :], cs_bc[:, :], stat[:, 2:3], None,
                                  ALU.mult))
                C(v.tensor_sub(l16[:, :], ps_lg[:, :], t16[:, :]))
                C(v.tensor_scalar(l16[:, :], l16[:, :], stat[:, 4:5], None,
                                  ALU.mult))
                C(v.tensor_add(l16[:, :], l16[:, :], rb_bc[:, :]))
                C(v.max(m8[:, :], l16[:, :]))
                C(v.tensor_scalar_mul(stat[:, 6:7], m8[:, 0:1], -1.0))
                C(v.tensor_sub(stat[:, 7:8], m8[:, 1:2], m8[:, 0:1]))

            @blk.scalar
            def _(s, i=i, j=j):
                Cs = Chain(s_s, s)
                s_v.wait(s)
                Cs(s.activation(e16[:, :], l16[:, :], AF.Exp, bias=stat[:, 6:7]))
                Cs(s.activation(stat[:, 8:9], stat[:, 7:8], AF.Exp))

            @blk.vector
            def _(v, i=i, j=j):
                C = Chain(s_v, v)
                s_s.wait(v)
                C(v.reduce_sum(stat[:, 9:10], e16[:, :], axis=AX.X))
                C(v.scalar_tensor_tensor(stat[:, 10:11], stat[:, 9:10], 1e-9,
                                         stat[:, 8:9], ALU.mult, ALU.add))
                C(v.tensor_scalar_add(stat[:, 10:11], stat[:, 10:11], 1.0))
                C(v.reciprocal(stat[:, 10:11], stat[:, 10:11]))
                C(v.tensor_mul(stat[:, 11:12], stat[:, 8:9], stat[:, 10:11]))
                C(v.tensor_scalar(u16[:, :], l16[:, :], m8[:, 0:1], None,
                                  ALU.is_ge))
                C(v.scalar_tensor_tensor(a16[:, :], u16[:, :], -BIG,
                                         iotaB_sb[:, :], ALU.mult, ALU.add))
                C(v.tensor_reduce(stat[:, 12:13], a16[:, :], axis=AX.X,
                                  op=ALU.min))
                C(v.tensor_scalar(oh1[:, :], iota16_sb[:, :], stat[:, 12:13],
                                  None, ALU.is_equal))
                C(v.tensor_scalar(u16[:, :], l16[:, :], m8[:, 1:2], None,
                                  ALU.is_ge))
                C(v.scalar_tensor_tensor(a16[:, :], u16[:, :], -BIG,
                                         iotaB_sb[:, :], ALU.mult, ALU.add))
                C(v.scalar_tensor_tensor(a16[:, :], oh1[:, :], 2 * BIG,
                                         a16[:, :], ALU.mult, ALU.add))
                C(v.tensor_reduce(stat[:, 13:14], a16[:, :], axis=AX.X,
                                  op=ALU.min))
                C(v.tensor_scalar(oh2[:, :], iota16_sb[:, :], stat[:, 13:14],
                                  None, ALU.is_equal))
                C(v.tensor_add(ohc[:, :], oh1[:, :], oh2[:, :]))

            @blk.tensor
            def _(t, i=i, j=j):
                s_v.wait(t)
                t.matmul(ps_cs[:, :], u128_sb[:, :], ohc[:, :],
                         start=True, stop=True)
                s_pe.inc(t.matmul(ps_ct[:, :], ones128b_sb[:, :], ohc[:, :],
                                  start=True, stop=True))

            @blk.vector
            def _(v, i=i, j=j):
                C = Chain(s_v, v)
                s_pe.wait(v)
                C(v.tensor_add(posf[:, :], ps_cs[:, :], prefix_bc[:, :]))
                C(v.tensor_mul(t16[:, :], posf[:, :], oh1[:, :]))
                C(v.reduce_sum(route[:, 6 * i + 1:6 * i + 2], t16[:, :],
                               axis=AX.X))
                C(v.tensor_mul(t16[:, :], posf[:, :], oh2[:, :]))
                C(v.reduce_sum(route[:, 6 * i + 4:6 * i + 5], t16[:, :],
                               axis=AX.X))
                v.tensor_copy(route[:, 6 * i + 0:6 * i + 1], stat[:, 12:13])
                v.tensor_copy(route[:, 6 * i + 2:6 * i + 3], stat[:, 10:11])
                v.tensor_copy(route[:, 6 * i + 3:6 * i + 4], stat[:, 13:14])
                v.tensor_copy(route[:, 6 * i + 5:6 * i + 6], stat[:, 11:12])
                C(v.tensor_copy(ct_row[:, :], ps_ct[:, :]))

            @blk.tensor
            def _(t, i=i, j=j):
                s_v.wait(t)
                s_pe.inc(t.matmul(ps_bc[:, :], onesrow_sb[:, :], ct_row[:, :],
                                  start=True, stop=True))

            @blk.vector
            def _(v, i=i, j=j):
                s_pe.wait(v)
                s_v.inc(v.tensor_add(prefix_bc[:, :], prefix_bc[:, :],
                                     ps_bc[:, :]))

    L = dict(locals())
    L["p1_acc_end"] = s_acc.n

    with nc.Block("C0") as blk:
        @blk.vector
        def _(v):
            s_v.inc(v.tensor_copy(ct_row[:, :], prefix_bc[0:1, :]))

        @blk.gpsimd
        def _(gg):
            s_v.wait(gg)
            s_g.incd(gg.dma_start(out=cnt_in_d.ap(), in_=ct_row[:, :]))
            s_g.wait(gg)
            s_cc.inc(gg.collective_compute(
                "AllGather", ALU.bypass, replica_groups=[list(range(NCORE))],
                ins=[cnt_in_d.ap().opt()], outs=[cnt_out_d.ap().opt()]))

        @blk.sync
        def _(sy):
            s_cc.wait(sy)
            s_cst.incd(sy.dma_start(out=cnt_sb[:, :], in_=cnt_out_d.ap()))

        @blk.tensor
        def _(t):
            s_cst.wait(t)
            s_v.wait(t)
            s_pe.inc(t.matmul(ps_ct[:, :], cmask_sb[:, :], cnt_sb[:, :],
                              start=True, stop=True))

        @blk.vector
        def _(v):
            s_pe.wait(v)
            s_v.inc(v.tensor_copy(ct_row[:, :], ps_ct[:, :]))

        @blk.tensor
        def _(t):
            s_v.wait(t)
            s_pe.inc(t.matmul(ps_bc[:, :], onesrow_sb[:, :], ct_row[:, :],
                              start=True, stop=True))

        @blk.vector
        def _(v):
            s_pe.wait(v)
            s_v.inc(v.tensor_copy(cpre_bc[:, :], ps_bc[:, :]))

    with nc.Block("P19") as blk:
        @blk.vector
        def _(v):
            C = Chain(s_v, v)
            for i in range(NT):
                for k in range(2):
                    cidx = route[:, 6 * i + 3 * k:6 * i + 3 * k + 1]
                    cpos = route[:, 6 * i + 3 * k + 1:6 * i + 3 * k + 2]
                    cval = route[:, 6 * i + 3 * k + 2:6 * i + 3 * k + 3]
                    ohk = oh1 if k == 0 else oh2
                    C(v.tensor_scalar(ohk[:, :], iota16_sb[:, :], cidx, None,
                                      ALU.is_equal))
                    C(v.tensor_mul(t16[:, :], ohk[:, :], cpre_bc[:, :]))
                    C(v.reduce_sum(stat[:, 0:1], t16[:, :], axis=AX.X))
                    C(v.tensor_add(stat[:, 1:2], cpos, stat[:, 0:1]))
                    C(v.tensor_scalar(stat[:, 2:3], stat[:, 1:2], float(CAP),
                                      None, ALU.is_lt))
                    C(v.tensor_mul(gates[:, 2 * i + k:2 * i + k + 1], cval,
                                   stat[:, 2:3]))
                    if moe == "sharded":
                        C(v.tensor_scalar(stat[:, 3:4], cidx, float(CAP), None,
                                          ALU.mult))
                        C(v.tensor_add(stat[:, 3:4], stat[:, 3:4], stat[:, 1:2]))
                        C(v.tensor_scalar(stat[:, 4:5], stat[:, 2:3], -OOBF,
                                          None, ALU.mult))
                        C(v.tensor_scalar_add(stat[:, 4:5], stat[:, 4:5], OOBF))
                        C(v.tensor_add(stat[:, 3:4], stat[:, 3:4], stat[:, 4:5]))
                        C(v.tensor_copy(retidx[:, 2 * i + k:2 * i + k + 1],
                                        stat[:, 3:4]))
                        v.tensor_copy(rt_sb[:, 4 * i + 2 * k:4 * i + 2 * k + 1],
                                      cidx)
                        C(v.tensor_copy(
                            rt_sb[:, 4 * i + 2 * k + 1:4 * i + 2 * k + 2],
                            stat[:, 1:2]))
                if moe == "dense":
                    C(v.tensor_scalar(t16[:, :], oh1[:, :],
                                      gates[:, 2 * i:2 * i + 1], None, ALU.mult))
                    C(v.tensor_scalar(a16[:, :], oh2[:, :],
                                      gates[:, 2 * i + 1:2 * i + 2], None,
                                      ALU.mult))
                    C(v.tensor_add(G_sb[:, i * E:(i + 1) * E], t16[:, :],
                                   a16[:, :]))
            s_v.inc(v.tensor_copy(stat[:, 15:16], stat[:, 0:1]))

    s_ccx = S(ctx, nc, "s_ccx")
    if moe == "sharded":
        with nc.Block("CC1") as blk:
            @blk.gpsimd
            def _(gg):
                s_v.wait(gg)
                s_g.incd(gg.dma_start(out=rt_in_d.ap(), in_=rt_sb[:, :]))
                s_g.wait(gg)
                s_ccx.incd(gg.collective_compute(
                    "AllGather", ALU.bypass,
                    replica_groups=[list(range(NCORE))],
                    ins=[xn_in_d.ap().bitcast(F32).opt()],
                    outs=[xn_all_d.ap().bitcast(F32).opt()]))
                s_ccx.incd(gg.collective_compute(
                    "AllGather", ALU.bypass,
                    replica_groups=[list(range(NCORE))],
                    ins=[rt_in_d.ap().opt()], outs=[rt_all_d.ap().opt()]))

    L = dict(locals())
    L["p1_acc_end"] = NT * 16
    L["dbg_p2"] = cfg.get("dbg_p2", False)
    if stop_after == "P1":
        emit_p5(nc, L)
        return nc, ctx, L
    ctx1.close()

    L["sb2"], L["ps2"] = sb2x, ps2x
    ctx2t = ExitStack()
    L["sb2t"], L["ps2t"] = mk_sb(ctx2t), mk_ps(ctx2t)
    emit_p15(nc, L)
    ctx2t.close()
    if stop_after == "P15":
        emit_p5(nc, L)
        return nc, ctx, L
    emit_p2(nc, L)
    ctx2.close()
    if stop_after == "P2":
        emit_p5(nc, L)
        return nc, ctx, L

    ctx3 = ExitStack()
    L["sb3"], L["ps3"] = mk_sb(ctx3), mk_ps(ctx3)
    L["Chain"] = Chain
    if moe == "sharded":
        L["s_ccx"] = s_ccx
        emit_p4s_prep(nc, L)
    emit_p3(nc, L)
    if stop_after == "P3":
        emit_p5(nc, L)
        return nc, ctx, L
    if moe == "dense":
        emit_p4_dense(nc, L)
    else:
        emit_p4_sharded(nc, L)
    ctx3.close()
    emit_p5(nc, L)
    return nc, ctx, L





def emit_p15(nc, L):
    g = lambda k: L[k]
    T, D, DH, NP, NCH, NT, NB = (L["T"], L["D"], L["DH"], L["NP"], L["NCH"],
                                 L["NT"], L["NB"])
    QSCALE, VSCALE = L["QSCALE"], L["VSCALE"]
    xnT, ident_sb = L["xnT"], L["ident_sb"]
    ones128b_sb, onesrow_sb = L["ones128b_sb"], L["onesrow_sb"]
    negM, stat = L["negM"], L["stat"]
    qT_d = L["qT_d"]
    Qp_in, Kp_in, Vp_in, Wp_in = L["Qp_in"], L["Kp_in"], L["Vp_in"], L["Wp_in"]
    s_v, s_s, s_pe, s_wl, s_q2, s_kv, s_g = (L["s_v"], L["s_s"], L["s_pe"],
                                        L["s_wl"], L["s_q2"], L["s_kv"], L["s_g"])
    sb2, ps2 = L["sb2"], L["ps2"]
    sb2t, ps2t = L["sb2t"], L["ps2t"]

    kT_sb = [sb2(f"kT{p}", [128, T], BF16) for p in range(NP)]
    ps_q = ps2t("ps_q", [128, 512])
    ps_k = ps2t("ps_k", [128, 512])
    qw_sb = sb2t("qw_sb", [128, NCH * 128], BF16)
    kw_sb = sb2t("kw_sb", [128, NCH * 128], BF16)
    vw_sb = sb2t("vw_sb", [128, NCH * 128], BF16)
    qevac = sb2t("qevac", [128, 512], BF16)
    sqs = sb2t("sqs", [128, 512], BF16)
    v_stage = sb2t("v_stage", [128, 520], BF16)
    mrow = sb2t("mrow", [1, 4], F32)
    L["kT_sb"] = kT_sb
    v_d = L["v_d"]

    ps_v = ps2t("ps_v", [128, 128])
    ps_m = ps2t("ps_m", [1, 512])
    ps_nm = ps2t("ps_nm", [128, 1])

    with nc.Block("P15") as blk:
        @blk.vector
        def _(v):
            s_v.wait(v)
            s_s.wait(v)
            v.memset(v_stage[:, :], 1.0)
            s_v.inc(v.memset(mrow[:, :], 0.0))

        for p in range(NP):
            @blk.gpsimd
            def _(gg, p=p):
                if p >= 1:
                    s_pe.wait(gg)
                s_wl.incd(gg.dma_start(
                    out=qw_sb[:, :].rearrange("p (c n) -> p c n", n=2 * DH),
                    in_=Qp_in[p, :, :].rearrange("(c p) n -> p c n", p=128)))
                s_wl.incd(gg.dma_start(
                    out=kw_sb[:, :].rearrange("p (c n) -> p c n", n=2 * DH),
                    in_=Kp_in[p, :, :].rearrange("(c p) n -> p c n", p=128)))
                s_wl.incd(gg.dma_start(
                    out=vw_sb[:, :].rearrange("p (c n) -> p c n", n=2 * DH),
                    in_=Vp_in[p, :, :].rearrange("(c p) n -> p c n", p=128)))

            for b in range(NB):
                @blk.tensor
                def _(t, p=p, b=b):
                    s_wl.wait(t)
                    s_s.wait(t)
                    s_v.wait(t)
                    for c in range(NCH):
                        t.matmul(ps_q[:, :], qw_sb[:, c * 128:(c + 1) * 128],
                                 xnT[c][:, b * 512:(b + 1) * 512],
                                 start=(c == 0), stop=(c == NCH - 1))
                    for c in range(NCH):
                        mm = t.matmul(ps_k[:, :], kw_sb[:, c * 128:(c + 1) * 128],
                                      xnT[c][:, b * 512:(b + 1) * 512],
                                      start=(c == 0), stop=(c == NCH - 1))
                    s_pe.inc(mm)

                @blk.scalar
                def _(s, p=p, b=b):
                    s_pe.wait(s)
                    s.activation(qevac[:, :], ps_q[:, :], AF.Copy, scale=QSCALE)
                    s.copy(kT_sb[p][:, b * 512:(b + 1) * 512], ps_k[:, :])
                    s_s.inc(s.activation(sqs[:, :], qevac[:, :], AF.Square))

                @blk.sync
                def _(sy, p=p, b=b):
                    s_s.wait(sy)
                    s_q2.incd(sy.dma_start(
                        out=qT_d[p, :, b * 512:(b + 1) * 512], in_=qevac[:, :]))

                @blk.tensor
                def _(t, p=p, b=b):
                    s_s.wait(t)
                    s_pe.inc(t.matmul(ps_m[:, :], ones128b_sb[:, :], sqs[:, :],
                                      start=True, stop=True))

                @blk.vector
                def _(v, p=p, b=b):
                    s_pe.wait(v)
                    s_v.inc(v.reduce_max(stat[0:1, 0:1], ps_m[:, :], axis=AX.X))
                    s_v.wait(v)
                    s_v.inc(v.tensor_max(mrow[:, 0:1], mrow[:, 0:1],
                                         stat[0:1, 0:1]))

                @blk.scalar
                def _(s, p=p, b=b):
                    s_v.wait(s)
                    s_s.inc(s.activation(sqs[:, :],
                                         kT_sb[p][:, b * 512:(b + 1) * 512],
                                         AF.Square))

                @blk.tensor
                def _(t, p=p, b=b):
                    s_s.wait(t)
                    s_pe.inc(t.matmul(ps_m[:, :], ones128b_sb[:, :], sqs[:, :],
                                      start=True, stop=True))

                @blk.vector
                def _(v, p=p, b=b):
                    s_pe.wait(v)
                    s_v.inc(v.reduce_max(stat[0:1, 0:1], ps_m[:, :], axis=AX.X))
                    s_v.wait(v)
                    s_v.inc(v.tensor_max(mrow[:, 1:2], mrow[:, 1:2],
                                         stat[0:1, 0:1]))

                sv_base = s_s.reserve(4)
                pe_v_base = s_pe.n

                @blk.tensor
                def _(t, p=p, b=b, sv_base=sv_base):
                    s_v.wait(t)
                    for tt in range(4):
                        i = b * 4 + tt
                        if tt >= 1:
                            s_s.wait_at(t, sv_base + tt)
                        for c in range(NCH):
                            mm = t.matmul(ps_v[:, :],
                                          xnT[c][:, i * 128:(i + 1) * 128],
                                          vw_sb[:, c * 128:(c + 1) * 128],
                                          start=(c == 0), stop=(c == NCH - 1))
                        s_pe.inc(mm)

                @blk.scalar
                def _(s, p=p, b=b, sv_base=sv_base, pe_v_base=pe_v_base):
                    s_g.wait_at(s, s_g.n)
                    for tt in range(4):
                        s_pe.wait_at(s, pe_v_base + tt + 1)
                        s.activation(v_stage[:, tt * 130:tt * 130 + 64],
                                     ps_v[:, 0:64], AF.Copy, scale=VSCALE)
                        s_s.inc_at(s.activation(
                            v_stage[:, tt * 130 + 65:tt * 130 + 129],
                            ps_v[:, 64:128], AF.Copy, scale=VSCALE))

                @blk.gpsimd
                def _(gg, p=p, b=b, sv_base=sv_base):
                    s_s.wait_at(gg, sv_base + 4)
                    s_g.incd(gg.dma_start(
                        out=v_d[p, :, b * 520:(b + 1) * 520],
                        in_=v_stage[:, :]))

        @blk.vector
        def _(v):
            s_v.wait(v)
            s_v.inc(v.tensor_mul(mrow[:, 2:3], mrow[:, 0:1], mrow[:, 1:2]))

        @blk.scalar
        def _(s):
            s_v.wait(s)
            s_s.inc(s.activation(mrow[:, 3:4], mrow[:, 2:3], AF.Sqrt))

        @blk.vector
        def _(v):
            s_s.wait(v)
            s_v.inc(v.tensor_scalar_mul(mrow[:, 3:4], mrow[:, 3:4], -1.02))

        @blk.tensor
        def _(t):
            s_v.wait(t)
            s_pe.inc(t.matmul(ps_nm[:, :], onesrow_sb[:, :], mrow[:, 3:4],
                              start=True, stop=True))

        @blk.vector
        def _(v):
            s_pe.wait(v)
            s_kv.inc(v.tensor_copy(negM[:, :], ps_nm[:, :]))


def emit_p2(nc, L):
    g = lambda k: L[k]
    T, D, NP, NB, NT = L["T"], L["D"], L["NP"], L["NB"], L["NT"]
    ZSCALE = L["ZSCALE"]
    kT_sb = L["kT_sb"]
    v_d, Wp_in = L["v_d"], L["Wp_in"]
    s_wl, s_g = L["s_wl"], L["s_g"]
    negM, mask_sb, onesrow_sb = L["negM"], L["mask_sb"], L["onesrow_sb"]
    qT_d, acc_d = L["qT_d"], L["acc_d"]
    s_q2, s_kv, s_acc, s_accg = L["s_q2"], L["s_kv"], L["s_acc"], L["s_accg"]
    s_p2a, s_p2s, s_p2v, s_p2y = L["s_p2a"], L["s_p2s"], L["s_p2v"], L["s_p2y"]
    sb2, ps2 = L["sb2"], L["ps2"]
    ps_att = [ps2("ps_attA", [128, 1024]), ps2("ps_attB", [128, 1024])]
    ps_y = [ps2("ps_y0", [65, 512]), ps2("ps_y1", [65, 512])]
    ps_z = ps2("ps_z", [128, D])

    Wp_sb = sb2("Wp_sb", [128, NP * D], BF16)
    qsb = sb2("qsb", [128, NP * 512], BF16)
    vq = [sb2("vq0", [128, NT * 130], BF16), sb2("vq1", [128, NT * 130], BF16)]
    s_vq = L["s_vq"]
    pT = [sb2("pT0", [128, 1024], BF16), sb2("pT1", [128, 1024], BF16)]
    ysb = [sb2("ysb0", [64, 512], BF16), sb2("ysb1", [64, 512], BF16)]
    yTn = sb2("yTn", [128, NP * 512], BF16)
    rs_row = sb2("rs_row", [1, 512], F32)
    zsb = [sb2("zsb0", [128, D], F32), sb2("zsb1", [128, D], F32)]

    p2_acc_base = s_accg.n
    with nc.Block("P2W") as blk:
        @blk.gpsimd
        def _(gg):
            s_wl.incd(gg.dma_start(
                out=Wp_sb[:, :].rearrange("p (np d) -> p np d", np=NP),
                in_=Wp_in.rearrange("np p d -> p np d")))

    for qb in range(NB):
        nk = 4 * qb + 4
        with nc.Block(f"P2_{qb}") as blk:
            a0, sB, v0, y0 = s_p2a.n, s_p2s.n, s_p2v.n, s_p2y.n
            s_p2a.reserve(NP * (nk + 2) + 4)
            s_p2s.reserve(NP * (nk + 2) + 4)
            s_p2v.reserve(NP * 8)
            s_p2y.reserve(NP * nk)
            A0 = lambda p, a0=a0, nk=nk: a0 + p * (nk + 2)
            S0 = lambda p, sB=sB, nk=nk: sB + p * (nk + 2)
            V0 = lambda p, v0=v0: v0 + p * 8
            Y0 = lambda p, y0=y0, nk=nk: y0 + p * nk
            az = a0 + NP * (nk + 2)
            sz = sB + NP * (nk + 2)

            vqb = s_vq.n

            @blk.sync
            def _(sy, qb=qb, a0=a0, nk=nk, Y0=Y0, y0=y0):
                s_p2a.wait_at(sy, a0)
                s_q2.wait(sy)
                for p in range(NP):
                    s_q2.incd(sy.dma_start(
                        out=qsb[:, p * 512:(p + 1) * 512],
                        in_=qT_d[p, :, qb * 512:(qb + 1) * 512]))
                for p in range(NP):
                    if p >= 2:
                        s_p2y.wait_at(sy, Y0(p - 1))
                    s_vq.incd(sy.dma_start(
                        out=vq[p % 2][:, 0:nk * 130],
                        in_=v_d[p, :, 0:nk * 130]))

            @blk.tensor
            def _(t, qb=qb, nk=nk, A0=A0, S0=S0, V0=V0, Y0=Y0, sB=sB, az=az,
                  sz=sz, v0=v0, vqb=vqb):
                s_q2.wait(t)
                s_kv.wait(t)
                L["s_s"].wait_at(t, L["s_s"].n)
                L["s_v"].wait_at(t, L["s_v"].n)
                s_p2s.wait_at(t, sB)
                for p in range(NP):

                    def pv(kt, p=p):
                        s_p2s.wait_at(t, S0(p) + kt + 1)
                        if kt >= nk - 4:
                            s_p2v.wait_at(t, V0(p) + (kt - (nk - 4)) + 1)
                        if kt == 0:
                            s_vq.wait_at(t, vqb + 16 * (p + 1))
                        t.matmul(ps_y[0][:, :],
                                 vq[p % 2][:, kt * 130:kt * 130 + 65],
                                 pT[kt % 2][:, 0:512], start=(kt == 0),
                                 stop=(kt == nk - 1))
                        mm = t.matmul(ps_y[1][:, :],
                                      vq[p % 2][:, kt * 130 + 65:kt * 130 + 130],
                                      pT[kt % 2][:, 512:1024], start=(kt == 0),
                                      stop=(kt == nk - 1))
                        s_p2y.inc_at(mm)

                    for kt in range(nk):
                        if kt >= 2:
                            s_p2s.wait_at(t, S0(p) + kt - 1)
                        elif p > 0:
                            s_p2s.wait_at(t, S0(p))
                            s_p2v.wait_at(t, V0(p))
                        t.matmul(ps_att[kt % 2][:, 0:512],
                                 kT_sb[p][0:64, kt * 128:(kt + 1) * 128],
                                 qsb[0:64, p * 512:(p + 1) * 512],
                                 start=True, stop=True, tile_position=(0, 0))
                        mm = t.matmul(ps_att[kt % 2][:, 512:1024],
                                      kT_sb[p][64:128, kt * 128:(kt + 1) * 128],
                                      qsb[64:128, p * 512:(p + 1) * 512],
                                      start=True, stop=True,
                                      tile_position=(64, 0))
                        s_p2a.inc_at(mm)
                        if kt >= 1:
                            pv(kt - 1)
                    pv(nk - 1)
                    for h in range(2):
                        s_p2v.wait_at(t, V0(p) + 5 + 2 * h)
                        s_p2s.wait_at(t, S0(p) + nk + 1 + h)
                        mm = t.matmul(ps_y[h][0:64, :], onesrow_sb[:, 0:64],
                                      rs_row[:, :], start=True, stop=True)
                        s_p2a.inc_at(mm)
                s_p2v.wait_at(t, v0 + NP * 8)
                dcols = [(a, min(a + 512, D)) for a in range(0, D, 512)]
                for tt in range(4):
                    if tt >= 1:
                        s_p2s.wait_at(t, sz + tt)
                    for p in range(NP):
                        for (ca, cb) in dcols:
                            mm = t.matmul(
                                ps_z[:, ca:cb],
                                yTn[:, p * 512 + tt * 128:p * 512 + (tt + 1) * 128],
                                Wp_sb[:, p * D + ca:p * D + cb],
                                start=(p == 0), stop=(p == NP - 1))
                    s_p2a.inc_at(mm)

            @blk.scalar
            def _(s, qb=qb, nk=nk, A0=A0, S0=S0, V0=V0, Y0=Y0, az=az, sz=sz,
                  p2_acc_base=p2_acc_base):
                if qb == 0:
                    s_kv.wait(s)
                for p in range(NP):
                    for kt in range(nk):
                        s_p2a.wait_at(s, A0(p) + kt + 1)
                        if kt >= 2:
                            s_p2y.wait_at(s, Y0(p) + kt - 1)
                        elif p > 0:
                            s_p2y.wait_at(s, Y0(p))
                        mm = s.activation(pT[kt % 2][:, :], ps_att[kt % 2][:, :],
                                          AF.Exp, bias=negM[:, :])
                        s_p2s.inc_at(mm)
                    s_p2y.wait_at(s, Y0(p) + nk)
                    if p > 0:
                        s_p2v.wait_at(s, V0(p))
                    s_p2s.inc_at(s.copy(ysb[0][:, :], ps_y[0][0:64, :]))
                    s_p2s.inc_at(s.copy(ysb[1][:, :], ps_y[1][0:64, :]))
                for tt in range(4):
                    m = qb * 4 + tt
                    s_p2a.wait_at(s, az + tt + 1)
                    if m >= 2:
                        s_accg.wait_at(s, p2_acc_base + 16 * (m - 1))
                    mm = s.activation(zsb[tt % 2][:, :], ps_z[:, :], AF.Copy,
                                      scale=ZSCALE)
                    s_p2s.inc_at(mm)

            @blk.vector
            def _(v, qb=qb, nk=nk, A0=A0, S0=S0, V0=V0, Y0=Y0):
                for p in range(NP):
                    for d in range(4):
                        kt = nk - 4 + d
                        s_p2s.wait_at(v, S0(p) + kt + 1)
                        v.tensor_mul(pT[kt % 2][:, 0:512], pT[kt % 2][:, 0:512],
                                     mask_sb[:, d * 512:(d + 1) * 512])
                        mm = v.tensor_mul(pT[kt % 2][:, 512:1024],
                                          pT[kt % 2][:, 512:1024],
                                          mask_sb[:, d * 512:(d + 1) * 512])
                        s_p2v.inc_at(mm)
                    for h in range(2):
                        s_p2y.wait_at(v, Y0(p) + nk)
                        mm = v.reciprocal(rs_row[:, :], ps_y[h][64:65, :])
                        s_p2v.inc_at(mm)
                        s_p2v.wait_at(v, V0(p) + 5 + 2 * h)
                        s_p2a.wait_at(v, A0(p) + nk + 1 + h)
                        s_p2s.wait_at(v, S0(p) + nk + 1 + h)
                        mm = v.tensor_mul(
                            yTn[64 * h:64 * h + 64, p * 512:(p + 1) * 512],
                            ysb[h][:, :], ps_y[h][0:64, :])
                        s_p2v.inc_at(mm)

            @blk.gpsimd
            def _(gg, qb=qb, sz=sz, p2_acc_base=p2_acc_base):
                for tt in range(4):
                    s_p2s.wait_at(gg, sz + tt + 1)
                    if qb == 0 and tt == 0:
                        s_acc.wait_at(gg, L["p1_acc_end"])
                    i = qb * 4 + tt
                    s_accg.incd(gg.dma_start(
                        out=acc_d[i * 128:(i + 1) * 128, :],
                        in_=zsb[tt % 2][:, :], accum_op=ALU.add))


def _emit_mlp(nc, blk, L, wgs, wus, wds, token_src, gate_col_fn, accdst,
              rows_base, n_blocks, first_waits, bf16_out=None,
              plain_store=False):
    D, FF, NCH, NFS = L["D"], L["FF"], L["NCH"], L["NFS"]
    s_p3, s_p3s, s_p3v = L["s_p3"], L["s_p3s"], L["s_p3v"]
    s_acc, s_accg = L["s_acc"], L["s_accg"]
    ps_g, ps_u, ps_z3 = L["ps_g3"], L["ps_u3"], L["ps_z3"]
    prod, gsb, zsb3 = L["prod"], L["gsb"], L["zsb3"]

    per_t, per_s, per_v = NFS + 4, NFS + 4, NFS
    tb = s_p3.reserve(n_blocks * per_t)
    sbb = s_p3s.reserve(n_blocks * per_s)
    vb = s_p3v.reserve(n_blocks * per_v)
    acc_base = s_accg.n
    n_tiles_total = n_blocks * 4

    def TB(b, k=0):
        return tb + b * per_t + k

    def SB(b, k=0):
        return sbb + b * per_s + k

    def VB(b, k=0):
        return vb + b * per_v + k

    @blk.tensor
    def _(t):
        for sem, val in first_waits:
            sem.wait_at(t, sem.n if val is None else val)
        for b in range(n_blocks):
            for fs in range(NFS):
                gidx = b * NFS + fs
                if gidx >= 2:
                    pg = gidx - 2
                    pb, pf = pg // NFS, pg % NFS
                    s_p3s.wait_at(t, SB(pb, pf + 1))
                    s_p3v.wait_at(t, VB(pb, pf + 1))
                jj = gidx % 2
                for c in range(NCH):
                    t.matmul(ps_g[jj][:, :],
                             wgs[:, c * FF + fs * 128:c * FF + (fs + 1) * 128],
                             token_src[c][:, b * 512:(b + 1) * 512],
                             start=(c == 0), stop=(c == NCH - 1))
                for c in range(NCH):
                    mm = t.matmul(ps_u[jj][:, :],
                                  wus[:, c * FF + fs * 128:c * FF + (fs + 1) * 128],
                                  token_src[c][:, b * 512:(b + 1) * 512],
                                  start=(c == 0), stop=(c == NCH - 1))
                s_p3.inc_at(mm)
            for tt in range(4):
                s_p3v.wait_at(t, VB(b, NFS))
                if not (b == 0 and tt == 0):
                    pb, pt = (b, tt - 1) if tt > 0 else (b - 1, 3)
                    s_p3s.wait_at(t, SB(pb, NFS + pt + 1))
                dcols = [(a, min(a + 512, D)) for a in range(0, D, 512)]
                for fs in range(NFS):
                    for (ca, cb) in dcols:
                        mm = t.matmul(
                            ps_z3[:, ca:cb],
                            prod[:, fs * 512 + tt * 128:fs * 512 + (tt + 1) * 128],
                            wds[:, fs * D + ca:fs * D + cb],
                            start=(fs == 0), stop=(fs == NFS - 1))
                s_p3.inc_at(mm)

    @blk.scalar
    def _(s):
        for b in range(n_blocks):
            for fs in range(NFS):
                gidx = b * NFS + fs
                jj = gidx % 2
                s_p3.wait_at(s, TB(b, fs + 1))
                if gidx >= 2:
                    pg = gidx - 2
                    s_p3v.wait_at(s, VB(pg // NFS, pg % NFS + 1))
                s_p3s.inc_at(s.activation(gsb[jj][:, :], ps_g[jj][:, :], AF.Silu))
            for tt in range(4):
                i = b * 4 + tt
                s_p3.wait_at(s, TB(b, NFS + tt + 1))
                if i >= 2:
                    s_accg.wait_at(s, acc_base + 16 * (i - 1))
                gc = gate_col_fn(i) if gate_col_fn is not None else None
                if gc is not None:
                    mm = s.activation(zsb3[i % 2][:, :], ps_z3[:, :], AF.Copy,
                                      scale=gc)
                else:
                    mm = s.copy(zsb3[i % 2][:, :], ps_z3[:, :])
                s_p3s.inc_at(mm)

    @blk.vector
    def _(v):
        for b in range(n_blocks):
            for fs in range(NFS):
                gidx = b * NFS + fs
                jj = gidx % 2
                s_p3s.wait_at(v, SB(b, fs + 1))
                mm = v.tensor_mul(prod[:, fs * 512:(fs + 1) * 512], gsb[jj][:, :],
                                  ps_u[jj][:, :])
                s_p3v.inc_at(mm)

    @blk.gpsimd
    def _(gg):
        for b in range(n_blocks):
            for tt in range(4):
                i = b * 4 + tt
                s_p3s.wait_at(gg, SB(b, NFS + tt + 1))
                s_accg.incd(gg.dma_start(
                    out=accdst[(rows_base + i) * 128:(rows_base + i + 1) * 128, :],
                    in_=zsb3[i % 2][:, :],
                    accum_op=(ALU.bypass if plain_store else ALU.add)))

    return TB, SB, VB


def emit_p3(nc, L):
    NCH, NFS, FF, D, NB, T = L["NCH"], L["NFS"], L["FF"], L["D"], L["NB"], L["T"]
    sb3, ps3 = L["sb3"], L["ps3"]
    s_mw, s_q2 = L["s_mw"], L["s_q2"]
    swg_in, swu_in, swd_in = L["swg_in"], L["swu_in"], L["swd_in"]
    xnT_d = L["xnT_d"]

    xnT3 = [sb3(f"xnT3_{c}", [128, T], BF16) for c in range(NCH)]
    L["xnT3"] = xnT3
    wgs = sb3("wgs", [128, NCH * FF], BF16)
    wus = sb3("wus", [128, NCH * FF], BF16)
    wds = sb3("wds", [128, NFS * D], BF16)
    gsb = [sb3("gsb0", [128, 512], BF16), sb3("gsb1", [128, 512], BF16)]
    prod = sb3("prod", [128, NFS * 512], BF16)
    zsb3 = [sb3("zsb3_0", [128, D], F32), sb3("zsb3_1", [128, D], F32)]
    ps_g = [ps3("ps_g3a", [128, 512]), ps3("ps_g3b", [128, 512])]
    ps_u = [ps3("ps_u3a", [128, 512]), ps3("ps_u3b", [128, 512])]
    ps_z3 = ps3("ps_z3", [128, D])
    L.update(wgs=wgs, wus=wus, wds=wds, gsb=gsb, prod=prod, zsb3=zsb3,
             ps_g3=ps_g, ps_u3=ps_u, ps_z3=ps_z3)

    with nc.Block("P3") as blk:
        @blk.sync
        def _(sy):
            L["s_p2a"].wait_at(sy, L["s_p2a"].n)
            L["s_p2s"].wait_at(sy, L["s_p2s"].n)
            L["s_p2v"].wait_at(sy, L["s_p2v"].n)
            for c in range(NCH):
                s_q2.incd(sy.dma_start(out=xnT3[c][:, :], in_=xnT_d[c, :, :]))

        @blk.gpsimd
        def _(gg):
            s_mw.incd(gg.dma_start(
                out=wgs[:, :].rearrange("p (c f) -> p c f", f=FF),
                in_=swg_in.rearrange("(c p) f -> p c f", p=128)))
            s_mw.incd(gg.dma_start(
                out=wus[:, :].rearrange("p (c f) -> p c f", f=FF),
                in_=swu_in.rearrange("(c p) f -> p c f", p=128)))
            s_mw.incd(gg.dma_start(
                out=wds[:, :].rearrange("p (c d) -> p c d", d=D),
                in_=swd_in.rearrange("(c p) d -> p c d", p=128)))

        tb_info = _emit_mlp(
            nc, blk, L, wgs, wus, wds, xnT3, None, L["acc_d"], 0, NB,
            first_waits=[(L["s_mw"], None), (L["s_q2"], None),
                         (L["s_p2a"], None), (L["s_p2s"], None),
                         (L["s_p2v"], None)])
        L["p3_last_t"] = L["s_p3"].n


def emit_p4_dense(nc, L):
    E, NB, NCH, NFS, FF, D = L["E"], L["NB"], L["NCH"], L["NFS"], L["FF"], L["D"]
    s_mw, s_p3 = L["s_mw"], L["s_p3"]
    wg_in, wu_in, wd_in = L["wg_in"], L["wu_in"], L["wd_in"]
    wgs, wus, wds = L["wgs"], L["wus"], L["wds"]
    G_sb = L["G_sb"]

    for e in range(E):
        with nc.Block(f"P4d_{e}") as blk:
            @blk.gpsimd
            def _(gg, e=e):
                s_p3.wait(gg)
                L["s_p3s"].wait(gg)
                L["s_p3v"].wait(gg)
                s_mw.incd(gg.dma_start(
                    out=wgs[:, :].rearrange("p (c f) -> p c f", f=FF),
                    in_=wg_in[e, :, :].rearrange("(c p) f -> p c f", p=128)))
                s_mw.incd(gg.dma_start(
                    out=wus[:, :].rearrange("p (c f) -> p c f", f=FF),
                    in_=wu_in[e, :, :].rearrange("(c p) f -> p c f", p=128)))
                s_mw.incd(gg.dma_start(
                    out=wds[:, :].rearrange("p (c d) -> p c d", d=D),
                    in_=wd_in[e, :, :].rearrange("(c p) d -> p c d", p=128)))

            _emit_mlp(nc, blk, L, wgs, wus, wds, L["xnT3"],
                      (lambda i, e=e: G_sb[:, i * E + e:i * E + e + 1]),
                      L["acc_d"], 0, NB, first_waits=[(s_mw, None)])


def emit_p4s_prep(nc, L):
    NT, T, CAP, NCH = L["NT"], L["T"], L["CAP"], L["NCH"]
    NCORE = 8
    sb3, ps3 = L["sb3"], L["ps3"]
    s_ccx, s_q2, s_v, s_pe = L["s_ccx"], L["s_q2"], L["s_v"], L["s_pe"]
    s_tb = L["s_tb"]
    rt_all_d, tbl_d = L["rt_all_d"], L["tbl_d"]
    myexp_sb, onesrow_sb = L["myexp_sb"], L["onesrow_sb"]
    OOBF = 1.0e9

    rt_t = sb3("rt_t", [128, NT * 4], F32)
    idx_f = sb3("idx_f", [128, NT * 2], F32)
    idx_i = sb3("idx_i", [128, NT * 2], I32)
    gid8 = sb3("gid8", [128, 2 * NT], F32)
    my0c = sb3("my0c", [128, 1], F32)
    my1c = sb3("my1c", [128, 1], F32)
    tmpA = sb3("tmpA", [128, NT], F32)
    tmpB = sb3("tmpB", [128, NT], F32)
    ini_t = sb3("ini_t", [128, 2 * (2 * CAP // 128)], F32)
    gidt = sb3("gidt", [128, NT], F32)
    ps_my = ps3("ps_my", [128, 2])

    with nc.Block("P4prep") as blk:
        @blk.tensor
        def _(t):
            s_pe.inc(t.matmul(ps_my[:, :], onesrow_sb[:, :], myexp_sb[:, :],
                              start=True, stop=True))

        @blk.vector
        def _(v):
            C = L["Chain"](s_v, v)
            s_pe.wait(v)
            C(v.tensor_copy(my0c[:, :], ps_my[:, 0:1]))
            C(v.tensor_copy(my1c[:, :], ps_my[:, 1:2]))
            C(v.memset(ini_t[:, :], OOBF))
            s_v.inc(v.memset(tmpA[:, :], 0.0))

        @blk.gpsimd
        def _(gg):
            s_v.wait(gg)
            gg.iota(gidt[:, :], pattern=[[128, NT]], base=0,
                    channel_multiplier=1,
                    allow_small_or_imprecise_dtypes=True).then_inc(s_tb.h, 1)
            s_tb.n += 1
            s_tb.incd(gg.dma_start(
                out=tbl_d.ap().rearrange("(p n) two -> p n two", p=128),
                in_=ini_t[:, :].rearrange("p (n two) -> p n two", two=2)))

        for c8 in range(NCORE):
            @blk.sync
            def _(sy, c8=c8):
                if c8 == 0:
                    s_ccx.wait(sy)
                else:
                    s_v.wait(sy)
                s_q2.incd(sy.dma_start(
                    out=rt_t[:, :], in_=rt_all_d[c8 * 128:(c8 + 1) * 128, :]))

            @blk.vector
            def _(v, c8=c8):
                C = L["Chain"](s_v, v)
                s_q2.wait(v)
                s_tb.wait(v)
                for k in range(2):
                    ecol = rt_t[:, 2 * k::4]
                    gcol = rt_t[:, 2 * k + 1::4]
                    C(v.tensor_scalar(tmpA[:, :], ecol, my0c[:, 0:1], None,
                                      ALU.is_equal))
                    C(v.tensor_scalar(tmpB[:, :], ecol, my1c[:, 0:1], None,
                                      ALU.is_equal))
                    C(v.scalar_tensor_tensor(tmpB[:, :], tmpB[:, :], float(CAP),
                                             gcol, ALU.mult, ALU.add))
                    C(v.tensor_add(tmpA[:, :], tmpA[:, :],
                                   rt_t[:, 2 * k::4])) if False else None
                    C(v.tensor_scalar(idx_f[:, k * NT:(k + 1) * NT], ecol,
                                      my0c[:, 0:1], None, ALU.is_equal))
                    C(v.tensor_scalar(tmpA[:, :], ecol, my1c[:, 0:1], None,
                                      ALU.is_equal))
                    C(v.tensor_add(tmpA[:, :], tmpA[:, :],
                                   idx_f[:, k * NT:(k + 1) * NT]))
                    C(v.tensor_scalar(tmpA[:, :], tmpA[:, :], -OOBF, None,
                                      ALU.mult))
                    C(v.tensor_scalar_add(tmpA[:, :], tmpA[:, :], OOBF))
                    C(v.tensor_add(tmpB[:, :], tmpB[:, :], tmpA[:, :]))
                    C(v.tensor_scalar(tmpA[:, :], gcol, float(CAP), None,
                                      ALU.is_ge))
                    C(v.scalar_tensor_tensor(tmpA[:, :], tmpA[:, :], OOBF,
                                             tmpB[:, :], ALU.mult, ALU.add))
                    C(v.tensor_copy(idx_f[:, k * NT:(k + 1) * NT], tmpA[:, :]))
                C(v.tensor_copy(idx_i[:, :], idx_f[:, :]))
                C(v.tensor_scalar_add(gid8[:, 0::2], gidt[:, :], float(c8 * T)))
                C(v.tensor_scalar_add(gid8[:, 1::2], gidt[:, :], float(c8 * T)))

            @blk.gpsimd
            def _(gg, c8=c8):
                s_v.wait(gg)
                for k in range(2):
                    for i in range(NT):
                        s_tb.incd(gg.indirect_dma_start(
                            out=tbl_d[0:1, :],
                            out_offset=IndirectOffsetOnAxis(
                                idx_i[:, k * NT + i:k * NT + i + 1], 0),
                            in_=gid8[:, 2 * i:2 * i + 2],
                            in_offset=None,
                            bounds_check=2 * CAP - 1, oob_is_err=False))


def emit_p4_sharded(nc, L):
    T, D, FF, CAP, NCH, NFS = (L["T"], L["D"], L["FF"], L["CAP"], L["NCH"],
                               L["NFS"])
    NCORE = 8
    sb3, ps3 = L["sb3"], L["ps3"]
    s_mw, s_tb, s_q2, s_v, s_s, s_pe = (L["s_mw"], L["s_tb"], L["s_q2"],
                                        L["s_v"], L["s_s"], L["s_pe"])
    s_eg, s_cc2, s_accg = L["s_eg"], L["s_cc2"], L["s_accg"]
    s_p3, s_p3s, s_p3v = L["s_p3"], L["s_p3s"], L["s_p3v"]
    tbl_d, xn_all_d, eo_in_d, eo_all_d = (L["tbl_d"], L["xn_all_d"],
                                          L["eo_in_d"], L["eo_all_d"])
    wg_in, wu_in, wd_in = L["wg_in"], L["wu_in"], L["wd_in"]
    wgs, wus, wds = L["wgs"], L["wus"], L["wds"]
    ident_sb = L["ident_sb"]
    NBLK = CAP // 512
    assert CAP % 512 == 0 or CAP % 128 == 0

    tcol_f = sb3("tcol_f", [128, 4], F32)
    tcol_i = sb3("tcol_i", [128, 4], I32)
    xe_tok = sb3("xe_tok", [128, 4 * D], BF16)
    xeT = [sb3(f"xeT{c}", [128, 512], BF16) for c in range(NCH)]
    ps_tre = [ps3(f"ps_tre{j}", [128, 128], BF16) for j in range(2)]

    nsub = CAP // 128
    for le in range(2):
        with nc.Block(f"P4s_{le}") as blk:
            @blk.gpsimd
            def _(gg, le=le):
                s_p3.wait(gg)
                s_p3s.wait(gg)
                s_p3v.wait(gg)
                if le == 0:
                    s_tb.wait(gg)
                s_mw.incd(gg.dma_start(
                    out=wgs[:, :].rearrange("p (c f) -> p c f", f=FF),
                    in_=wg_in[le, :, :].rearrange("(c p) f -> p c f", p=128)))
                s_mw.incd(gg.dma_start(
                    out=wus[:, :].rearrange("p (c f) -> p c f", f=FF),
                    in_=wu_in[le, :, :].rearrange("(c p) f -> p c f", p=128)))
                s_mw.incd(gg.dma_start(
                    out=wds[:, :].rearrange("p (c d) -> p c d", d=D),
                    in_=wd_in[le, :, :].rearrange("(c p) d -> p c d", p=128)))

            for blk_i in range(CAP // 512):
                @blk.sync
                def _(sy, le=le, blk_i=blk_i):
                    if not (le == 0 and blk_i == 0):
                        s_pe.wait(sy)
                    s_v.wait(sy)
                    base = le * CAP + blk_i * 512
                    for u in range(4):
                        s_q2.incd(sy.dma_start(
                            out=tcol_f[:, u:u + 1],
                            in_=tbl_d[base + u * 128:base + (u + 1) * 128, 0:1]))

                @blk.vector
                def _(v, le=le, blk_i=blk_i):
                    C = L["Chain"](s_v, v)
                    s_q2.wait(v)
                    C(v.tensor_copy(tcol_i[:, :], tcol_f[:, :]))
                    C(v.memset(xe_tok[:, :], 0.0))

                @blk.gpsimd
                def _(gg, le=le, blk_i=blk_i):
                    s_v.wait(gg)
                    if le == 0 and blk_i == 0:
                        s_ccx = L["s_ccx"]
                        s_ccx.wait(gg)
                    for u in range(4):
                        s_eg.incd(gg.indirect_dma_start(
                            out=xe_tok[:, u * D:(u + 1) * D],
                            out_offset=None,
                            in_=xn_all_d[0:1, :],
                            in_offset=IndirectOffsetOnAxis(
                                tcol_i[:, u:u + 1], 0),
                            bounds_check=NCORE * T - 1, oob_is_err=False))

                ev_base = s_s.reserve(4 * NCH)

                @blk.tensor
                def _(t, le=le, blk_i=blk_i, ev_base=ev_base):
                    s_eg.wait(t)
                    for u in range(4):
                        for c in range(NCH):
                            n = u * NCH + c
                            if n >= 2:
                                s_s.wait_at(t, ev_base + n - 1)
                            s_pe.inc(t.transpose(
                                ps_tre[n % 2][:, :],
                                xe_tok[:, u * D + c * 128:u * D + (c + 1) * 128],
                                ident_sb[:, :]))

                @blk.scalar
                def _(s, le=le, blk_i=blk_i, ev_base=ev_base,
                      pe_base=s_pe.n - 4 * NCH):
                    for u in range(4):
                        for c in range(NCH):
                            n = u * NCH + c
                            s_pe.wait_at(s, pe_base + n + 1)
                            s_s.inc_at(s.copy(
                                xeT[c][:, u * 128:(u + 1) * 128],
                                ps_tre[n % 2][:, :]))

                _emit_mlp(nc, blk, L, wgs, wus, wds, xeT, None, eo_in_d,
                          (le * CAP + blk_i * 512) // 128, 1,
                          first_waits=[(s_mw, None), (s_s, None),
                                       (s_p3, None), (s_p3s, None),
                                       (s_p3v, None)],
                          plain_store=True)

    with nc.Block("CC2") as blk:
        @blk.gpsimd
        def _(gg):
            s_accg.wait(gg)
            s_cc2.incd(gg.collective_compute(
                "AllGather", ALU.bypass, replica_groups=[list(range(NCORE))],
                ins=[eo_in_d.ap().bitcast(F32).opt()],
                outs=[eo_all_d.ap().bitcast(F32).opt()]))


def emit_p5(nc, L):
    T, D, NT = L["T"], L["D"], L["NT"]
    acc_d, out = L["acc_d"], L["out"]
    s_acc, s_st, s_accg = L["s_acc"], L["s_st"], L["s_accg"]
    moe = L["moe"]
    ob = [L["sb"]("ob0", [128, D], F32), L["sb"]("ob1", [128, D], F32)]

    if moe != "sharded":
        with nc.Block("P5") as blk:
            @blk.sync
            def _(sy):
                s_acc.wait(sy)
                s_accg.wait(sy)
                for i in range(NT):
                    jj = i % 2
                    s_st.incd(sy.dma_start(out=ob[jj][:, :],
                                           in_=acc_d[i * 128:(i + 1) * 128, :]))
                    s_st.wait_at(sy, s_st.n)
                    s_st.incd(sy.dma_start(out=out[i * 128:(i + 1) * 128, :],
                                           in_=ob[jj][:, :]))
                s_st.wait(sy)
        return

    s_cc2, s_p5g, s_v = L["s_cc2"], L["s_p5g"], L["s_v"]
    retidx, gates = L["retidx"], L["gates"]
    eo_all_d = L["eo_all_d"]
    CAP = L["CAP"]
    NCORE = 8
    er = [L["sb"](f"er{j}", [128, D], BF16) for j in range(2)]
    ALUx = ALU

    with nc.Block("P5") as blk:
        @blk.vector
        def _(v):
            C = L["Chain"](s_v, v)
            C(v.memset(er[0][:, :], 0.0))
            C(v.memset(er[1][:, :], 0.0))

        @blk.sync
        def _(sy):
            s_acc.wait(sy)
            s_accg.wait(sy)
            for i in range(NT):
                jj = i % 2
                if i >= 2:
                    s_v.wait_at(sy, s_v.n + (i - 1) * 4)
                s_st.incd(sy.dma_start(out=ob[jj][:, :],
                                       in_=acc_d[i * 128:(i + 1) * 128, :]))

        @blk.gpsimd
        def _(gg):
            s_cc2.wait(gg)
            s_v.wait(gg)
            vbase = s_v.n
            for i in range(NT):
                for k in range(2):
                    kk = k
                    if i >= 1:
                        s_v.wait_at(gg, vbase + (i - 1) * 4 + 2 + k + 1)
                    s_p5g.incd(gg.indirect_dma_start(
                        out=er[kk][:, :], out_offset=None,
                        in_=eo_all_d[0:1, :],
                        in_offset=IndirectOffsetOnAxis(
                            retidx[:, 2 * i + k:2 * i + k + 1], 0),
                        bounds_check=NCORE * 2 * CAP - 1, oob_is_err=False))

        @blk.vector
        def _(v, vbase=s_v.n):
            C = L["Chain"](s_v, v)
            for i in range(NT):
                jj = i % 2
                s_st.wait_at(v, 16 * (i + 1))
                s_p5g.wait_at(v, 32 * (i + 1))
                C(v.scalar_tensor_tensor(ob[jj][:, :], er[0][:, :],
                                         gates[:, 2 * i:2 * i + 1],
                                         ob[jj][:, :], ALUx.mult, ALUx.add))
                C(v.scalar_tensor_tensor(ob[jj][:, :], er[1][:, :],
                                         gates[:, 2 * i + 1:2 * i + 2],
                                         ob[jj][:, :], ALUx.mult, ALUx.add))
                C(v.tensor_copy(ob[jj][:, :], ob[jj][:, :])) if False else None
                s_v.inc(v.tensor_copy(ob[jj][0:1, 0:1], ob[jj][0:1, 0:1]))
                s_v.inc(v.tensor_copy(ob[jj][0:1, 0:1], ob[jj][0:1, 0:1]))

        @blk.sync
        def _(sy, vb2=None):
            for i in range(NT):
                jj = i % 2
                s_v.wait_at(sy, s_v.n - (NT - 1 - i) * 4)
                s_st.incd(sy.dma_start(out=out[i * 128:(i + 1) * 128, :],
                                       in_=ob[jj][:, :]))
            s_st.wait(sy)


_CFG = dict(T=4096, D=1024, H=16, DH=64, FF=1408, E=16, CAP=5120,
            moe="dense", stop_after="P5", race=False)


def kernel(**inputs):
    from concourse.bass_utils import run_bass_kernel_spmd
    cfg = dict(_CFG)
    shards = host_prep(inputs, cfg)
    nc, ctx, L = build(cfg)
    res = run_bass_kernel_spmd(nc, shards, core_ids=list(range(NCORE)))
    return np.stack([r["out"] for r in res.results]).astype(np.float32)


# version 03
# speedup vs baseline: 1.3359x; 1.0118x over previous
import numpy as np
import ml_dtypes
import concourse.bass as bass
import concourse.mybir as mybir
from contextlib import ExitStack


F32 = mybir.dt.float32
BF16 = mybir.dt.bfloat16
I32 = mybir.dt.int32
AF = mybir.ActivationFunctionType
ALU = mybir.AluOpType
AX = mybir.AxisListType

NCORE = 8
BIG = 65536.0
OOBF = 1.0e9


class S:
    def __init__(self, ctx, nc, name):
        self.h = ctx.enter_context(nc.semaphore(name))
        self.n = 0

    def inc(self, inst, amt=1):
        inst.then_inc(self.h, amt)
        self.n += amt
        return inst

    def incd(self, inst):
        return self.inc(inst, 16)

    def reserve(self, amt):
        base = self.n
        self.n += amt
        return base

    def inc_at(self, inst, amt=1):
        inst.then_inc(self.h, amt)
        return inst

    def wait(self, eng):
        if self.n > 0:
            eng.wait_ge(self.h, self.n)

    def wait_at(self, eng, val):
        if val > 0:
            eng.wait_ge(self.h, val)




class Chain:

    def __init__(self, sem, eng):
        self.sem, self.eng = sem, eng

    def __call__(self, inst):
        self.sem.inc(inst)
        self.eng.wait_ge(self.sem.h, self.sem.n)
        return inst


def default_cfg():
    return dict(T=4096, D=1024, H=16, DH=64, FF=1408, E=16, CAP=5120,
                moe="dense", stop_after="P5")


def host_prep(inputs, cfg):
    T, D, H, DH, FF, E = cfg["T"], cfg["D"], cfg["H"], cfg["DH"], cfg["FF"], cfg["E"]
    x = np.ascontiguousarray(inputs["L0_attn_x"])
    NP = H // 2

    def pack_qkv(M):
        Mt = np.transpose(M, (2, 1, 0))
        return np.ascontiguousarray(
            np.concatenate([Mt[0::2], Mt[1::2]], axis=2))

    Qp = pack_qkv(inputs["L0_attn_Q"])
    Kp = pack_qkv(inputs["L0_attn_K"])
    Vp = pack_qkv(inputs["L0_attn_V"])
    Wt = np.transpose(inputs["L0_attn_W"], (2, 1, 0))
    Wp = np.ascontiguousarray(np.concatenate([Wt[0::2], Wt[1::2]], axis=1))

    shards = []
    NT = T // 128
    for c in range(NCORE):
        xc = np.ascontiguousarray(x[c])
        xq = np.ascontiguousarray(xc.reshape(NT, 128, D).transpose(0, 2, 1))
        sh = {
            "x_tok": xc, "xq": xq,
            "Qp": Qp, "Kp": Kp, "Vp": Vp, "Wp": Wp,
            "rw": np.ascontiguousarray(inputs["L0_ffn_router_w"]),
            "rb": np.ascontiguousarray(inputs["L0_ffn_router_b"].reshape(1, E)),
            "swg": np.ascontiguousarray(inputs["L0_ffn_sw_gate"]),
            "swu": np.ascontiguousarray(inputs["L0_ffn_sw_up"]),
            "swd": np.ascontiguousarray(inputs["L0_ffn_sw_down"]),
            "coremask": np.ascontiguousarray(
                (np.arange(NCORE) < c).astype(np.float32).reshape(NCORE, 1)),
            "gidbase": np.full((1, 1), float(c * T), np.float32),
        }
        if cfg["moe"] == "dense":
            sh["wg"] = np.ascontiguousarray(inputs["L0_ffn_w_gate"])
            sh["wu"] = np.ascontiguousarray(inputs["L0_ffn_w_up"])
            sh["wd"] = np.ascontiguousarray(inputs["L0_ffn_w_down"])
        else:
            sh["wg"] = np.ascontiguousarray(inputs["L0_ffn_w_gate"][2 * c:2 * c + 2])
            sh["wu"] = np.ascontiguousarray(inputs["L0_ffn_w_up"][2 * c:2 * c + 2])
            sh["wd"] = np.ascontiguousarray(inputs["L0_ffn_w_down"][2 * c:2 * c + 2])
            sh["myexp"] = np.array([[2 * c, 2 * c + 1]], np.float32)
        shards.append(sh)
    return shards


def build(cfg):
    T, D, H, DH = cfg["T"], cfg["D"], cfg["H"], cfg["DH"]
    FF, E, CAP = cfg["FF"], cfg["E"], cfg["CAP"]
    moe = cfg["moe"]
    stop_after = cfg.get("stop_after", "P5")
    NT, NB, NCH, NFS, NP = T // 128, T // 512, D // 128, FF // 128, H // 2
    QSCALE = 1.0 / D
    ZSCALE = float(np.sqrt(D / DH) / H)
    VSCALE = float(np.sqrt(DH / D))

    nc = bass.Bass("TRN2", target_bir_lowering=False,
                   detect_race_conditions=cfg.get("race", True))

    x_tok = nc.dram_tensor("x_tok", [T, D], F32, kind="ExternalInput").ap()
    xq_in = nc.dram_tensor("xq", [NT, D, 128], F32, kind="ExternalInput").ap()
    Qp_in = nc.dram_tensor("Qp", [NP, D, 2 * DH], F32, kind="ExternalInput").ap()
    Kp_in = nc.dram_tensor("Kp", [NP, D, 2 * DH], F32, kind="ExternalInput").ap()
    Vp_in = nc.dram_tensor("Vp", [NP, D, 2 * DH], F32, kind="ExternalInput").ap()
    Wp_in = nc.dram_tensor("Wp", [NP, 2 * DH, D], F32, kind="ExternalInput").ap()
    rw_in = nc.dram_tensor("rw", [D, E], F32, kind="ExternalInput").ap()
    rb_in = nc.dram_tensor("rb", [1, E], F32, kind="ExternalInput").ap()
    swg_in = nc.dram_tensor("swg", [D, FF], F32, kind="ExternalInput").ap()
    swu_in = nc.dram_tensor("swu", [D, FF], F32, kind="ExternalInput").ap()
    swd_in = nc.dram_tensor("swd", [FF, D], F32, kind="ExternalInput").ap()
    nexp_w = E if moe == "dense" else 2
    wg_in = nc.dram_tensor("wg", [nexp_w, D, FF], F32, kind="ExternalInput").ap()
    wu_in = nc.dram_tensor("wu", [nexp_w, D, FF], F32, kind="ExternalInput").ap()
    wd_in = nc.dram_tensor("wd", [nexp_w, FF, D], F32, kind="ExternalInput").ap()
    cmask_in = nc.dram_tensor("coremask", [NCORE, 1], F32, kind="ExternalInput").ap()
    if moe == "sharded":
        myexp_in = nc.dram_tensor("myexp", [1, 2], F32, kind="ExternalInput").ap()
    gidb_in = nc.dram_tensor("gidbase", [1, 1], F32, kind="ExternalInput").ap()

    out = nc.dram_tensor("out", [T, D], F32, kind="ExternalOutput").ap()

    acc_d = nc.dram_tensor("acc_d", [T, D], F32, kind="Internal")
    qT_d = nc.dram_tensor("qT_d", [NP, 128, T], BF16, kind="Internal")
    v_d = nc.dram_tensor("v_d", [NP, 128, (T // 128) * 130], BF16, kind="Internal")
    xnT_d = nc.dram_tensor("xnT_d", [NCH, 128, T], BF16, kind="Internal")
    cnt_in_d = nc.dram_tensor("cnt_in_d", [1, E], F32, kind="Internal")
    cnt_out_d = nc.dram_tensor("cnt_out_d", [NCORE, E], F32, kind="Internal")
    if moe == "sharded":
        xn_in_d = nc.dram_tensor("xn_in_d", [T, D], BF16, kind="Internal")
        xn_all_d = nc.dram_tensor("xn_all_d", [NCORE * T, D], BF16, kind="Internal")
        rt_in_d = nc.dram_tensor("rt_in_d", [128, NT * 4], F32, kind="Internal")
        rt_all_d = nc.dram_tensor("rt_all_d", [NCORE * 128, NT * 4], F32,
                                  kind="Internal")
        tbl_d = nc.dram_tensor("tbl_d", [2 * CAP, 2], F32, kind="Internal")
        eo_in_d = nc.dram_tensor("eo_in_d", [2 * CAP, D], BF16, kind="Internal")
        eo_all_d = nc.dram_tensor("eo_all_d", [NCORE * 2 * CAP, D], BF16,
                                  kind="Internal")

    ident_c = nc.inline_tensor(np.eye(128, dtype=ml_dtypes.bfloat16), "ident128")
    u128_c = nc.inline_tensor(
        np.triu(np.ones((128, 128), np.float32), 1).astype(ml_dtypes.bfloat16),
        "u128")
    ones128b_c = nc.inline_tensor(np.ones((128, 1), ml_dtypes.bfloat16), "ones128b")
    onescol_c = nc.inline_tensor(np.ones((128, 1), np.float32), "onescol")
    onesrow_c = nc.inline_tensor(np.ones((1, 128), np.float32), "onesrow")
    iota16_c = nc.inline_tensor(
        np.tile(np.arange(E, dtype=np.float32), (128, 1)), "iota16")
    iotaB_c = nc.inline_tensor(
        np.tile(np.arange(E, dtype=np.float32) + BIG, (128, 1)), "iotaB")
    masks_np = np.stack([
        (np.arange(128)[:, None] + 128 * d <= np.arange(512)[None, :])
        .astype(ml_dtypes.bfloat16) for d in range(4)])
    mask_c = nc.inline_tensor(
        np.ascontiguousarray(masks_np.transpose(1, 0, 2).reshape(128, 4 * 512)),
        "cmask")

    ctx = ExitStack()
    ctx2 = ExitStack()
    ctx1 = ExitStack()

    def mk_sb(stack):
        def f(name, shape, dt):
            return stack.enter_context(nc.sbuf_tensor(name, shape, dt))
        return f

    def mk_ps(stack):
        def f(name, shape, dt=F32):
            return stack.enter_context(nc.psum_tensor(name, shape, dt))
        return f

    sb, psum = mk_sb(ctx), mk_ps(ctx)
    sb2x, ps2x = mk_sb(ctx2), mk_ps(ctx2)
    sb1, ps1 = mk_sb(ctx1), mk_ps(ctx1)

    route = sb("route", [128, NT * 6], F32)
    gates = sb("gates", [128, NT * 2], F32)
    G_sb = sb("G", [128, NT * E], F32) if moe == "dense" else None
    retidx = sb("retidx", [128, NT * 2], I32) if moe == "sharded" else None
    rt_sb = sb("rt_sb", [128, NT * 4], F32) if moe == "sharded" else None
    myexp_sb = sb("myexp_sb", [1, 2], F32) if moe == "sharded" else None
    ident_sb = sb("ident_sb", [128, 128], BF16)
    u128_sb = sb("u128_sb", [128, 128], BF16)
    ones128b_sb = sb("ones128b_sb", [128, 1], BF16)
    onescol_sb = sb("onescol_sb", [128, 1], F32)
    onesrow_sb = sb("onesrow_sb", [1, 128], F32)
    iota16_sb = sb("iota16_sb", [128, E], F32)
    iotaB_sb = sb("iotaB_sb", [128, E], F32)
    mask_sb = sb("mask_sb", [128, 4 * 512], BF16)
    rw_sb = sb("rw_sb", [128, NCH * E], F32)
    rb_bc = sb("rb_bc", [128, E], F32)
    cs_bc = sb("cs_bc", [128, E], F32)
    cpre_bc = sb("cpre_bc", [128, E], F32)
    prefix_bc = sb("prefix_bc", [128, E], F32)
    cnt_sb = sb("cnt_sb", [NCORE, E], F32)
    cmask_sb = sb("cmask_sb", [NCORE, 1], F32)
    gidb_sb = sb("gidb_sb", [1, 1], F32)
    negM = sb("negM", [128, 1], F32)
    stat = sb("stat", [128, 16], F32)
    ct_row = sb("ct_row", [1, E], F32)
    l16 = sb("l16", [128, E], F32)
    oh1 = sb("oh1", [128, E], F32)
    oh2 = sb("oh2", [128, E], F32)
    t16 = sb("t16", [128, E], F32)
    a16 = sb("a16", [128, E], F32)
    xnT = [sb2x(f"xnT{c}", [128, T], BF16) for c in range(NCH)]

    xa = [sb1(f"xa{j}", [128, D], F32) for j in range(2)]
    xqb = [sb1(f"xqb{j}", [128, D], F32) for j in range(2)]
    xnb = [sb1(f"xnb{j}", [128, D], BF16) for j in range(2)]
    xnf = sb1("xnf", [128, D], F32)
    sqscr = sb1("sqscr", [128, D], F32)
    acc0 = [sb1(f"acc0{j}", [128, D], F32) for j in range(2)]
    e16 = sb1("e16", [128, E], F32)
    m8 = sb1("m8", [128, 8], F32)
    u16 = sb1("u16", [128, E], F32)
    ohc = sb1("ohc", [128, E], BF16)
    posf = sb1("posf", [128, E], F32)
    ps_lg = ps1("ps_lg", [128, E])
    ps_cs = ps1("ps_cs", [128, E])
    ps_ct = ps1("ps_ct", [1, E])
    ps_bc = ps1("ps_bc", [128, E])
    ps_tr = [ps1(f"ps_tr{j}", [128, 128], BF16) for j in range(2)]

    s_cst = S(ctx, nc, "s_cst")
    s_x = S(ctx, nc, "s_x")
    s_sq = S(ctx, nc, "s_sq")
    s_v = S(ctx, nc, "s_v")
    s_s = S(ctx, nc, "s_s")
    s_pe = S(ctx, nc, "s_pe")
    s_g = S(ctx, nc, "s_g")
    s_acc = S(ctx, nc, "s_acc")
    s_accg = S(ctx, nc, "s_accg")
    s_cc = S(ctx, nc, "s_cc")
    s_wl = S(ctx, nc, "s_wl")
    s_q2 = S(ctx, nc, "s_q2")
    s_st = S(ctx, nc, "s_st")
    s_kv = S(ctx, nc, "s_kv")
    s_p2a = S(ctx, nc, "s_p2a")
    s_p2s = S(ctx, nc, "s_p2s")
    s_p2v = S(ctx, nc, "s_p2v")
    s_p2y = S(ctx, nc, "s_p2y")
    s_mw = S(ctx, nc, "s_mw")
    s_p3 = S(ctx, nc, "s_p3")
    s_p3s = S(ctx, nc, "s_p3s")
    s_p3v = S(ctx, nc, "s_p3v")
    s_p4 = S(ctx, nc, "s_p4")
    s_vq = S(ctx, nc, "s_vq")
    s_tb = S(ctx, nc, "s_tb")
    s_eg = S(ctx, nc, "s_eg")
    s_cc2 = S(ctx, nc, "s_cc2")
    s_p5g = S(ctx, nc, "s_p5g")
    s_v5 = S(ctx, nc, "s_v5")
    s_ld5 = S(ctx, nc, "s_ld5")
    s_p5 = S(ctx, nc, "s_p5")

    with nc.Block("P0") as blk:
        @blk.sync
        def _(sy):
            for dst, src in [(ident_sb, ident_c), (u128_sb, u128_c),
                             (ones128b_sb, ones128b_c), (onescol_sb, onescol_c),
                             (onesrow_sb, onesrow_c), (iota16_sb, iota16_c),
                             (iotaB_sb, iotaB_c), (mask_sb, mask_c)]:
                s_cst.incd(sy.dma_start(out=dst[:, :], in_=src.ap()))
            s_cst.incd(sy.dma_start(
                out=rw_sb[:, :].rearrange("p (c e) -> p c e", e=E),
                in_=rw_in.rearrange("(c p) e -> p c e", p=128)))
            s_cst.incd(sy.dma_start(out=rb_bc[0:1, :], in_=rb_in[:, :]))
            s_cst.incd(sy.dma_start(out=cmask_sb[:, :], in_=cmask_in[:, :]))
            s_cst.incd(sy.dma_start(out=gidb_sb[:, :], in_=gidb_in[:, :]))
            if moe == "sharded":
                s_cst.incd(sy.dma_start(out=myexp_sb[:, :], in_=myexp_in[:, :]))

        @blk.tensor
        def _(t):
            s_cst.wait(t)
            for c in range(NCH):
                mm = t.matmul(ps_ct[:, :], onescol_sb[:, :],
                              rw_sb[:, c * E:(c + 1) * E],
                              start=(c == 0), stop=(c == NCH - 1))
            s_pe.inc(mm)
            s_pe.inc(t.matmul(ps_bc[:, :], onesrow_sb[:, :], rb_bc[0:1, :],
                              start=True, stop=True))

        @blk.vector
        def _(v):
            s_pe.wait_at(v, 1)
            s_v.inc(v.tensor_copy(ct_row[:, :], ps_ct[:, :]))
            s_pe.wait_at(v, 2)
            s_v.inc(v.tensor_copy(rb_bc[:, :], ps_bc[:, :]))

        @blk.tensor
        def _(t):
            s_v.wait(t)
            s_pe.inc(t.matmul(ps_bc[:, :], onesrow_sb[:, :], ct_row[:, :],
                              start=True, stop=True))

        @blk.vector
        def _(v):
            s_pe.wait(v)
            s_v.inc(v.tensor_copy(cs_bc[:, :], ps_bc[:, :]))
            v.memset(prefix_bc[:, :], 0.0)
            s_v.inc(v.memset(stat[:, :], 0.0))

    for i in range(NT):
        j = i % 2
        with nc.Block(f"P1_{i}") as blk:
            vp, sp, pp = s_v.n, s_s.n, s_pe.n

            @blk.sync
            def _(sy, i=i, j=j, vp=vp, sp=sp, pp=pp):
                if i >= 2:
                    s_v.wait_at(sy, vp)
                    s_s.wait_at(sy, sp)
                    s_pe.wait_at(sy, pp)
                    s_acc.wait_at(sy, 16 * (i - 1))
                s_x.incd(sy.dma_start(out=xa[j][:, :],
                                      in_=x_tok[i * 128:(i + 1) * 128, :]))
                s_x.incd(sy.dma_start(
                    out=xqb[j][:, :].rearrange("p (c n) -> p c n", n=128),
                    in_=xq_in[i, :, :].rearrange("(c p) n -> p c n", p=128)))

            @blk.scalar
            def _(s, i=i, j=j, vp=vp):
                s_x.wait_at(s, (2 * i + 2) * 16)
                s_v.wait_at(s, vp)
                s_sq.inc(s.activation(sqscr[:, :], xa[j][:, :], AF.Square))

            @blk.vector
            def _(v, i=i, j=j):
                C = Chain(s_v, v)
                s_x.wait_at(v, (2 * i + 2) * 16)
                C(v.reduce_sum(stat[:, 0:1], xa[j][:, :], axis=AX.X))
                C(v.tensor_scalar_mul(stat[:, 2:3], stat[:, 0:1], 1.0 / D))
                s_sq.wait(v)
                C(v.reduce_sum(stat[:, 1:2], sqscr[:, :], axis=AX.X))
                C(v.tensor_mul(stat[:, 3:4], stat[:, 2:3], stat[:, 2:3]))
                C(v.scalar_tensor_tensor(stat[:, 3:4], stat[:, 1:2], 1.0 / D,
                                         stat[:, 3:4], ALU.mult, ALU.subtract))
                C(v.tensor_scalar_add(stat[:, 3:4], stat[:, 3:4], 1e-5))

            @blk.scalar
            def _(s, i=i, j=j):
                Cs = Chain(s_s, s)
                s_v.wait(s)
                Cs(s.activation(stat[:, 4:5], stat[:, 3:4], AF.Ln))
                Cs(s.activation(stat[:, 4:5], stat[:, 4:5], AF.Exp, scale=-0.5))

            @blk.vector
            def _(v, i=i, j=j):
                C = Chain(s_v, v)
                s_s.wait(v)
                C(v.tensor_scalar(xnf[:, :], xa[j][:, :], stat[:, 2:3],
                                  stat[:, 4:5], ALU.subtract, ALU.mult))
                C(v.tensor_copy(xnb[j][:, :], xnf[:, :]))
                C(v.tensor_add(acc0[j][:, :], xa[j][:, :], xnf[:, :]))

            @blk.sync
            def _(sy, i=i, j=j):
                s_v.wait(sy)
                s_acc.incd(sy.dma_start(out=acc_d[i * 128:(i + 1) * 128, :],
                                        in_=acc0[j][:, :]))

            if moe == "sharded":
                @blk.gpsimd
                def _(gg, i=i, j=j):
                    s_v.wait(gg)
                    s_g.incd(gg.dma_start(out=xn_in_d[i * 128:(i + 1) * 128, :],
                                          in_=xnb[j][:, :]))

            ev_base = s_s.reserve(NCH)
            pe_tr_base = s_pe.n

            @blk.tensor
            def _(t, i=i, j=j, ev_base=ev_base):
                s_v.wait(t)
                for c in range(NCH):
                    if c >= 2:
                        s_s.wait_at(t, ev_base + c - 1)
                    s_pe.inc(t.transpose(ps_tr[c % 2][:, :],
                                         xnb[j][:, c * 128:(c + 1) * 128],
                                         ident_sb[:, :]))

            @blk.scalar
            def _(s, i=i, j=j, ev_base=ev_base, pe_tr_base=pe_tr_base):
                for c in range(NCH):
                    s_pe.wait_at(s, pe_tr_base + c + 1)
                    s_s.inc_at(s.copy(xnT[c][:, i * 128:(i + 1) * 128],
                                      ps_tr[c % 2][:, :]))

            @blk.gpsimd
            def _(gg, i=i, j=j, ev_base=ev_base):
                s_s.wait_at(gg, ev_base + NCH)
                for c in range(NCH):
                    s_g.incd(gg.dma_start(
                        out=xnT_d[c, :, i * 128:(i + 1) * 128],
                        in_=xnT[c][:, i * 128:(i + 1) * 128]))

            @blk.tensor
            def _(t, i=i, j=j):
                s_x.wait_at(t, (2 * i + 2) * 16)
                s_v.wait_at(t, s_v.n)
                for c in range(NCH):
                    mm = t.matmul(ps_lg[:, :], xqb[j][:, c * 128:(c + 1) * 128],
                                  rw_sb[:, c * E:(c + 1) * E],
                                  start=(c == 0), stop=(c == NCH - 1))
                s_pe.inc(mm)

            @blk.vector
            def _(v, i=i, j=j):
                C = Chain(s_v, v)
                s_pe.wait(v)
                C(v.tensor_scalar(t16[:, :], cs_bc[:, :], stat[:, 2:3], None,
                                  ALU.mult))
                C(v.tensor_sub(l16[:, :], ps_lg[:, :], t16[:, :]))
                C(v.tensor_scalar(l16[:, :], l16[:, :], stat[:, 4:5], None,
                                  ALU.mult))
                C(v.tensor_add(l16[:, :], l16[:, :], rb_bc[:, :]))
                C(v.max(m8[:, :], l16[:, :]))
                C(v.tensor_scalar_mul(stat[:, 6:7], m8[:, 0:1], -1.0))
                C(v.tensor_sub(stat[:, 7:8], m8[:, 1:2], m8[:, 0:1]))

            @blk.scalar
            def _(s, i=i, j=j):
                Cs = Chain(s_s, s)
                s_v.wait(s)
                Cs(s.activation(e16[:, :], l16[:, :], AF.Exp, bias=stat[:, 6:7]))
                Cs(s.activation(stat[:, 8:9], stat[:, 7:8], AF.Exp))

            @blk.vector
            def _(v, i=i, j=j):
                C = Chain(s_v, v)
                s_s.wait(v)
                C(v.reduce_sum(stat[:, 9:10], e16[:, :], axis=AX.X))
                C(v.scalar_tensor_tensor(stat[:, 10:11], stat[:, 9:10], 1e-9,
                                         stat[:, 8:9], ALU.mult, ALU.add))
                C(v.tensor_scalar_add(stat[:, 10:11], stat[:, 10:11], 1.0))
                C(v.reciprocal(stat[:, 10:11], stat[:, 10:11]))
                C(v.tensor_mul(stat[:, 11:12], stat[:, 8:9], stat[:, 10:11]))
                C(v.tensor_scalar(u16[:, :], l16[:, :], m8[:, 0:1], None,
                                  ALU.is_ge))
                C(v.scalar_tensor_tensor(a16[:, :], u16[:, :], -BIG,
                                         iotaB_sb[:, :], ALU.mult, ALU.add))
                C(v.tensor_reduce(stat[:, 12:13], a16[:, :], axis=AX.X,
                                  op=ALU.min))
                C(v.tensor_scalar(oh1[:, :], iota16_sb[:, :], stat[:, 12:13],
                                  None, ALU.is_equal))
                C(v.tensor_scalar(u16[:, :], l16[:, :], m8[:, 1:2], None,
                                  ALU.is_ge))
                C(v.scalar_tensor_tensor(a16[:, :], u16[:, :], -BIG,
                                         iotaB_sb[:, :], ALU.mult, ALU.add))
                C(v.scalar_tensor_tensor(a16[:, :], oh1[:, :], 2 * BIG,
                                         a16[:, :], ALU.mult, ALU.add))
                C(v.tensor_reduce(stat[:, 13:14], a16[:, :], axis=AX.X,
                                  op=ALU.min))
                C(v.tensor_scalar(oh2[:, :], iota16_sb[:, :], stat[:, 13:14],
                                  None, ALU.is_equal))
                C(v.tensor_add(ohc[:, :], oh1[:, :], oh2[:, :]))

            @blk.tensor
            def _(t, i=i, j=j):
                s_v.wait(t)
                t.matmul(ps_cs[:, :], u128_sb[:, :], ohc[:, :],
                         start=True, stop=True)
                s_pe.inc(t.matmul(ps_ct[:, :], ones128b_sb[:, :], ohc[:, :],
                                  start=True, stop=True))

            @blk.vector
            def _(v, i=i, j=j):
                C = Chain(s_v, v)
                s_pe.wait(v)
                C(v.tensor_add(posf[:, :], ps_cs[:, :], prefix_bc[:, :]))
                C(v.tensor_mul(t16[:, :], posf[:, :], oh1[:, :]))
                C(v.reduce_sum(route[:, 6 * i + 1:6 * i + 2], t16[:, :],
                               axis=AX.X))
                C(v.tensor_mul(t16[:, :], posf[:, :], oh2[:, :]))
                C(v.reduce_sum(route[:, 6 * i + 4:6 * i + 5], t16[:, :],
                               axis=AX.X))
                v.tensor_copy(route[:, 6 * i + 0:6 * i + 1], stat[:, 12:13])
                v.tensor_copy(route[:, 6 * i + 2:6 * i + 3], stat[:, 10:11])
                v.tensor_copy(route[:, 6 * i + 3:6 * i + 4], stat[:, 13:14])
                v.tensor_copy(route[:, 6 * i + 5:6 * i + 6], stat[:, 11:12])
                C(v.tensor_copy(ct_row[:, :], ps_ct[:, :]))

            @blk.tensor
            def _(t, i=i, j=j):
                s_v.wait(t)
                s_pe.inc(t.matmul(ps_bc[:, :], onesrow_sb[:, :], ct_row[:, :],
                                  start=True, stop=True))

            @blk.vector
            def _(v, i=i, j=j):
                s_pe.wait(v)
                s_v.inc(v.tensor_add(prefix_bc[:, :], prefix_bc[:, :],
                                     ps_bc[:, :]))

    L = dict(locals())
    L["p1_acc_end"] = s_acc.n

    with nc.Block("C0") as blk:
        @blk.vector
        def _(v):
            s_v.inc(v.tensor_copy(ct_row[:, :], prefix_bc[0:1, :]))

        @blk.gpsimd
        def _(gg):
            s_v.wait(gg)
            s_g.incd(gg.dma_start(out=cnt_in_d.ap(), in_=ct_row[:, :]))
            s_g.wait(gg)
            s_cc.inc(gg.collective_compute(
                "AllGather", ALU.bypass, replica_groups=[list(range(NCORE))],
                ins=[cnt_in_d.ap().opt()], outs=[cnt_out_d.ap().opt()]))

        @blk.sync
        def _(sy):
            s_cc.wait(sy)
            s_cst.incd(sy.dma_start(out=cnt_sb[:, :], in_=cnt_out_d.ap()))

        @blk.tensor
        def _(t):
            s_cst.wait(t)
            s_v.wait(t)
            s_pe.inc(t.matmul(ps_ct[:, :], cmask_sb[:, :], cnt_sb[:, :],
                              start=True, stop=True))

        @blk.vector
        def _(v):
            s_pe.wait(v)
            s_v.inc(v.tensor_copy(ct_row[:, :], ps_ct[:, :]))

        @blk.tensor
        def _(t):
            s_v.wait(t)
            s_pe.inc(t.matmul(ps_bc[:, :], onesrow_sb[:, :], ct_row[:, :],
                              start=True, stop=True))

        @blk.vector
        def _(v):
            s_pe.wait(v)
            s_v.inc(v.tensor_copy(cpre_bc[:, :], ps_bc[:, :]))

    with nc.Block("P19") as blk:
        @blk.vector
        def _(v):
            C = Chain(s_v, v)
            for i in range(NT):
                for k in range(2):
                    cidx = route[:, 6 * i + 3 * k:6 * i + 3 * k + 1]
                    cpos = route[:, 6 * i + 3 * k + 1:6 * i + 3 * k + 2]
                    cval = route[:, 6 * i + 3 * k + 2:6 * i + 3 * k + 3]
                    ohk = oh1 if k == 0 else oh2
                    C(v.tensor_scalar(ohk[:, :], iota16_sb[:, :], cidx, None,
                                      ALU.is_equal))
                    C(v.tensor_mul(t16[:, :], ohk[:, :], cpre_bc[:, :]))
                    C(v.reduce_sum(stat[:, 0:1], t16[:, :], axis=AX.X))
                    C(v.tensor_add(stat[:, 1:2], cpos, stat[:, 0:1]))
                    C(v.tensor_scalar(stat[:, 2:3], stat[:, 1:2], float(CAP),
                                      None, ALU.is_lt))
                    C(v.tensor_mul(gates[:, 2 * i + k:2 * i + k + 1], cval,
                                   stat[:, 2:3]))
                    if moe == "sharded":
                        C(v.tensor_scalar(stat[:, 3:4], cidx, float(CAP), None,
                                          ALU.mult))
                        C(v.tensor_add(stat[:, 3:4], stat[:, 3:4], stat[:, 1:2]))
                        C(v.tensor_scalar(stat[:, 4:5], stat[:, 2:3], -OOBF,
                                          None, ALU.mult))
                        C(v.tensor_scalar_add(stat[:, 4:5], stat[:, 4:5], OOBF))
                        C(v.tensor_add(stat[:, 3:4], stat[:, 3:4], stat[:, 4:5]))
                        C(v.tensor_copy(retidx[:, 2 * i + k:2 * i + k + 1],
                                        stat[:, 3:4]))
                        v.tensor_copy(rt_sb[:, 4 * i + 2 * k:4 * i + 2 * k + 1],
                                      cidx)
                        C(v.tensor_copy(
                            rt_sb[:, 4 * i + 2 * k + 1:4 * i + 2 * k + 2],
                            stat[:, 1:2]))
                if moe == "dense":
                    C(v.tensor_scalar(t16[:, :], oh1[:, :],
                                      gates[:, 2 * i:2 * i + 1], None, ALU.mult))
                    C(v.tensor_scalar(a16[:, :], oh2[:, :],
                                      gates[:, 2 * i + 1:2 * i + 2], None,
                                      ALU.mult))
                    C(v.tensor_add(G_sb[:, i * E:(i + 1) * E], t16[:, :],
                                   a16[:, :]))
            s_v.inc(v.tensor_copy(stat[:, 15:16], stat[:, 0:1]))

    s_ccx = S(ctx, nc, "s_ccx")
    if moe == "sharded":
        with nc.Block("CC1") as blk:
            @blk.gpsimd
            def _(gg):
                s_v.wait(gg)
                s_g.incd(gg.dma_start(out=rt_in_d.ap(), in_=rt_sb[:, :]))
                s_g.wait(gg)
                s_ccx.incd(gg.collective_compute(
                    "AllGather", ALU.bypass,
                    replica_groups=[list(range(NCORE))],
                    ins=[xn_in_d.ap().bitcast(F32).opt()],
                    outs=[xn_all_d.ap().bitcast(F32).opt()]))
                s_ccx.incd(gg.collective_compute(
                    "AllGather", ALU.bypass,
                    replica_groups=[list(range(NCORE))],
                    ins=[rt_in_d.ap().opt()], outs=[rt_all_d.ap().opt()]))

    L = dict(locals())
    L["p1_acc_end"] = NT * 16
    L["dbg_p2"] = cfg.get("dbg_p2", False)
    if stop_after == "P1":
        emit_p5(nc, L)
        return nc, ctx, L
    ctx1.close()

    L["sb2"], L["ps2"] = sb2x, ps2x
    ctx2t = ExitStack()
    L["sb2t"], L["ps2t"] = mk_sb(ctx2t), mk_ps(ctx2t)
    emit_p15(nc, L)
    ctx2t.close()
    if stop_after == "P15":
        emit_p5(nc, L)
        return nc, ctx, L
    emit_p2(nc, L)
    ctx2.close()
    if stop_after == "P2":
        emit_p5(nc, L)
        return nc, ctx, L

    ctx3 = ExitStack()
    L["sb3"], L["ps3"] = mk_sb(ctx3), mk_ps(ctx3)
    L["Chain"] = Chain
    if moe == "sharded":
        L["s_ccx"] = s_ccx
        emit_p4s_prep(nc, L)
    emit_p3(nc, L)
    if stop_after == "P3":
        emit_p5(nc, L)
        return nc, ctx, L
    if moe == "dense":
        emit_p4_dense(nc, L)
    else:
        emit_p4_sharded(nc, L)
    ctx3.close()
    emit_p5(nc, L)
    return nc, ctx, L





def emit_p15(nc, L):
    g = lambda k: L[k]
    T, D, DH, NP, NCH, NT, NB = (L["T"], L["D"], L["DH"], L["NP"], L["NCH"],
                                 L["NT"], L["NB"])
    QSCALE, VSCALE = L["QSCALE"], L["VSCALE"]
    xnT, ident_sb = L["xnT"], L["ident_sb"]
    ones128b_sb, onesrow_sb = L["ones128b_sb"], L["onesrow_sb"]
    negM, stat = L["negM"], L["stat"]
    qT_d = L["qT_d"]
    Qp_in, Kp_in, Vp_in, Wp_in = L["Qp_in"], L["Kp_in"], L["Vp_in"], L["Wp_in"]
    s_v, s_s, s_pe, s_wl, s_q2, s_kv, s_g = (L["s_v"], L["s_s"], L["s_pe"],
                                        L["s_wl"], L["s_q2"], L["s_kv"], L["s_g"])
    sb2, ps2 = L["sb2"], L["ps2"]
    sb2t, ps2t = L["sb2t"], L["ps2t"]

    kT_sb = [sb2(f"kT{p}", [128, T], BF16) for p in range(NP)]
    ps_q = ps2t("ps_q", [128, 512])
    ps_k = ps2t("ps_k", [128, 512])
    qw_sb = sb2t("qw_sb", [128, NCH * 128], BF16)
    kw_sb = sb2t("kw_sb", [128, NCH * 128], BF16)
    vw_sb = sb2t("vw_sb", [128, NCH * 128], BF16)
    qevac = sb2t("qevac", [128, 512], BF16)
    sqs = sb2t("sqs", [128, 512], BF16)
    v_stage = sb2t("v_stage", [128, 520], BF16)
    mrow = sb2t("mrow", [1, 4], F32)
    L["kT_sb"] = kT_sb
    v_d = L["v_d"]

    ps_v = ps2t("ps_v", [128, 128])
    ps_m = ps2t("ps_m", [1, 512])
    ps_nm = ps2t("ps_nm", [128, 1])

    with nc.Block("P15") as blk:
        @blk.vector
        def _(v):
            s_v.wait(v)
            s_s.wait(v)
            v.memset(v_stage[:, :], 1.0)
            s_v.inc(v.memset(mrow[:, :], 0.0))

        for p in range(NP):
            @blk.gpsimd
            def _(gg, p=p):
                if p >= 1:
                    s_pe.wait(gg)
                s_wl.incd(gg.dma_start(
                    out=qw_sb[:, :].rearrange("p (c n) -> p c n", n=2 * DH),
                    in_=Qp_in[p, :, :].rearrange("(c p) n -> p c n", p=128)))
                s_wl.incd(gg.dma_start(
                    out=kw_sb[:, :].rearrange("p (c n) -> p c n", n=2 * DH),
                    in_=Kp_in[p, :, :].rearrange("(c p) n -> p c n", p=128)))
                s_wl.incd(gg.dma_start(
                    out=vw_sb[:, :].rearrange("p (c n) -> p c n", n=2 * DH),
                    in_=Vp_in[p, :, :].rearrange("(c p) n -> p c n", p=128)))

            for b in range(NB):
                @blk.tensor
                def _(t, p=p, b=b):
                    s_wl.wait(t)
                    s_s.wait(t)
                    s_v.wait(t)
                    for c in range(NCH):
                        t.matmul(ps_q[:, :], qw_sb[:, c * 128:(c + 1) * 128],
                                 xnT[c][:, b * 512:(b + 1) * 512],
                                 start=(c == 0), stop=(c == NCH - 1))
                    for c in range(NCH):
                        mm = t.matmul(ps_k[:, :], kw_sb[:, c * 128:(c + 1) * 128],
                                      xnT[c][:, b * 512:(b + 1) * 512],
                                      start=(c == 0), stop=(c == NCH - 1))
                    s_pe.inc(mm)

                @blk.scalar
                def _(s, p=p, b=b):
                    s_pe.wait(s)
                    s.activation(qevac[:, :], ps_q[:, :], AF.Copy, scale=QSCALE)
                    s.copy(kT_sb[p][:, b * 512:(b + 1) * 512], ps_k[:, :])
                    s_s.inc(s.activation(sqs[:, :], qevac[:, :], AF.Square))

                @blk.sync
                def _(sy, p=p, b=b):
                    s_s.wait(sy)
                    s_q2.incd(sy.dma_start(
                        out=qT_d[p, :, b * 512:(b + 1) * 512], in_=qevac[:, :]))

                @blk.tensor
                def _(t, p=p, b=b):
                    s_s.wait(t)
                    s_pe.inc(t.matmul(ps_m[:, :], ones128b_sb[:, :], sqs[:, :],
                                      start=True, stop=True))

                @blk.vector
                def _(v, p=p, b=b):
                    s_pe.wait(v)
                    s_v.inc(v.reduce_max(stat[0:1, 0:1], ps_m[:, :], axis=AX.X))
                    s_v.wait(v)
                    s_v.inc(v.tensor_max(mrow[:, 0:1], mrow[:, 0:1],
                                         stat[0:1, 0:1]))

                @blk.scalar
                def _(s, p=p, b=b):
                    s_v.wait(s)
                    s_s.inc(s.activation(sqs[:, :],
                                         kT_sb[p][:, b * 512:(b + 1) * 512],
                                         AF.Square))

                @blk.tensor
                def _(t, p=p, b=b):
                    s_s.wait(t)
                    s_pe.inc(t.matmul(ps_m[:, :], ones128b_sb[:, :], sqs[:, :],
                                      start=True, stop=True))

                @blk.vector
                def _(v, p=p, b=b):
                    s_pe.wait(v)
                    s_v.inc(v.reduce_max(stat[0:1, 0:1], ps_m[:, :], axis=AX.X))
                    s_v.wait(v)
                    s_v.inc(v.tensor_max(mrow[:, 1:2], mrow[:, 1:2],
                                         stat[0:1, 0:1]))

                sv_base = s_s.reserve(4)
                pe_v_base = s_pe.n

                @blk.tensor
                def _(t, p=p, b=b, sv_base=sv_base):
                    s_v.wait(t)
                    for tt in range(4):
                        i = b * 4 + tt
                        if tt >= 1:
                            s_s.wait_at(t, sv_base + tt)
                        for c in range(NCH):
                            mm = t.matmul(ps_v[:, :],
                                          xnT[c][:, i * 128:(i + 1) * 128],
                                          vw_sb[:, c * 128:(c + 1) * 128],
                                          start=(c == 0), stop=(c == NCH - 1))
                        s_pe.inc(mm)

                @blk.scalar
                def _(s, p=p, b=b, sv_base=sv_base, pe_v_base=pe_v_base):
                    s_g.wait_at(s, s_g.n)
                    for tt in range(4):
                        s_pe.wait_at(s, pe_v_base + tt + 1)
                        s.activation(v_stage[:, tt * 130:tt * 130 + 64],
                                     ps_v[:, 0:64], AF.Copy, scale=VSCALE)
                        s_s.inc_at(s.activation(
                            v_stage[:, tt * 130 + 65:tt * 130 + 129],
                            ps_v[:, 64:128], AF.Copy, scale=VSCALE))

                @blk.gpsimd
                def _(gg, p=p, b=b, sv_base=sv_base):
                    s_s.wait_at(gg, sv_base + 4)
                    s_g.incd(gg.dma_start(
                        out=v_d[p, :, b * 520:(b + 1) * 520],
                        in_=v_stage[:, :]))

        @blk.vector
        def _(v):
            s_v.wait(v)
            s_v.inc(v.tensor_mul(mrow[:, 2:3], mrow[:, 0:1], mrow[:, 1:2]))

        @blk.scalar
        def _(s):
            s_v.wait(s)
            s_s.inc(s.activation(mrow[:, 3:4], mrow[:, 2:3], AF.Sqrt))

        @blk.vector
        def _(v):
            s_s.wait(v)
            s_v.inc(v.tensor_scalar_mul(mrow[:, 3:4], mrow[:, 3:4], -1.02))

        @blk.tensor
        def _(t):
            s_v.wait(t)
            s_pe.inc(t.matmul(ps_nm[:, :], onesrow_sb[:, :], mrow[:, 3:4],
                              start=True, stop=True))

        @blk.vector
        def _(v):
            s_pe.wait(v)
            s_kv.inc(v.tensor_copy(negM[:, :], ps_nm[:, :]))


def emit_p2(nc, L):
    g = lambda k: L[k]
    T, D, NP, NB, NT = L["T"], L["D"], L["NP"], L["NB"], L["NT"]
    ZSCALE = L["ZSCALE"]
    kT_sb = L["kT_sb"]
    v_d, Wp_in = L["v_d"], L["Wp_in"]
    s_wl, s_g = L["s_wl"], L["s_g"]
    negM, mask_sb, onesrow_sb = L["negM"], L["mask_sb"], L["onesrow_sb"]
    qT_d, acc_d = L["qT_d"], L["acc_d"]
    s_q2, s_kv, s_acc, s_accg = L["s_q2"], L["s_kv"], L["s_acc"], L["s_accg"]
    s_p2a, s_p2s, s_p2v, s_p2y = L["s_p2a"], L["s_p2s"], L["s_p2v"], L["s_p2y"]
    sb2, ps2 = L["sb2"], L["ps2"]
    ps_att = [ps2("ps_attA", [128, 1024]), ps2("ps_attB", [128, 1024])]
    ps_y = [ps2("ps_y0", [65, 512]), ps2("ps_y1", [65, 512])]
    ps_z = ps2("ps_z", [128, D])

    Wp_sb = sb2("Wp_sb", [128, NP * D], BF16)
    qsb = sb2("qsb", [128, NP * 512], BF16)
    vq = [sb2("vq0", [128, NT * 130], BF16), sb2("vq1", [128, NT * 130], BF16)]
    s_vq = L["s_vq"]
    pT = [sb2("pT0", [128, 1024], BF16), sb2("pT1", [128, 1024], BF16)]
    ysb = [sb2("ysb0", [64, 512], BF16), sb2("ysb1", [64, 512], BF16)]
    yTn = sb2("yTn", [128, NP * 512], BF16)
    rs_row = sb2("rs_row", [1, 512], F32)
    zsb = [sb2("zsb0", [128, D], F32), sb2("zsb1", [128, D], F32)]

    p2_acc_base = s_accg.n
    with nc.Block("P2W") as blk:
        @blk.gpsimd
        def _(gg):
            s_wl.incd(gg.dma_start(
                out=Wp_sb[:, :].rearrange("p (np d) -> p np d", np=NP),
                in_=Wp_in.rearrange("np p d -> p np d")))

    for qb in range(NB):
        nk = 4 * qb + 4
        with nc.Block(f"P2_{qb}") as blk:
            a0, sB, v0, y0 = s_p2a.n, s_p2s.n, s_p2v.n, s_p2y.n
            s_p2a.reserve(NP * (nk + 2) + 4)
            s_p2s.reserve(NP * (nk + 2) + 4)
            s_p2v.reserve(NP * 8)
            s_p2y.reserve(NP * nk)
            A0 = lambda p, a0=a0, nk=nk: a0 + p * (nk + 2)
            S0 = lambda p, sB=sB, nk=nk: sB + p * (nk + 2)
            V0 = lambda p, v0=v0: v0 + p * 8
            Y0 = lambda p, y0=y0, nk=nk: y0 + p * nk
            az = a0 + NP * (nk + 2)
            sz = sB + NP * (nk + 2)

            vqb = s_vq.n

            @blk.sync
            def _(sy, qb=qb, a0=a0, nk=nk, Y0=Y0, y0=y0):
                s_p2a.wait_at(sy, a0)
                s_q2.wait(sy)
                for p in range(NP):
                    s_q2.incd(sy.dma_start(
                        out=qsb[:, p * 512:(p + 1) * 512],
                        in_=qT_d[p, :, qb * 512:(qb + 1) * 512]))
                for p in range(NP):
                    if p >= 2:
                        s_p2y.wait_at(sy, Y0(p - 1))
                    s_vq.incd(sy.dma_start(
                        out=vq[p % 2][:, 0:nk * 130],
                        in_=v_d[p, :, 0:nk * 130]))

            @blk.tensor
            def _(t, qb=qb, nk=nk, A0=A0, S0=S0, V0=V0, Y0=Y0, sB=sB, az=az,
                  sz=sz, v0=v0, vqb=vqb):
                s_q2.wait(t)
                s_kv.wait(t)
                L["s_s"].wait_at(t, L["s_s"].n)
                L["s_v"].wait_at(t, L["s_v"].n)
                s_p2s.wait_at(t, sB)
                for p in range(NP):

                    def pv(kt, p=p):
                        s_p2s.wait_at(t, S0(p) + kt + 1)
                        if kt >= nk - 4:
                            s_p2v.wait_at(t, V0(p) + (kt - (nk - 4)) + 1)
                        if kt == 0:
                            s_vq.wait_at(t, vqb + 16 * (p + 1))
                        t.matmul(ps_y[0][:, :],
                                 vq[p % 2][:, kt * 130:kt * 130 + 65],
                                 pT[kt % 2][:, 0:512], start=(kt == 0),
                                 stop=(kt == nk - 1))
                        mm = t.matmul(ps_y[1][:, :],
                                      vq[p % 2][:, kt * 130 + 65:kt * 130 + 130],
                                      pT[kt % 2][:, 512:1024], start=(kt == 0),
                                      stop=(kt == nk - 1))
                        s_p2y.inc_at(mm)

                    for kt in range(nk):
                        if kt >= 2:
                            s_p2s.wait_at(t, S0(p) + kt - 1)
                        elif p > 0:
                            s_p2s.wait_at(t, S0(p))
                            s_p2v.wait_at(t, V0(p))
                        t.matmul(ps_att[kt % 2][:, 0:512],
                                 kT_sb[p][0:64, kt * 128:(kt + 1) * 128],
                                 qsb[0:64, p * 512:(p + 1) * 512],
                                 start=True, stop=True, tile_position=(0, 0))
                        mm = t.matmul(ps_att[kt % 2][:, 512:1024],
                                      kT_sb[p][64:128, kt * 128:(kt + 1) * 128],
                                      qsb[64:128, p * 512:(p + 1) * 512],
                                      start=True, stop=True,
                                      tile_position=(64, 0))
                        s_p2a.inc_at(mm)
                        if kt >= 1:
                            pv(kt - 1)
                    pv(nk - 1)
                    for h in range(2):
                        s_p2v.wait_at(t, V0(p) + 5 + 2 * h)
                        s_p2s.wait_at(t, S0(p) + nk + 1 + h)
                        mm = t.matmul(ps_y[h][0:64, :], onesrow_sb[:, 0:64],
                                      rs_row[:, :], start=True, stop=True)
                        s_p2a.inc_at(mm)
                s_p2v.wait_at(t, v0 + NP * 8)
                dcols = [(a, min(a + 512, D)) for a in range(0, D, 512)]
                for tt in range(4):
                    if tt >= 1:
                        s_p2s.wait_at(t, sz + tt)
                    for p in range(NP):
                        for (ca, cb) in dcols:
                            mm = t.matmul(
                                ps_z[:, ca:cb],
                                yTn[:, p * 512 + tt * 128:p * 512 + (tt + 1) * 128],
                                Wp_sb[:, p * D + ca:p * D + cb],
                                start=(p == 0), stop=(p == NP - 1))
                    s_p2a.inc_at(mm)

            @blk.scalar
            def _(s, qb=qb, nk=nk, A0=A0, S0=S0, V0=V0, Y0=Y0, az=az, sz=sz,
                  p2_acc_base=p2_acc_base):
                if qb == 0:
                    s_kv.wait(s)
                for p in range(NP):
                    for kt in range(nk):
                        s_p2a.wait_at(s, A0(p) + kt + 1)
                        if kt >= 2:
                            s_p2y.wait_at(s, Y0(p) + kt - 1)
                        elif p > 0:
                            s_p2y.wait_at(s, Y0(p))
                        mm = s.activation(pT[kt % 2][:, :], ps_att[kt % 2][:, :],
                                          AF.Exp, bias=negM[:, :])
                        s_p2s.inc_at(mm)
                    s_p2y.wait_at(s, Y0(p) + nk)
                    if p > 0:
                        s_p2v.wait_at(s, V0(p))
                    s_p2s.inc_at(s.copy(ysb[0][:, :], ps_y[0][0:64, :]))
                    s_p2s.inc_at(s.copy(ysb[1][:, :], ps_y[1][0:64, :]))
                for tt in range(4):
                    m = qb * 4 + tt
                    s_p2a.wait_at(s, az + tt + 1)
                    if m >= 2:
                        s_accg.wait_at(s, p2_acc_base + 16 * (m - 1))
                    mm = s.activation(zsb[tt % 2][:, :], ps_z[:, :], AF.Copy,
                                      scale=ZSCALE)
                    s_p2s.inc_at(mm)

            @blk.vector
            def _(v, qb=qb, nk=nk, A0=A0, S0=S0, V0=V0, Y0=Y0):
                for p in range(NP):
                    for d in range(4):
                        kt = nk - 4 + d
                        s_p2s.wait_at(v, S0(p) + kt + 1)
                        v.tensor_mul(pT[kt % 2][:, 0:512], pT[kt % 2][:, 0:512],
                                     mask_sb[:, d * 512:(d + 1) * 512])
                        mm = v.tensor_mul(pT[kt % 2][:, 512:1024],
                                          pT[kt % 2][:, 512:1024],
                                          mask_sb[:, d * 512:(d + 1) * 512])
                        s_p2v.inc_at(mm)
                    for h in range(2):
                        s_p2y.wait_at(v, Y0(p) + nk)
                        mm = v.reciprocal(rs_row[:, :], ps_y[h][64:65, :])
                        s_p2v.inc_at(mm)
                        s_p2v.wait_at(v, V0(p) + 5 + 2 * h)
                        s_p2a.wait_at(v, A0(p) + nk + 1 + h)
                        s_p2s.wait_at(v, S0(p) + nk + 1 + h)
                        mm = v.tensor_mul(
                            yTn[64 * h:64 * h + 64, p * 512:(p + 1) * 512],
                            ysb[h][:, :], ps_y[h][0:64, :])
                        s_p2v.inc_at(mm)

            @blk.gpsimd
            def _(gg, qb=qb, sz=sz, p2_acc_base=p2_acc_base):
                for tt in range(4):
                    s_p2s.wait_at(gg, sz + tt + 1)
                    if qb == 0 and tt == 0:
                        s_acc.wait_at(gg, L["p1_acc_end"])
                    i = qb * 4 + tt
                    s_accg.incd(gg.dma_start(
                        out=acc_d[i * 128:(i + 1) * 128, :],
                        in_=zsb[tt % 2][:, :], accum_op=ALU.add))


def _emit_mlp(nc, blk, L, wgs, wus, wds, token_src, gate_col_fn, accdst,
              rows_base, n_blocks, first_waits, bf16_out=None,
              plain_store=False, down_waits=()):
    D, FF, NCH, NFS = L["D"], L["FF"], L["NCH"], L["NFS"]
    s_p3, s_p3s, s_p3v = L["s_p3"], L["s_p3s"], L["s_p3v"]
    s_acc, s_accg = L["s_acc"], L["s_accg"]
    ps_g, ps_u, ps_z3 = L["ps_g3"], L["ps_u3"], L["ps_z3"]
    prod, gsb, zsb3 = L["prod"], L["gsb"], L["zsb3"]

    per_t, per_s, per_v = NFS + 4, NFS + 4, NFS
    tb = s_p3.reserve(n_blocks * per_t)
    sbb = s_p3s.reserve(n_blocks * per_s)
    vb = s_p3v.reserve(n_blocks * per_v)
    acc_base = s_accg.n
    n_tiles_total = n_blocks * 4

    def TB(b, k=0):
        return tb + b * per_t + k

    def SB(b, k=0):
        return sbb + b * per_s + k

    def VB(b, k=0):
        return vb + b * per_v + k

    @blk.tensor
    def _(t):
        for sem, val in first_waits:
            sem.wait_at(t, sem.n if val is None else val)
        for b in range(n_blocks):
            for fs in range(NFS):
                gidx = b * NFS + fs
                if gidx >= 2:
                    pg = gidx - 2
                    pb, pf = pg // NFS, pg % NFS
                    s_p3s.wait_at(t, SB(pb, pf + 1))
                    s_p3v.wait_at(t, VB(pb, pf + 1))
                jj = gidx % 2
                for c in range(NCH):
                    t.matmul(ps_g[jj][:, :],
                             wgs[:, c * FF + fs * 128:c * FF + (fs + 1) * 128],
                             token_src[c][:, b * 512:(b + 1) * 512],
                             start=(c == 0), stop=(c == NCH - 1))
                for c in range(NCH):
                    mm = t.matmul(ps_u[jj][:, :],
                                  wus[:, c * FF + fs * 128:c * FF + (fs + 1) * 128],
                                  token_src[c][:, b * 512:(b + 1) * 512],
                                  start=(c == 0), stop=(c == NCH - 1))
                s_p3.inc_at(mm)
            for tt in range(4):
                s_p3v.wait_at(t, VB(b, NFS))
                if b == 0 and tt == 0:
                    for sem, val in down_waits:
                        sem.wait_at(t, sem.n if val is None else val)
                else:
                    pb, pt = (b, tt - 1) if tt > 0 else (b - 1, 3)
                    s_p3s.wait_at(t, SB(pb, NFS + pt + 1))
                dcols = [(a, min(a + 512, D)) for a in range(0, D, 512)]
                for fs in range(NFS):
                    for (ca, cb) in dcols:
                        mm = t.matmul(
                            ps_z3[:, ca:cb],
                            prod[:, fs * 512 + tt * 128:fs * 512 + (tt + 1) * 128],
                            wds[:, fs * D + ca:fs * D + cb],
                            start=(fs == 0), stop=(fs == NFS - 1))
                s_p3.inc_at(mm)

    @blk.scalar
    def _(s):
        for b in range(n_blocks):
            for fs in range(NFS):
                gidx = b * NFS + fs
                jj = gidx % 2
                s_p3.wait_at(s, TB(b, fs + 1))
                if gidx >= 2:
                    pg = gidx - 2
                    s_p3v.wait_at(s, VB(pg // NFS, pg % NFS + 1))
                s_p3s.inc_at(s.activation(gsb[jj][:, :], ps_g[jj][:, :], AF.Silu))
            for tt in range(4):
                i = b * 4 + tt
                s_p3.wait_at(s, TB(b, NFS + tt + 1))
                if i >= 2:
                    s_accg.wait_at(s, acc_base + 16 * (i - 1))
                gc = gate_col_fn(i) if gate_col_fn is not None else None
                if gc is not None:
                    mm = s.activation(zsb3[i % 2][:, :], ps_z3[:, :], AF.Copy,
                                      scale=gc)
                else:
                    mm = s.copy(zsb3[i % 2][:, :], ps_z3[:, :])
                s_p3s.inc_at(mm)

    @blk.vector
    def _(v):
        for b in range(n_blocks):
            for fs in range(NFS):
                gidx = b * NFS + fs
                jj = gidx % 2
                s_p3s.wait_at(v, SB(b, fs + 1))
                mm = v.tensor_mul(prod[:, fs * 512:(fs + 1) * 512], gsb[jj][:, :],
                                  ps_u[jj][:, :])
                s_p3v.inc_at(mm)

    @blk.gpsimd
    def _(gg):
        for b in range(n_blocks):
            for tt in range(4):
                i = b * 4 + tt
                s_p3s.wait_at(gg, SB(b, NFS + tt + 1))
                s_accg.incd(gg.dma_start(
                    out=accdst[(rows_base + i) * 128:(rows_base + i + 1) * 128, :],
                    in_=zsb3[i % 2][:, :],
                    accum_op=(ALU.bypass if plain_store else ALU.add)))

    return TB, SB, VB


def emit_p3(nc, L):
    NCH, NFS, FF, D, NB, T = L["NCH"], L["NFS"], L["FF"], L["D"], L["NB"], L["T"]
    sb3, ps3 = L["sb3"], L["ps3"]
    s_mw, s_q2 = L["s_mw"], L["s_q2"]
    swg_in, swu_in, swd_in = L["swg_in"], L["swu_in"], L["swd_in"]
    xnT_d = L["xnT_d"]

    xnT3 = [sb3(f"xnT3_{c}", [128, T], BF16) for c in range(NCH)]
    L["xnT3"] = xnT3
    wgs = [sb3("wgs0", [128, NCH * FF], BF16), sb3("wgs1", [128, NCH * FF], BF16)]
    wus = [sb3("wus0", [128, NCH * FF], BF16), sb3("wus1", [128, NCH * FF], BF16)]
    wds = sb3("wds", [128, NFS * D], BF16)
    gsb = [sb3("gsb0", [128, 512], BF16), sb3("gsb1", [128, 512], BF16)]
    prod = sb3("prod", [128, NFS * 512], BF16)
    zsb3 = [sb3("zsb3_0", [128, D], F32), sb3("zsb3_1", [128, D], F32)]
    ps_g = [ps3("ps_g3a", [128, 512]), ps3("ps_g3b", [128, 512])]
    ps_u = [ps3("ps_u3a", [128, 512]), ps3("ps_u3b", [128, 512])]
    ps_z3 = ps3("ps_z3", [128, D])
    L.update(wgs=wgs, wus=wus, wds=wds, gsb=gsb, prod=prod, zsb3=zsb3,
             ps_g3=ps_g, ps_u3=ps_u, ps_z3=ps_z3)

    with nc.Block("P3") as blk:
        @blk.sync
        def _(sy):
            L["s_p2a"].wait_at(sy, L["s_p2a"].n)
            L["s_p2s"].wait_at(sy, L["s_p2s"].n)
            L["s_p2v"].wait_at(sy, L["s_p2v"].n)
            for c in range(NCH):
                s_q2.incd(sy.dma_start(out=xnT3[c][:, :], in_=xnT_d[c, :, :]))

        @blk.gpsimd
        def _(gg):
            s_mw.incd(gg.dma_start(
                out=wgs[0][:, :].rearrange("p (c f) -> p c f", f=FF),
                in_=swg_in.rearrange("(c p) f -> p c f", p=128)))
            s_mw.incd(gg.dma_start(
                out=wus[0][:, :].rearrange("p (c f) -> p c f", f=FF),
                in_=swu_in.rearrange("(c p) f -> p c f", p=128)))
            s_mw.incd(gg.dma_start(
                out=wds[:, :].rearrange("p (c d) -> p c d", d=D),
                in_=swd_in.rearrange("(c p) d -> p c d", p=128)))

        tb_info = _emit_mlp(
            nc, blk, L, wgs[0], wus[0], wds, xnT3, None, L["acc_d"], 0, NB,
            first_waits=[(L["s_mw"], None), (L["s_q2"], None),
                         (L["s_p2a"], None), (L["s_p2s"], None),
                         (L["s_p2v"], None)])
        L["p3_last_t"] = L["s_p3"].n


def emit_p4_dense(nc, L):
    E, NB, NCH, NFS, FF, D = L["E"], L["NB"], L["NCH"], L["NFS"], L["FF"], L["D"]
    s_mw, s_p3 = L["s_mw"], L["s_p3"]
    wg_in, wu_in, wd_in = L["wg_in"], L["wu_in"], L["wd_in"]
    wgs, wus, wds = L["wgs"], L["wus"], L["wds"]
    G_sb = L["G_sb"]
    last_gu_marker = [None]

    for e in range(E):
        jj = (e + 1) % 2
        with nc.Block(f"P4d_{e}") as blk:
            @blk.gpsimd
            def _(gg, e=e, jj=jj):
                if e == 0:
                    pass
                else:
                    s_p3.wait_at(gg, last_gu_marker[0])
                s_mw.incd(gg.dma_start(
                    out=wgs[jj][:, :].rearrange("p (c f) -> p c f", f=FF),
                    in_=wg_in[e, :, :].rearrange("(c p) f -> p c f", p=128)))
                s_mw.incd(gg.dma_start(
                    out=wus[jj][:, :].rearrange("p (c f) -> p c f", f=FF),
                    in_=wu_in[e, :, :].rearrange("(c p) f -> p c f", p=128)))
                s_p3.wait(gg)
                s_mw.incd(gg.dma_start(
                    out=wds[:, :].rearrange("p (c d) -> p c d", d=D),
                    in_=wd_in[e, :, :].rearrange("(c p) d -> p c d", p=128)))

            TB, SB, VB = _emit_mlp(
                nc, blk, L, wgs[jj], wus[jj], wds, L["xnT3"],
                (lambda i, e=e: G_sb[:, i * E + e:i * E + e + 1]),
                L["acc_d"], 0, NB,
                first_waits=[(s_mw, s_mw.n - 16),
                             (L["s_p3s"], L["s_p3s"].n),
                             (L["s_p3v"], L["s_p3v"].n)],
                down_waits=[(s_mw, None)])
            last_gu_marker[0] = TB(NB - 1, NFS)


def emit_p4s_prep(nc, L):
    NT, T, CAP, NCH = L["NT"], L["T"], L["CAP"], L["NCH"]
    NCORE = 8
    sb3, ps3 = L["sb3"], L["ps3"]
    s_ccx, s_q2, s_v, s_pe = L["s_ccx"], L["s_q2"], L["s_v"], L["s_pe"]
    s_tb = L["s_tb"]
    rt_all_d, tbl_d = L["rt_all_d"], L["tbl_d"]
    myexp_sb, onesrow_sb = L["myexp_sb"], L["onesrow_sb"]
    OOBF = 1.0e9

    rt_t = sb3("rt_t", [128, NT * 4], F32)
    idx_f = sb3("idx_f", [128, NT * 2], F32)
    idx_i = sb3("idx_i", [128, NT * 2], I32)
    gid8 = sb3("gid8", [128, 2 * NT], F32)
    my0c = sb3("my0c", [128, 1], F32)
    my1c = sb3("my1c", [128, 1], F32)
    tmpA = sb3("tmpA", [128, NT], F32)
    tmpB = sb3("tmpB", [128, NT], F32)
    ini_t = sb3("ini_t", [128, 2 * (2 * CAP // 128)], F32)
    gidt = sb3("gidt", [128, NT], F32)
    ps_my = ps3("ps_my", [128, 2])

    with nc.Block("P4prep") as blk:
        @blk.tensor
        def _(t):
            s_pe.inc(t.matmul(ps_my[:, :], onesrow_sb[:, :], myexp_sb[:, :],
                              start=True, stop=True))

        @blk.vector
        def _(v):
            C = L["Chain"](s_v, v)
            s_pe.wait(v)
            C(v.tensor_copy(my0c[:, :], ps_my[:, 0:1]))
            C(v.tensor_copy(my1c[:, :], ps_my[:, 1:2]))
            C(v.memset(ini_t[:, :], OOBF))
            s_v.inc(v.memset(tmpA[:, :], 0.0))

        @blk.gpsimd
        def _(gg):
            s_v.wait(gg)
            gg.iota(gidt[:, :], pattern=[[128, NT]], base=0,
                    channel_multiplier=1,
                    allow_small_or_imprecise_dtypes=True).then_inc(s_tb.h, 1)
            s_tb.n += 1
            s_tb.incd(gg.dma_start(
                out=tbl_d.ap().rearrange("(p n) two -> p n two", p=128),
                in_=ini_t[:, :].rearrange("p (n two) -> p n two", two=2)))

        for c8 in range(NCORE):
            @blk.sync
            def _(sy, c8=c8):
                if c8 == 0:
                    s_ccx.wait(sy)
                else:
                    s_v.wait(sy)
                s_q2.incd(sy.dma_start(
                    out=rt_t[:, :], in_=rt_all_d[c8 * 128:(c8 + 1) * 128, :]))

            @blk.vector
            def _(v, c8=c8):
                C = L["Chain"](s_v, v)
                s_q2.wait(v)
                s_tb.wait(v)
                for k in range(2):
                    ecol = rt_t[:, 2 * k::4]
                    gcol = rt_t[:, 2 * k + 1::4]
                    C(v.tensor_scalar(tmpA[:, :], ecol, my0c[:, 0:1], None,
                                      ALU.is_equal))
                    C(v.tensor_scalar(tmpB[:, :], ecol, my1c[:, 0:1], None,
                                      ALU.is_equal))
                    C(v.scalar_tensor_tensor(tmpB[:, :], tmpB[:, :], float(CAP),
                                             gcol, ALU.mult, ALU.add))
                    C(v.tensor_add(tmpA[:, :], tmpA[:, :],
                                   rt_t[:, 2 * k::4])) if False else None
                    C(v.tensor_scalar(idx_f[:, k * NT:(k + 1) * NT], ecol,
                                      my0c[:, 0:1], None, ALU.is_equal))
                    C(v.tensor_scalar(tmpA[:, :], ecol, my1c[:, 0:1], None,
                                      ALU.is_equal))
                    C(v.tensor_add(tmpA[:, :], tmpA[:, :],
                                   idx_f[:, k * NT:(k + 1) * NT]))
                    C(v.tensor_scalar(tmpA[:, :], tmpA[:, :], -OOBF, None,
                                      ALU.mult))
                    C(v.tensor_scalar_add(tmpA[:, :], tmpA[:, :], OOBF))
                    C(v.tensor_add(tmpB[:, :], tmpB[:, :], tmpA[:, :]))
                    C(v.tensor_scalar(tmpA[:, :], gcol, float(CAP), None,
                                      ALU.is_ge))
                    C(v.scalar_tensor_tensor(tmpA[:, :], tmpA[:, :], OOBF,
                                             tmpB[:, :], ALU.mult, ALU.add))
                    C(v.tensor_copy(idx_f[:, k * NT:(k + 1) * NT], tmpA[:, :]))
                C(v.tensor_copy(idx_i[:, :], idx_f[:, :]))
                C(v.tensor_scalar_add(gid8[:, 0::2], gidt[:, :], float(c8 * T)))
                C(v.tensor_scalar_add(gid8[:, 1::2], gidt[:, :], float(c8 * T)))

            @blk.gpsimd
            def _(gg, c8=c8):
                s_v.wait(gg)
                for k in range(2):
                    for i in range(NT):
                        s_tb.incd(gg.indirect_dma_start(
                            out=tbl_d[0:1, :],
                            out_offset=IndirectOffsetOnAxis(
                                idx_i[:, k * NT + i:k * NT + i + 1], 0),
                            in_=gid8[:, 2 * i:2 * i + 2],
                            in_offset=None,
                            bounds_check=2 * CAP - 1, oob_is_err=False))


def emit_p4_sharded(nc, L):
    T, D, FF, CAP, NCH, NFS = (L["T"], L["D"], L["FF"], L["CAP"], L["NCH"],
                               L["NFS"])
    NCORE = 8
    sb3, ps3 = L["sb3"], L["ps3"]
    s_mw, s_tb, s_q2, s_v, s_s, s_pe = (L["s_mw"], L["s_tb"], L["s_q2"],
                                        L["s_v"], L["s_s"], L["s_pe"])
    s_eg, s_cc2, s_accg = L["s_eg"], L["s_cc2"], L["s_accg"]
    s_p3, s_p3s, s_p3v = L["s_p3"], L["s_p3s"], L["s_p3v"]
    tbl_d, xn_all_d, eo_in_d, eo_all_d = (L["tbl_d"], L["xn_all_d"],
                                          L["eo_in_d"], L["eo_all_d"])
    wg_in, wu_in, wd_in = L["wg_in"], L["wu_in"], L["wd_in"]
    wgs, wus, wds = L["wgs"], L["wus"], L["wds"]
    ident_sb = L["ident_sb"]
    NBLK = CAP // 512
    assert CAP % 512 == 0 or CAP % 128 == 0

    tcol_f = sb3("tcol_f", [128, 4], F32)
    tcol_i = sb3("tcol_i", [128, 4], I32)
    xe_tok = sb3("xe_tok", [128, 4 * D], BF16)
    xeT = [sb3(f"xeT{c}", [128, 512], BF16) for c in range(NCH)]
    ps_tre = [ps3(f"ps_tre{j}", [128, 128], BF16) for j in range(2)]

    nsub = CAP // 128
    for le in range(2):
        with nc.Block(f"P4s_{le}") as blk:
            @blk.gpsimd
            def _(gg, le=le):
                s_p3.wait(gg)
                s_p3s.wait(gg)
                s_p3v.wait(gg)
                if le == 0:
                    s_tb.wait(gg)
                s_mw.incd(gg.dma_start(
                    out=wgs[:, :].rearrange("p (c f) -> p c f", f=FF),
                    in_=wg_in[le, :, :].rearrange("(c p) f -> p c f", p=128)))
                s_mw.incd(gg.dma_start(
                    out=wus[:, :].rearrange("p (c f) -> p c f", f=FF),
                    in_=wu_in[le, :, :].rearrange("(c p) f -> p c f", p=128)))
                s_mw.incd(gg.dma_start(
                    out=wds[:, :].rearrange("p (c d) -> p c d", d=D),
                    in_=wd_in[le, :, :].rearrange("(c p) d -> p c d", p=128)))

            for blk_i in range(CAP // 512):
                @blk.sync
                def _(sy, le=le, blk_i=blk_i):
                    if not (le == 0 and blk_i == 0):
                        s_pe.wait(sy)
                    s_v.wait(sy)
                    base = le * CAP + blk_i * 512
                    for u in range(4):
                        s_q2.incd(sy.dma_start(
                            out=tcol_f[:, u:u + 1],
                            in_=tbl_d[base + u * 128:base + (u + 1) * 128, 0:1]))

                @blk.vector
                def _(v, le=le, blk_i=blk_i):
                    C = L["Chain"](s_v, v)
                    s_q2.wait(v)
                    C(v.tensor_copy(tcol_i[:, :], tcol_f[:, :]))
                    C(v.memset(xe_tok[:, :], 0.0))

                @blk.gpsimd
                def _(gg, le=le, blk_i=blk_i):
                    s_v.wait(gg)
                    if le == 0 and blk_i == 0:
                        s_ccx = L["s_ccx"]
                        s_ccx.wait(gg)
                    for u in range(4):
                        s_eg.incd(gg.indirect_dma_start(
                            out=xe_tok[:, u * D:(u + 1) * D],
                            out_offset=None,
                            in_=xn_all_d[0:1, :],
                            in_offset=IndirectOffsetOnAxis(
                                tcol_i[:, u:u + 1], 0),
                            bounds_check=NCORE * T - 1, oob_is_err=False))

                ev_base = s_s.reserve(4 * NCH)

                @blk.tensor
                def _(t, le=le, blk_i=blk_i, ev_base=ev_base):
                    s_eg.wait(t)
                    for u in range(4):
                        for c in range(NCH):
                            n = u * NCH + c
                            if n >= 2:
                                s_s.wait_at(t, ev_base + n - 1)
                            s_pe.inc(t.transpose(
                                ps_tre[n % 2][:, :],
                                xe_tok[:, u * D + c * 128:u * D + (c + 1) * 128],
                                ident_sb[:, :]))

                @blk.scalar
                def _(s, le=le, blk_i=blk_i, ev_base=ev_base,
                      pe_base=s_pe.n - 4 * NCH):
                    for u in range(4):
                        for c in range(NCH):
                            n = u * NCH + c
                            s_pe.wait_at(s, pe_base + n + 1)
                            s_s.inc_at(s.copy(
                                xeT[c][:, u * 128:(u + 1) * 128],
                                ps_tre[n % 2][:, :]))

                _emit_mlp(nc, blk, L, wgs, wus, wds, xeT, None, eo_in_d,
                          (le * CAP + blk_i * 512) // 128, 1,
                          first_waits=[(s_mw, None), (s_s, None),
                                       (s_p3, None), (s_p3s, None),
                                       (s_p3v, None)],
                          plain_store=True)

    with nc.Block("CC2") as blk:
        @blk.gpsimd
        def _(gg):
            s_accg.wait(gg)
            s_cc2.incd(gg.collective_compute(
                "AllGather", ALU.bypass, replica_groups=[list(range(NCORE))],
                ins=[eo_in_d.ap().bitcast(F32).opt()],
                outs=[eo_all_d.ap().bitcast(F32).opt()]))


def emit_p5(nc, L):
    T, D, NT = L["T"], L["D"], L["NT"]
    acc_d, out = L["acc_d"], L["out"]
    s_acc, s_st, s_accg = L["s_acc"], L["s_st"], L["s_accg"]
    moe = L["moe"]
    ob = [L["sb"]("ob0", [128, D], F32), L["sb"]("ob1", [128, D], F32)]

    if moe != "sharded":
        with nc.Block("P5") as blk:
            @blk.sync
            def _(sy):
                s_acc.wait(sy)
                s_accg.wait(sy)
                for i in range(NT):
                    jj = i % 2
                    s_st.incd(sy.dma_start(out=ob[jj][:, :],
                                           in_=acc_d[i * 128:(i + 1) * 128, :]))
                    s_st.wait_at(sy, s_st.n)
                    s_st.incd(sy.dma_start(out=out[i * 128:(i + 1) * 128, :],
                                           in_=ob[jj][:, :]))
                s_st.wait(sy)
        return

    s_cc2, s_p5g, s_v = L["s_cc2"], L["s_p5g"], L["s_v"]
    retidx, gates = L["retidx"], L["gates"]
    eo_all_d = L["eo_all_d"]
    CAP = L["CAP"]
    NCORE = 8
    er = [L["sb"](f"er{j}", [128, D], BF16) for j in range(2)]
    ALUx = ALU

    with nc.Block("P5") as blk:
        @blk.vector
        def _(v):
            C = L["Chain"](s_v, v)
            C(v.memset(er[0][:, :], 0.0))
            C(v.memset(er[1][:, :], 0.0))

        @blk.sync
        def _(sy):
            s_acc.wait(sy)
            s_accg.wait(sy)
            for i in range(NT):
                jj = i % 2
                if i >= 2:
                    s_v.wait_at(sy, s_v.n + (i - 1) * 4)
                s_st.incd(sy.dma_start(out=ob[jj][:, :],
                                       in_=acc_d[i * 128:(i + 1) * 128, :]))

        @blk.gpsimd
        def _(gg):
            s_cc2.wait(gg)
            s_v.wait(gg)
            vbase = s_v.n
            for i in range(NT):
                for k in range(2):
                    kk = k
                    if i >= 1:
                        s_v.wait_at(gg, vbase + (i - 1) * 4 + 2 + k + 1)
                    s_p5g.incd(gg.indirect_dma_start(
                        out=er[kk][:, :], out_offset=None,
                        in_=eo_all_d[0:1, :],
                        in_offset=IndirectOffsetOnAxis(
                            retidx[:, 2 * i + k:2 * i + k + 1], 0),
                        bounds_check=NCORE * 2 * CAP - 1, oob_is_err=False))

        @blk.vector
        def _(v, vbase=s_v.n):
            C = L["Chain"](s_v, v)
            for i in range(NT):
                jj = i % 2
                s_st.wait_at(v, 16 * (i + 1))
                s_p5g.wait_at(v, 32 * (i + 1))
                C(v.scalar_tensor_tensor(ob[jj][:, :], er[0][:, :],
                                         gates[:, 2 * i:2 * i + 1],
                                         ob[jj][:, :], ALUx.mult, ALUx.add))
                C(v.scalar_tensor_tensor(ob[jj][:, :], er[1][:, :],
                                         gates[:, 2 * i + 1:2 * i + 2],
                                         ob[jj][:, :], ALUx.mult, ALUx.add))
                C(v.tensor_copy(ob[jj][:, :], ob[jj][:, :])) if False else None
                s_v.inc(v.tensor_copy(ob[jj][0:1, 0:1], ob[jj][0:1, 0:1]))
                s_v.inc(v.tensor_copy(ob[jj][0:1, 0:1], ob[jj][0:1, 0:1]))

        @blk.sync
        def _(sy, vb2=None):
            for i in range(NT):
                jj = i % 2
                s_v.wait_at(sy, s_v.n - (NT - 1 - i) * 4)
                s_st.incd(sy.dma_start(out=out[i * 128:(i + 1) * 128, :],
                                       in_=ob[jj][:, :]))
            s_st.wait(sy)


_CFG = dict(T=4096, D=1024, H=16, DH=64, FF=1408, E=16, CAP=5120,
            moe="dense", stop_after="P5", race=False)


def kernel(**inputs):
    from concourse.bass_utils import run_bass_kernel_spmd
    cfg = dict(_CFG)
    shards = host_prep(inputs, cfg)
    nc, ctx, L = build(cfg)
    res = run_bass_kernel_spmd(nc, shards, core_ids=list(range(NCORE)))
    return np.stack([r["out"] for r in res.results]).astype(np.float32)
